# version 07
# baseline (speedup 1.0000x reference)
import numpy as np
import ml_dtypes
import concourse.bass as bass
import concourse.mybir as mybir
from concourse.bass_utils import run_bass_kernel_spmd
from contextlib import ExitStack
from concourse.bass_core_rust import IndirectOffsetOnAxis

F32 = mybir.dt.float32
BF16 = mybir.dt.bfloat16
I32 = mybir.dt.int32
AF = mybir.ActivationFunctionType
ALU = mybir.AluOpType
AX = mybir.AxisListType

D = 1024
NHP = 8
NE = 16
NSH = 4
NU = 2 * NE + 2 * NSH
EPS = 1e-5
SIN_SCALE = 30.0 / 8.0
BIG = 1e30


class Prog:
    ENGINES = ("sync", "gpsimd", "tensor", "vector", "scalar")
    ENG_SEM = {"tensor": "pe", "vector": "dve", "scalar": "act",
               "sync": "sq", "gpsimd": "gq"}

    def __init__(self):
        self.ops = {e: [] for e in self.ENGINES}
        self.count = {"pe": 0, "dve": 0, "act": 0}
        self.lastwait = {e: {} for e in self.ENGINES}
        self.regions = {}
        self.sems = None

    def _region(self, key):
        r = self.regions.get(key)
        if r is None:
            r = {"w": None, "r": {}}
            self.regions[key] = r
        return r

    def _wait(self, eng, sem, val):
        if val is None or val <= 0:
            return
        if eng == "tensor" and sem == "pe":
            return
        lw = self.lastwait[eng]
        if lw.get(sem, 0) >= val:
            return
        lw[sem] = val

        def f(h, self=self, sem=sem, val=val):
            h.wait_ge(self.sems[sem], val)
        self.ops[eng].append(f)

    def grp(self, eng, body, reads=(), writes=(), dsem=None):
        if eng in ("sync", "gpsimd"):
            assert dsem is not None
        edges = {}
        for key in reads:
            r = self._region(key)
            if r["w"] is not None:
                s, v = r["w"]
                edges[s] = max(edges.get(s, 0), v)
        for key in writes:
            r = self._region(key)
            if r["w"] is not None:
                s, v = r["w"]
                edges[s] = max(edges.get(s, 0), v)
            for s, v in r["r"].items():
                edges[s] = max(edges.get(s, 0), v)
        for s in sorted(edges):
            self._wait(eng, s, edges[s])
        sem = dsem if dsem is not None else self.ENG_SEM[eng]
        amt = 16 if dsem is not None else 1
        self.count.setdefault(sem, 0)
        self.count[sem] += amt
        val = self.count[sem]

        def f(h, self=self, body=body, sem=sem, amt=amt):
            body(h).then_inc(self.sems[sem], amt)
        self.ops[eng].append(f)
        for key in reads:
            r = self._region(key)
            r["r"][sem] = max(r["r"].get(sem, 0), val)
        for key in writes:
            self.regions[key] = {"w": (sem, val), "r": {}}
        return val

    def barrier_vals(self):
        return dict(self.count)

    def barrier_wait(self, eng, vals, sems=("pe", "dve", "act")):
        for s in sems:
            self._wait(eng, s, vals.get(s, 0))


def _pbanks(nc, handle, shape, dtype):
    ml = nc.lookup_mloc(handle)
    per_part = int(np.prod(shape[1:])) * mybir.dt.size(dtype)
    nb = (per_part + 2047) // 2048
    return tuple(("ps", ml.bank + i) for i in range(max(nb, 1)))


def build(T, debug=False):
    assert T % 512 == 0
    nT = T // 128
    nQC = T // 512
    NCH = T // 512
    nc = bass.Bass("TRN2", target_bir_lowering=False, dynamic_dma_scratch_size=16384)

    x_d = nc.dram_tensor("x", [T, D], F32, kind="ExternalInput").ap()
    out_d = nc.dram_tensor("out", [T, D], F32, kind="ExternalOutput").ap()
    qkvwo_d = nc.dram_tensor("qkvwo", [NHP, 128, 5376], F32, kind="ExternalInput").ap()
    wunits_d = nc.dram_tensor("wunits", [NU, 128, 6144], F32, kind="ExternalInput").ap()
    bqkv_d = nc.dram_tensor("bqkvT_d", [128, 16], F32, kind="ExternalInput").ap()
    CAP = T // 4
    NB = CAP // 128
    SC = min(128, CAP)
    SCB = SC // 128
    rout_d = nc.dram_tensor("routimg", [128, 280], F32, kind="ExternalInput").ap()
    fill_d = nc.dram_tensor("fillimg", [128, 64 * NB], F32, kind="ExternalInput").ap()
    glw_d = nc.dram_tensor("glw", [16 * CAP + 8, 4], F32, kind="Internal").ap()
    y2_d = nc.dram_tensor("y2", [2 * T + 128, 1024], BF16, kind="Internal").ap()
    xn_d = nc.dram_tensor("xnhbm", [T, 1024], BF16, kind="Internal").ap()
    wg_d = nc.dram_tensor("wgimg", [128, 136], F32, kind="ExternalInput").ap()
    id_d = nc.dram_tensor("ident", [128, 128], F32, kind="ExternalInput").ap()

    P = Prog()

    def tap(name, ap, reads, bf=True):
        if not debug or (isinstance(debug, (list, tuple, set)) and name not in debug):
            return
        d = nc.dram_tensor(f"dbg_{name}", list(ap.shape), F32, kind="ExternalOutput").ap()
        eng = "gpsimd" if bf else "sync"
        P.grp(eng, lambda h: h.dma_start(out=d[:], in_=ap), reads=tuple(reads), writes=(),
              dsem="d_tapg" if bf else "d_tap")

    def xn_keys(t0, t1):
        return tuple(f"xnT{i}" for i in range(t0 // 128, (t1 + 127) // 128))

    es = ExitStack()
    ctx = es.enter_context
    with es:
        acc = ctx(nc.sbuf_tensor("acc", [128, nT * 1024], BF16))
        xnT = ctx(nc.sbuf_tensor("xnT", [128, 8 * T], BF16))
        wg_bf = ctx(nc.sbuf_tensor("wg_bf", [128, 136], BF16))
        bqkvT = ctx(nc.sbuf_tensor("bqkvT", [128, 16], F32))
        stats = ctx(nc.sbuf_tensor("stats", [128, 16], F32))
        sig_all = ctx(nc.sbuf_tensor("sig_all", [128, T // 128], F32))
        idxs_i = ctx(nc.sbuf_tensor("idxs_i", [128, 32 * NB], I32))
        wext = ctx(nc.sbuf_tensor("wext", [128, 16 * NB], F32))
        xg_tok = ctx(nc.sbuf_tensor("xg_tok", [128, SCB * 1024], BF16))
        xgT = xg_tok
        assert SCB == 1
        ysc = ctx(nc.sbuf_tensor("ysc0", [128, 512], BF16))
        qkvb = ctx(nc.sbuf_tensor("qkvb", [128, 5376], BF16))
        if T >= 4096:
            sxb = ExitStack()
            xbuf = [sxb.enter_context(nc.sbuf_tensor(f"xbuf{i}", [128, 1024], F32)) for i in range(2)]
            sxb.close()
            QT = ctx(nc.sbuf_tensor("QT", [128, T], BF16))
            assert nc.lookup_mloc(QT).addr == nc.lookup_mloc(xbuf[0]).addr
        else:
            xbuf = [ctx(nc.sbuf_tensor(f"xbuf{i}", [128, 1024], F32)) for i in range(2)]
            QT = ctx(nc.sbuf_tensor("QT", [128, T], BF16))
        KT = ctx(nc.sbuf_tensor("KT", [128, T], BF16))
        V_sb = ctx(nc.sbuf_tensor("V_sb", [128, T], BF16))
        st_sb = ctx(nc.sbuf_tensor("stsb", [128, 1024], BF16))
        ctx_sb = ctx(nc.sbuf_tensor("ctxsb", [128, 512], BF16))
        id_bf = ctx(nc.sbuf_tensor("id_perm", [128, 128], BF16))
        sg_sb = ctx(nc.sbuf_tensor("sg_sb", [128, 512], BF16))
        srt = ExitStack()
        gat = srt.enter_context(nc.sbuf_tensor("gat", [128, 256], F32))
        rout_sb = srt.enter_context(nc.sbuf_tensor("rout_sb", [128, 280], F32))
        carry = srt.enter_context(nc.sbuf_tensor("carry", [1, 16], F32))
        glw_sb = srt.enter_context(nc.sbuf_tensor("glw_sb", [128, 64 * NB], F32))
        quad = srt.enter_context(nc.sbuf_tensor("quad", [128, 8], F32))
        idx_i = srt.enter_context(nc.sbuf_tensor("idx_i", [128, 2], I32))
        pos_i = srt.enter_context(nc.sbuf_tensor("pos_i", [128, 32], I32))
        hole_start = nc.lookup_mloc(gat).addr
        hole_end = nc.lookup_mloc(pos_i).addr + 128
        srt.close()
        h_sb = ctx(nc.sbuf_tensor("h_sb", [128, 2048], BF16))
        assert nc.lookup_mloc(h_sb).addr == hole_start
        spacer = hole_end - (hole_start + 4096)
        if spacer > 0:
            ctx(nc.sbuf_tensor("h_spacer", [128, spacer], mybir.dt.uint8))
        mbuf = [ctx(nc.sbuf_tensor(f"mbuf{i}", [128, 6144], BF16)) for i in range(2)]
        assert nc.lookup_mloc(mbuf[0]).addr >= hole_end

        if T >= 2048:
            sq_sb = KT[:, 1024:2048]
            xnbf = KT[:, 0:1024]
            sq_key, xnbf_key = "KT", "KT"
        else:
            sq_t = ctx(nc.sbuf_tensor("sq_t", [128, 1024], F32))
            xnbf_t = ctx(nc.sbuf_tensor("xnbf_t", [128, 1024], BF16))
            sq_sb, xnbf = sq_t[:], xnbf_t[:]
            sq_key, xnbf_key = "sqt", "xnbft"
        ps0s = ExitStack()
        psTR = [ps0s.enter_context(nc.psum_tensor(f"psTR{i}", [128, 128], BF16)) for i in range(2)]
        trb = [_pbanks(nc, psTR[i], [128, 128], BF16) for i in range(2)]

        P.grp("sync", lambda h: h.dma_start(out=bqkvT[:], in_=bqkv_d[:]),
              writes=("bqkv",), dsem="d_cst")
        P.grp("gpsimd", lambda h: h.dma_start(out=id_bf[:], in_=id_d[:]),
              writes=("idp",), dsem="d_cstg")
        P.grp("gpsimd", lambda h: h.dma_start(out=wg_bf[:], in_=wg_d[:]),
              writes=("wgbf",), dsem="d_cstg")
        P.grp("sync", lambda h: h.dma_start(out=rout_sb[:], in_=rout_d[:]),
              writes=("rout",), dsem="d_cst")
        P.grp("vector", lambda h: h.memset(carry[:], 0.0), writes=("carry",))
        P.grp("vector", lambda h: h.memset(quad[:], 0.0), writes=("quad",))
        P.grp("sync", lambda h: h.dma_start(out=glw_sb[:], in_=fill_d[:]),
              writes=("glwsb",), dsem="d_cst")
        P.grp("sync",
              lambda h: h.dma_start(
                  out=glw_d[0:16 * CAP, :].rearrange("(p r) c -> p (r c)", p=128),
                  in_=glw_sb[:]),
              reads=("glwsb",), writes=("glwd",), dsem="d_cst")
        for _k in ("bqkv", "rout", "glwsb", "glwd"):
            P.regions[_k]["w"] = ("d_cst", P.count["d_cst"])
        for _k in ("idp", "wgbf"):
            P.regions[_k]["w"] = ("d_cstg", P.count["d_cstg"])
        acc_keys = tuple(f"acc{tt}" for tt in range(nT))
        P.grp("vector", lambda h: h.memset(acc[:], 0.0), writes=acc_keys)

        inv_d = 1.0 / D
        for i in range(nT):
            par = i % 2
            xt = xbuf[par]
            st = par * 8
            P.grp("sync",
                  lambda h, o=xt, i=i: h.dma_start(out=o[:], in_=x_d[i * 128:(i + 1) * 128, :]),
                  writes=(f"xb{par}",), dsem=f"d_xb{par}")
            P.grp("scalar",
                  lambda h, xt=xt, st=st: h.activation(
                      out=sq_sb, in_=xt[:], func=AF.Square,
                      accum_out=stats[:, st + 1:st + 2]),
                  reads=(f"xb{par}",), writes=(sq_key, f"s2_{par}"))
            P.grp("vector",
                  lambda h, xt=xt, st=st: h.reduce_sum(stats[:, st:st + 1], xt[:], axis=AX.X),
                  reads=(f"xb{par}",), writes=(f"s1_{par}",))
            P.grp("vector",
                  lambda h, st=st: h.tensor_scalar(
                      out=stats[:, st + 2:st + 3], in0=stats[:, st:st + 1],
                      scalar1=inv_d, scalar2=None, op0=ALU.mult),
                  reads=(f"s1_{par}",), writes=(f"mean_{par}",))
            P.grp("vector",
                  lambda h, st=st: h.tensor_scalar(
                      out=stats[:, st + 3:st + 4], in0=stats[:, st + 1:st + 2],
                      scalar1=inv_d, scalar2=None, op0=ALU.mult),
                  reads=(f"s2_{par}",), writes=(f"e2_{par}",))
            P.grp("vector",
                  lambda h, st=st: h.tensor_mul(
                      stats[:, st + 4:st + 5], stats[:, st + 2:st + 3], stats[:, st + 2:st + 3]),
                  reads=(f"mean_{par}",), writes=(f"msq_{par}",))
            P.grp("vector",
                  lambda h, st=st: h.tensor_scalar(
                      out=stats[:, st + 5:st + 6], in0=stats[:, st + 3:st + 4],
                      scalar1=stats[:, st + 4:st + 5], scalar2=EPS,
                      op0=ALU.subtract, op1=ALU.add),
                  reads=(f"e2_{par}", f"msq_{par}"), writes=(f"var_{par}",))
            P.grp("scalar",
                  lambda h, st=st: h.activation(
                      out=stats[:, st + 6:st + 7], in_=stats[:, st + 5:st + 6],
                      func=AF.Sqrt),
                  reads=(f"var_{par}",), writes=(f"std_{par}",))
            P.grp("vector",
                  lambda h, st=st: h.reciprocal(stats[:, st + 7:st + 8], stats[:, st + 6:st + 7]),
                  reads=(f"std_{par}",), writes=(f"rstd_{par}",))
            P.grp("vector",
                  lambda h, xt=xt, st=st: h.tensor_scalar(
                      out=xnbf, in0=xt[:],
                      scalar1=stats[:, st + 2:st + 3], scalar2=stats[:, st + 7:st + 8],
                      op0=ALU.subtract, op1=ALU.mult),
                  reads=(f"xb{par}", f"mean_{par}", f"rstd_{par}"),
                  writes=(xnbf_key,))
            P.grp("sync",
                  lambda h, i=i: h.dma_start(
                      out=xn_d[i * 128:(i + 1) * 128, :], in_=xnbf),
                  reads=(xnbf_key,), writes=("xnhbm",), dsem="d_xn")
            for dc in range(8):
                tr = dc % 2
                P.grp("tensor",
                      lambda h, dc=dc, tr=tr: h.transpose(
                          psTR[tr][:], xnbf[:, dc * 128:(dc + 1) * 128], id_bf[:]),
                      reads=(xnbf_key, "idp"), writes=trb[tr])
                P.grp("vector",
                      lambda h, i=i, dc=dc, tr=tr: h.tensor_copy(
                          xnT[:, dc * T + i * 128: dc * T + (i + 1) * 128], psTR[tr][:]),
                      reads=trb[tr], writes=(f"xnT{i}",))
        all_xn = xn_keys(0, T)
        tap("xnT", xnT[:], all_xn)
        vals0 = P.barrier_vals()
        ps0s.close()

        psM = ExitStack()
        psL = psM.enter_context(nc.psum_tensor("psL", [128, 1024], F32))
        psCxA = psM.enter_context(nc.psum_tensor("psCxA", [128, 512], F32))
        psCxB = psM.enter_context(nc.psum_tensor("psCxB", [128, 512], F32))
        pQO = psM.enter_context(nc.psum_tensor("pQO", [128, 512], F32))
        pg = psM.enter_context(nc.psum_tensor("pg", [128, 512], F32))
        pu = psM.enter_context(nc.psum_tensor("pu", [128, 512], F32))
        spx = ExitStack()
        psTX = spx.enter_context(nc.psum_tensor("psTX", [128, 1024], BF16))
        spx.close()
        psY = psM.enter_context(nc.psum_tensor("psY", [128, 512], F32))
        assert nc.lookup_mloc(psTX).bank == nc.lookup_mloc(psY).bank
        psLb = _pbanks(nc, psL, [128, 1024], F32)
        psCxAb = _pbanks(nc, psCxA, [128, 512], F32)
        psCxBb = _pbanks(nc, psCxB, [128, 512], F32)
        pQOb = _pbanks(nc, pQO, [128, 512], F32)
        pgb = _pbanks(nc, pg, [128, 512], F32)
        pub = _pbanks(nc, pu, [128, 512], F32)
        psYb = _pbanks(nc, psY, [128, 512], F32)

        g_eg = gat[:, 0:16]
        g_msk = gat[:, 16:32]
        g_tmp = gat[:, 32:48]
        g_ind = gat[:, 48:64]
        g_wsel = gat[:, 64:80]
        g_nm = gat[:, 80:81]
        g_rmin = gat[:, 81:82]
        g_s2 = gat[:, 82:83]
        g_r2 = gat[:, 83:84]
        g_m1n = gat[:, 84:85]

        def genA():
            for hp in range(NHP):
                P.grp("gpsimd",
                      lambda h, hp=hp: h.dma_start(out=qkvb[:], in_=qkvwo_d[hp, :, :]),
                      writes=("qkvb",), dsem="d_qkvb")
                yield
                for j, dkey in ((0, "QT"), (1, "KT")):
                    dst = QT if j == 0 else KT
                    for tc in range(nQC):
                        def qkbody(h, j=j, tc=tc):
                            last = None
                            for dc in range(8):
                                last = h.matmul(
                                    pQO[:], qkvb[:, j * 1024 + dc * 128: j * 1024 + (dc + 1) * 128],
                                    xnT[:, dc * T + tc * 512: dc * T + tc * 512 + 512],
                                    start=(dc == 0), stop=(dc == 7))
                            return last
                        P.grp("tensor", qkbody,
                              reads=("qkvb",) + xn_keys(tc * 512, tc * 512 + 512), writes=pQOb)
                        P.grp("vector",
                              lambda h, dst=dst, tc=tc, hp=hp, j=j: h.tensor_scalar(
                                  out=dst[:, tc * 512: tc * 512 + 512], in0=pQO[:],
                                  scalar1=bqkvT[:, hp * 2 + j: hp * 2 + j + 1], scalar2=None,
                                  op0=ALU.add),
                              reads=pQOb + ("bqkv",), writes=(dkey,))
                        yield
                for tt in range(nT):
                    def vbody(h, tt=tt, hp=hp):
                        for dc in range(8):
                            h.matmul(pQO[:, 0:128],
                                     xnT[:, dc * T + tt * 128: dc * T + (tt + 1) * 128],
                                     qkvb[:, 2048 + dc * 128: 2048 + (dc + 1) * 128],
                                     start=(dc == 0), stop=False)
                        return h.matmul(pQO[:, 0:128], qkvb[0:1, 5248:5376],
                                        qkvb[0:1, 4096:4224],
                                        start=False, stop=True)
                    P.grp("tensor", vbody,
                          reads=("qkvb",) + xn_keys(tt * 128, tt * 128 + 128),
                          writes=pQOb)
                    P.grp("vector",
                          lambda h, tt=tt: h.tensor_copy(
                              V_sb[:, tt * 128:(tt + 1) * 128], pQO[:, 0:128]),
                          reads=pQOb, writes=("V",))
                    yield
                if hp == 0:
                    tap("QT", QT[:], ("QT",))
                    tap("KT", KT[:], ("KT",))
                    tap("V", V_sb[:], ("V",))
                for qc in range(nQC):
                    def emit_L(kt):
                        def lbody(h, qc=qc, kt=kt):
                            h.matmul(psL[:, 0:512], KT[0:64, kt * 128:(kt + 1) * 128],
                                     QT[0:64, qc * 512: qc * 512 + 512],
                                     start=True, stop=True, tile_position=(0, 0))
                            return h.matmul(psL[:, 512:1024], KT[64:128, kt * 128:(kt + 1) * 128],
                                            QT[64:128, qc * 512: qc * 512 + 512],
                                            start=True, stop=True, tile_position=(64, 0))
                        P.grp("tensor", lbody, reads=("QT", "KT"), writes=psLb)
                        P.grp("scalar",
                              lambda h: h.activation(
                                  out=st_sb[:], in_=psL[:], func=AF.Sin, scale=SIN_SCALE),
                              reads=psLb, writes=("stsb",))

                    def emit_SV(kt):
                        def svbody(h, kt=kt):
                            h.matmul(psCxA[0:64, :], V_sb[:, kt * 128: kt * 128 + 64],
                                     st_sb[:, 0:512],
                                     start=(kt == 0), stop=(kt == nT - 1), tile_position=(0, 0))
                            return h.matmul(psCxB[64:128, :], V_sb[:, kt * 128 + 64: kt * 128 + 128],
                                            st_sb[:, 512:1024],
                                            start=(kt == 0), stop=(kt == nT - 1), tile_position=(0, 64))
                        P.grp("tensor", svbody, reads=("stsb", "V"),
                              writes=psCxAb + psCxBb)

                    emit_L(0)
                    yield
                    for kt in range(1, nT):
                        emit_SV(kt - 1)
                        emit_L(kt)
                        yield
                    emit_SV(nT - 1)

                    def cxbody(h):
                        h.tensor_copy(ctx_sb[0:64, :], psCxA[0:64, :])
                        return h.tensor_copy(ctx_sb[64:128, :], psCxB[64:128, :])
                    P.grp("vector", cxbody, reads=psCxAb + psCxBb, writes=("ctx",))
                    yield
                    for st4 in range(4):
                        for n in range(2):
                            def obody(h, st4=st4, n=n, hp=hp):
                                last = h.matmul(pQO[:], ctx_sb[:, st4 * 128:(st4 + 1) * 128],
                                                qkvb[:, 3072 + n * 512: 3072 + n * 512 + 512],
                                                start=True, stop=(hp != 0))
                                if hp == 0:
                                    last = h.matmul(pQO[:], qkvb[0:1, 5248:5376],
                                                    qkvb[0:1, 4224 + n * 512: 4224 + n * 512 + 512],
                                                    start=False, stop=True)
                                return last
                            P.grp("tensor", obody,
                                  reads=("ctx", "qkvb"), writes=pQOb)
                            tt = qc * 4 + st4
                            P.grp("vector",
                                  lambda h, tt=tt, n=n: h.tensor_tensor(
                                      out=acc[:, tt * 1024 + n * 512: tt * 1024 + n * 512 + 512],
                                      in0=pQO[:],
                                      in1=acc[:, tt * 1024 + n * 512: tt * 1024 + n * 512 + 512],
                                      op=ALU.add),
                                  reads=pQOb + (f"acc{tt}",), writes=(f"acc{tt}",))
                            yield

        r_ind = gat[:, 96:112]
        r_pos = gat[:, 112:128]
        r_q = gat[:, 128:144]
        r_t1 = gat[:, 144:160]
        r_t2 = gat[:, 160:176]
        r_slot = gat[:, 176:192]
        r_w = gat[:, 192:208]
        r_eAv = gat[:, 208:209]
        r_sA = gat[:, 209:210]
        r_sB = gat[:, 210:211]
        r_wA = gat[:, 211:212]
        r_wB = gat[:, 212:213]
        r_tok = gat[:, 213:214]
        r_indA = gat[:, 224:240]
        r_indB = gat[:, 240:256]
        LTm = rout_sb[:, 0:128]
        ONEC = rout_sb[:, 128:129]
        PIDX = rout_sb[:, 129:130]
        EOFF = rout_sb[:, 130:146]
        BIGI = float(1 << 28)

        def genB():
            for tt in range(nT):
                def gbody(h, tt=tt):
                    last = None
                    for dc in range(8):
                        last = h.matmul(psG[:, 0:17],
                                        xnT[:, dc * T + tt * 128: dc * T + (tt + 1) * 128],
                                        wg_bf[:, dc * 17:(dc + 1) * 17],
                                        start=(dc == 0), stop=(dc == 7))
                    return last
                P.grp("tensor", gbody,
                      reads=("wgbf",) + xn_keys(tt * 128, tt * 128 + 128), writes=pgb)
                P.grp("vector",
                      lambda h: h.reduce_max(g_nm, psG[:, 0:16], axis=AX.X, negate=True),
                      reads=pgb, writes=("g_nm",))
                P.grp("scalar",
                      lambda h: h.activation(out=g_eg, in_=psG[:, 0:16], func=AF.Exp, bias=g_nm),
                      reads=pgb + ("g_nm",), writes=("g_eg",))
                P.grp("scalar",
                      lambda h, tt=tt: h.activation(
                          out=sig_all[:, tt:tt + 1], in_=psG[:, 16:17], func=AF.Sigmoid),
                      reads=pgb, writes=(f"sig{tt}",))
                P.grp("vector", lambda h: h.reduce_max(g_m1n, g_eg, axis=AX.X, negate=True),
                      reads=("g_eg",), writes=("g_m1n",))
                P.grp("vector",
                      lambda h: h.tensor_scalar(out=g_msk, in0=g_eg, scalar1=g_m1n,
                                                scalar2=0.0, op0=ALU.add, op1=ALU.is_ge),
                      reads=("g_eg", "g_m1n"), writes=("g_msk",))
                P.grp("vector",
                      lambda h: h.scalar_tensor_tensor(out=g_tmp, in0=g_msk, scalar=BIG,
                                                       in1=g_eg, op0=ALU.mult, op1=ALU.subtract),
                      reads=("g_msk", "g_eg"), writes=("g_tmp",))
                P.grp("vector",
                      lambda h: h.tensor_reduce(g_rmin, g_tmp, axis=AX.X, op=ALU.min),
                      reads=("g_tmp",), writes=("g_rmin",))
                P.grp("vector",
                      lambda h: h.tensor_scalar(out=g_ind, in0=g_eg, scalar1=g_rmin,
                                                scalar2=0.0, op0=ALU.add, op1=ALU.is_ge),
                      reads=("g_eg", "g_rmin"), writes=("g_ind",))
                P.grp("vector", lambda h: h.tensor_mul(g_wsel, g_eg, g_ind),
                      reads=("g_eg", "g_ind"), writes=("g_wsel",))
                P.grp("vector", lambda h: h.reduce_sum(g_s2, g_wsel, axis=AX.X),
                      reads=("g_wsel",), writes=("g_s2",))
                P.grp("vector", lambda h: h.reciprocal(g_r2, g_s2),
                      reads=("g_s2",), writes=("g_r2",))
                P.grp("vector",
                      lambda h: h.tensor_scalar(out=r_w, in0=g_wsel,
                                                scalar1=g_r2, scalar2=None, op0=ALU.mult),
                      reads=("g_wsel", "g_r2"), writes=("r_w",))
                P.grp("vector",
                      lambda h: h.tensor_scalar(out=r_ind, in0=r_w, scalar1=0.0,
                                                scalar2=None, op0=ALU.is_gt),
                      reads=("r_w",), writes=("r_ind",))

                def cums(h):
                    h.matmul(pu[:, 0:16], LTm, r_ind, start=True, stop=False)
                    h.matmul(pu[:, 0:16], rout_sb[0:1, 148:276], carry[:], start=False, stop=True)
                    return h.matmul(pu[0:1, 16:32], ONEC, r_ind, start=True, stop=True)
                P.grp("tensor", cums, reads=("rout", "carry", "r_ind"), writes=pub)
                P.grp("vector", lambda h: h.tensor_copy(r_pos, pu[:, 0:16]),
                      reads=pub, writes=("r_pos",))
                P.grp("vector", lambda h: h.tensor_add(carry[:], pu[0:1, 16:32], carry[:]),
                      reads=pub + ("carry",), writes=("carry",))
                P.grp("vector", lambda h: h.tensor_copy(pos_i[:, 0:16], r_pos),
                      reads=("r_pos",), writes=("posi",))
                P.grp("vector",
                      lambda h: h.tensor_scalar(out=pos_i[:, 16:32], in0=pos_i[:, 0:16],
                                                scalar1=127, scalar2=None,
                                                op0=ALU.bitwise_and),
                      reads=("posi",), writes=("posq",))
                P.grp("vector", lambda h: h.tensor_copy(r_q, pos_i[:, 16:32]),
                      reads=("posq",), writes=("r_q",))
                P.grp("vector",
                      lambda h: h.scalar_tensor_tensor(out=r_t1, in0=r_q, scalar=float(16 * NB),
                                                       in1=EOFF, op0=ALU.mult, op1=ALU.add),
                      reads=("r_q", "rout"), writes=("r_t1",))
                P.grp("vector", lambda h: h.tensor_sub(r_t2, r_pos, r_q),
                      reads=("r_pos", "r_q"), writes=("r_t2",))
                P.grp("vector",
                      lambda h: h.scalar_tensor_tensor(out=r_slot, in0=r_t2, scalar=1.0 / 128.0,
                                                       in1=r_t1, op0=ALU.mult, op1=ALU.add),
                      reads=("r_t2", "r_t1"), writes=("r_slot",))
                P.grp("vector",
                      lambda h: h.tensor_scalar(out=r_t1, in0=r_pos, scalar1=float(CAP),
                                                scalar2=BIGI, op0=ALU.is_ge, op1=ALU.mult),
                      reads=("r_pos",), writes=("r_t1",))
                P.grp("vector", lambda h: h.tensor_add(r_slot, r_slot, r_t1),
                      reads=("r_slot", "r_t1"), writes=("r_slot",))
                P.grp("vector",
                      lambda h: h.tensor_scalar(
                          out=r_slot, in0=r_slot, scalar1=float(16 * CAP),
                          scalar2=None, op0=ALU.min),
                      reads=("r_slot",), writes=("r_slot",))
                P.grp("vector",
                      lambda h: h.tensor_scalar(out=r_t2, in0=r_ind, scalar1=-BIGI,
                                                scalar2=BIGI, op0=ALU.mult, op1=ALU.add),
                      reads=("r_ind",), writes=("r_t2",))
                P.grp("vector",
                      lambda h: h.scalar_tensor_tensor(out=r_t1, in0=EOFF, scalar=1.0,
                                                       in1=r_t2, op0=ALU.mult, op1=ALU.add),
                      reads=("rout", "r_t2"), writes=("r_t1",))
                P.grp("vector", lambda h: h.tensor_reduce(r_eAv, r_t1, axis=AX.X, op=ALU.min),
                      reads=("r_t1",), writes=("r_eAv",))
                P.grp("vector",
                      lambda h: h.tensor_scalar(out=r_indA, in0=r_t1, scalar1=r_eAv,
                                                scalar2=None, op0=ALU.is_equal),
                      reads=("r_t1", "r_eAv"), writes=("r_indA",))
                P.grp("vector", lambda h: h.tensor_sub(r_indB, r_ind, r_indA),
                      reads=("r_ind", "r_indA"), writes=("r_indB",))
                for nm, indv, sdst, wdst in (("A", r_indA, r_sA, r_wA), ("B", r_indB, r_sB, r_wB)):
                    P.grp("vector",
                          lambda h, indv=indv: h.tensor_mul(r_t1, r_slot, indv),
                          reads=("r_slot", f"r_ind{nm}"), writes=("r_t1",))
                    P.grp("vector",
                          lambda h, sdst=sdst: h.reduce_sum(sdst, r_t1, axis=AX.X),
                          reads=("r_t1",), writes=(f"r_s{nm}",))
                    P.grp("vector",
                          lambda h, indv=indv: h.tensor_mul(r_t1, r_w, indv),
                          reads=("r_w", f"r_ind{nm}"), writes=("r_t1",))
                    P.grp("vector",
                          lambda h, wdst=wdst: h.reduce_sum(wdst, r_t1, axis=AX.X),
                          reads=("r_t1",), writes=(f"r_w{nm}",))
                P.grp("vector",
                      lambda h, tt=tt: h.tensor_scalar(out=r_tok, in0=PIDX,
                                                       scalar1=float(tt * 128),
                                                       scalar2=None, op0=ALU.add),
                      reads=("rout",), writes=("r_tok",))
                P.grp("vector", lambda h: h.tensor_copy(quad[:, 0:1], r_tok),
                      reads=("r_tok",), writes=("quad",))
                P.grp("vector", lambda h: h.tensor_copy(quad[:, 1:2], r_tok),
                      reads=("r_tok", "quad"), writes=("quad",))
                P.grp("vector", lambda h: h.tensor_copy(quad[:, 2:3], r_wA),
                      reads=("r_wA", "quad"), writes=("quad",))
                P.grp("vector", lambda h: h.tensor_copy(quad[:, 4:5], r_tok),
                      reads=("r_tok", "quad"), writes=("quad",))
                P.grp("vector",
                      lambda h: h.tensor_scalar(out=quad[:, 5:6], in0=r_tok,
                                                scalar1=float(T), scalar2=None, op0=ALU.add),
                      reads=("r_tok", "quad"), writes=("quad",))
                P.grp("vector", lambda h: h.tensor_copy(quad[:, 6:7], r_wB),
                      reads=("r_wB", "quad"), writes=("quad",))
                P.grp("vector", lambda h: h.tensor_copy(idx_i[:, 0:1], r_sA),
                      reads=("r_sA",), writes=("idxi",))
                P.grp("vector", lambda h: h.tensor_copy(idx_i[:, 1:2], r_sB),
                      reads=("r_sB", "idxi"), writes=("idxi",))
                P.grp("gpsimd",
                      lambda h: h.indirect_dma_start(
                          out=glw_d[:],
                          out_offset=IndirectOffsetOnAxis(ap=idx_i[:, 0:1], axis=0),
                          in_=quad[:, 0:4], in_offset=None),
                      reads=("quad", "idxi", "glwd"), writes=(), dsem="d_sc")
                P.grp("gpsimd",
                      lambda h: h.indirect_dma_start(
                          out=glw_d[:],
                          out_offset=IndirectOffsetOnAxis(ap=idx_i[:, 1:2], axis=0),
                          in_=quad[:, 4:8], in_offset=None),
                      reads=("quad", "idxi", "glwd"), writes=(), dsem="d_sc")
                yield
            P._wait("sync", "d_sc", P.count["d_sc"])
            P.grp("sync",
                  lambda h: h.dma_start(
                      out=glw_sb[:],
                      in_=glw_d[0:16 * CAP, :].rearrange("(p r) c -> p (r c)", p=128)),
                  reads=("glwd",), writes=("glwsb",), dsem="d_glwr")
            P.grp("vector",
                  lambda h: h.tensor_copy(
                      idxs_i[:, 0:16 * NB],
                      glw_sb[:].rearrange("p (r c) -> p r c", c=4)[:, :, 0]),
                  reads=("glwsb",), writes=("idxs",))
            P.grp("vector",
                  lambda h: h.tensor_copy(
                      idxs_i[:, 16 * NB:32 * NB],
                      glw_sb[:].rearrange("p (r c) -> p r c", c=4)[:, :, 1]),
                  reads=("glwsb", "idxs"), writes=("idxs",))
            P.grp("vector",
                  lambda h: h.tensor_copy(
                      wext[:], glw_sb[:].rearrange("p (r c) -> p r c", c=4)[:, :, 2]),
                  reads=("glwsb",), writes=("wext",))
            uctr = 0
            for e in range(NE):
                bufs = []
                bkeys = []
                for half in range(2):
                    mpar = uctr % 2
                    uctr += 1
                    P.grp("gpsimd",
                          lambda h, u=2 * e + half, mpar=mpar: h.dma_start(
                              out=mbuf[mpar][:], in_=wunits_d[u, :, :]),
                          writes=(f"mbuf{mpar}",), dsem=f"d_mb{mpar}")
                    bufs.append(mbuf[mpar])
                    bkeys.append(f"mbuf{mpar}")
                for scj in range(CAP // SC):
                    for blk in range(SCB):
                        bg = scj * SCB + blk
                        P.grp("gpsimd",
                              lambda h, e=e, bg=bg, blk=blk: h.indirect_dma_start(
                                  out=xg_tok[:, blk * 1024:(blk + 1) * 1024],
                                  out_offset=None,
                                  in_=xn_d[:],
                                  in_offset=IndirectOffsetOnAxis(
                                      ap=idxs_i[:, e * NB + bg: e * NB + bg + 1],
                                      axis=0)),
                              reads=("idxs", "xnhbm"), writes=("xgk",), dsem="d_xg")
                    for wave in range(SCB):
                        def twav(h, wave=wave):
                            last = None
                            for k in range(8):
                                j = wave * 8 + k
                                blk, dc = j // 8, j % 8
                                last = h.transpose(
                                    psTX[:, k * 128:(k + 1) * 128],
                                    xg_tok[:, blk * 1024 + dc * 128: blk * 1024 + dc * 128 + 128],
                                    id_bf[:])
                            return last
                        P.grp("tensor", twav, reads=("xgk", "idp"), writes=psYb)

                        def tcop(h, wave=wave):
                            last = None
                            for k in range(8):
                                j = wave * 8 + k
                                blk, dc = j // 8, j % 8
                                last = h.tensor_copy(
                                    xgT[:, dc * SC + blk * 128: dc * SC + blk * 128 + 128],
                                    psTX[:, k * 128:(k + 1) * 128])
                            return last
                        P.grp("vector", tcop, reads=psYb, writes=("xgk",))
                        yield
                    for hc in range(4):
                        hf, hc2 = hc // 2, hc % 2
                        bufh = bufs[hf]
                        bkey = bkeys[hf]

                        def gfill(h, bufh=bufh, hc2=hc2):
                            last = None
                            for dc in range(8):
                                last = h.matmul(
                                    pg[:, 0:SC],
                                    bufh[:, dc * 256 + hc2 * 128: dc * 256 + hc2 * 128 + 128],
                                    xgT[:, dc * SC: dc * SC + SC],
                                    start=(dc == 0), stop=(dc == 7))
                            return last
                        P.grp("tensor", gfill, reads=(bkey, "xgk"), writes=pgb)
                        P.grp("scalar",
                              lambda h: h.activation(out=sg_sb[:, 0:SC], in_=pg[:, 0:SC], func=AF.Silu),
                              reads=pgb, writes=("sgk",))
                        yield

                        def ufill(h, bufh=bufh, hc2=hc2):
                            last = None
                            for dc in range(8):
                                last = h.matmul(
                                    pu[:, 0:SC],
                                    bufh[:, 2048 + dc * 256 + hc2 * 128: 2048 + dc * 256 + hc2 * 128 + 128],
                                    xgT[:, dc * SC: dc * SC + SC],
                                    start=(dc == 0), stop=(dc == 7))
                            return last
                        P.grp("tensor", ufill, reads=(bkey, "xgk"), writes=pub)
                        P.grp("vector",
                              lambda h, hc=hc: h.tensor_mul(
                                  h_sb[:, hc * SC:(hc + 1) * SC], sg_sb[:, 0:SC], pu[:, 0:SC]),
                              reads=("sgk",) + pub, writes=(f"hsb{hc}",))
                        yield
                    for blk in range(SCB):
                        bg = scj * SCB + blk
                        stok_ap = idxs_i[:, 16 * NB + e * NB + bg: 16 * NB + e * NB + bg + 1]
                        wcol = wext[:, e * NB + bg: e * NB + bg + 1]
                        for n in range(2):
                            def dbody(h, blk=blk, n=n, bufs=tuple(bufs)):
                                last = None
                                for hc in range(4):
                                    last = h.matmul(
                                        psY[:],
                                        h_sb[:, hc * SC + blk * 128: hc * SC + blk * 128 + 128],
                                        bufs[hc // 2][:, 4096 + (hc % 2) * 1024 + n * 512:
                                                       4096 + (hc % 2) * 1024 + n * 512 + 512],
                                        start=(hc == 0), stop=(hc == 3))
                                return last
                            P.grp("tensor", dbody,
                                  reads=("hsb0", "hsb1", "hsb2", "hsb3") + tuple(bkeys),
                                  writes=psYb)
                            P.grp("vector",
                                  lambda h, wcol=wcol: h.tensor_scalar(
                                      out=ysc[:], in0=psY[:], scalar1=wcol,
                                      scalar2=None, op0=ALU.mult),
                                  reads=psYb + ("wext",), writes=("ysck",))
                            P.grp("gpsimd",
                                  lambda h, n=n, stok_ap=stok_ap: h.indirect_dma_start(
                                      out=y2_d[:],
                                      out_offset=IndirectOffsetOnAxis(ap=stok_ap, axis=0),
                                      in_=ysc[:], in_offset=None,
                                      element_offset=n * 512),
                                  reads=("ysck", "idxs"), writes=(), dsem="d_y2")
                            yield
            for sl in range(NSH):
                bufs = []
                bkeys = []
                for half in range(2):
                    mpar = uctr % 2
                    uctr += 1
                    P.grp("gpsimd",
                          lambda h, u=2 * NE + 2 * sl + half, mpar=mpar: h.dma_start(
                              out=mbuf[mpar][:], in_=wunits_d[u, :, :]),
                          writes=(f"mbuf{mpar}",), dsem=f"d_mb{mpar}")
                    bufs.append(mbuf[mpar])
                    bkeys.append(f"mbuf{mpar}")
                for c in range(NCH):
                    ck = xn_keys(c * 512, c * 512 + 512)
                    for hc in range(4):
                        hf, hc2 = hc // 2, hc % 2
                        bufh = bufs[hf]
                        bkey = bkeys[hf]

                        def gfill(h, bufh=bufh, hc2=hc2, c=c):
                            last = None
                            for dc in range(8):
                                last = h.matmul(
                                    pg[:], bufh[:, dc * 256 + hc2 * 128: dc * 256 + hc2 * 128 + 128],
                                    xnT[:, dc * T + c * 512: dc * T + c * 512 + 512],
                                    start=(dc == 0), stop=(dc == 7))
                            return last
                        P.grp("tensor", gfill, reads=(bkey,) + ck, writes=pgb)
                        P.grp("scalar",
                              lambda h: h.activation(out=sg_sb[:], in_=pg[:], func=AF.Silu),
                              reads=pgb, writes=("sgk",))
                        yield

                        def ufill(h, bufh=bufh, hc2=hc2, c=c):
                            last = None
                            for dc in range(8):
                                last = h.matmul(
                                    pu[:], bufh[:, 2048 + dc * 256 + hc2 * 128: 2048 + dc * 256 + hc2 * 128 + 128],
                                    xnT[:, dc * T + c * 512: dc * T + c * 512 + 512],
                                    start=(dc == 0), stop=(dc == 7))
                            return last
                        P.grp("tensor", ufill, reads=(bkey,) + ck, writes=pub)
                        P.grp("vector",
                              lambda h, hc=hc: h.tensor_mul(
                                  h_sb[:, hc * 512:(hc + 1) * 512], sg_sb[:], pu[:]),
                              reads=("sgk",) + pub, writes=(f"hsb{hc}",))
                        yield
                    for st4 in range(4):
                        tt = c * 4 + st4
                        col = sig_all[:, tt:tt + 1]
                        ckey = f"sig{tt}"
                        for n in range(2):
                            def dbody(h, st4=st4, n=n, bufs=tuple(bufs)):
                                last = None
                                for hc in range(4):
                                    last = h.matmul(
                                        psY[:],
                                        h_sb[:, hc * 512 + st4 * 128: hc * 512 + st4 * 128 + 128],
                                        bufs[hc // 2][:, 4096 + (hc % 2) * 1024 + n * 512:
                                                       4096 + (hc % 2) * 1024 + n * 512 + 512],
                                        start=(hc == 0), stop=(hc == 3))
                                return last
                            P.grp("tensor", dbody,
                                  reads=("hsb0", "hsb1", "hsb2", "hsb3") + tuple(bkeys),
                                  writes=psYb)
                            P.grp("vector",
                                  lambda h, tt=tt, n=n, col=col: h.scalar_tensor_tensor(
                                      out=acc[:, tt * 1024 + n * 512: tt * 1024 + n * 512 + 512],
                                      in0=psY[:], scalar=col,
                                      in1=acc[:, tt * 1024 + n * 512: tt * 1024 + n * 512 + 512],
                                      op0=ALU.mult, op1=ALU.add),
                                  reads=psYb + (ckey, f"acc{tt}"), writes=(f"acc{tt}",))
                            yield

        psG = pg

        nA = NHP * (2 * nQC + nT + nQC * (nT + 1 + 8))
        nB = nT + NE * (CAP // SC) * (2 * SCB + 8 + 2 * SCB * 2) + NSH * NCH * 16
        ratio = nB / nA
        ga, gb = genA(), genB()
        a_alive = b_alive = True
        mcarry = 0.0
        while a_alive or b_alive:
            if a_alive:
                try:
                    next(ga)
                except StopIteration:
                    a_alive = False
            mcarry += ratio if a_alive else 1.0
            while mcarry >= 1.0 and b_alive:
                mcarry -= 1.0
                try:
                    next(gb)
                except StopIteration:
                    b_alive = False
            if not a_alive and not b_alive:
                break

        P.barrier_wait("sync", P.barrier_vals())
        P._wait("sync", "d_y2", P.count.get("d_y2", 0))
        if T >= 4096:
            ybuf = [(KT[:, 2048 * i:2048 * i + 1024], KT[:, 2048 * i + 1024:2048 * i + 2048])
                    for i in range(2)]
            ykeys = [("KT", "KT"), ("KT", "KT")]
        else:
            ybt = [ctx(nc.sbuf_tensor(f"ybt{i}", [128, 1024], BF16)) for i in range(4)]
            ybuf = [(ybt[0][:], ybt[1][:]), (ybt[2][:], ybt[3][:])]
        for tt in range(nT):
            fp = tt % 2
            ya, yb = ybuf[fp]
            P.grp("sync",
                  lambda h, tt=tt, fp=fp: h.dma_start(
                      out=xbuf[fp][:], in_=x_d[tt * 128:(tt + 1) * 128, :]),
                  writes=(f"xb{fp}",), dsem=f"d_xb{fp}")
            P.grp("sync",
                  lambda h, tt=tt, ya=ya: h.dma_start(
                      out=ya, in_=y2_d[tt * 128:(tt + 1) * 128, :]),
                  writes=(f"ya{fp}",), dsem=f"d_ya{fp}")
            P.grp("sync",
                  lambda h, tt=tt, yb=yb: h.dma_start(
                      out=yb, in_=y2_d[T + tt * 128: T + (tt + 1) * 128, :]),
                  writes=(f"yb{fp}",), dsem=f"d_yb{fp}")
            P.grp("vector",
                  lambda h, tt=tt, fp=fp: h.tensor_add(
                      xbuf[fp][:], acc[:, tt * 1024:(tt + 1) * 1024], xbuf[fp][:]),
                  reads=(f"xb{fp}", f"acc{tt}"), writes=(f"xb{fp}",))
            P.grp("vector",
                  lambda h, fp=fp, ya=ya: h.tensor_add(xbuf[fp][:], ya, xbuf[fp][:]),
                  reads=(f"xb{fp}", f"ya{fp}"), writes=(f"xb{fp}",))
            P.grp("vector",
                  lambda h, fp=fp, yb=yb: h.tensor_add(xbuf[fp][:], yb, xbuf[fp][:]),
                  reads=(f"xb{fp}", f"yb{fp}"), writes=(f"xb{fp}",))
            P.grp("sync",
                  lambda h, tt=tt, fp=fp: h.dma_start(
                      out=out_d[tt * 128:(tt + 1) * 128, :], in_=xbuf[fp][:]),
                  reads=(f"xb{fp}",), writes=(), dsem=f"d_out{fp}")
        P._wait("sync", "d_out0", P.count["d_out0"])
        P._wait("sync", "d_out1", P.count["d_out1"])
        psM.close()

        sems = ExitStack()
        P.sems = {k: sems.enter_context(nc.semaphore(f"sem_{k}"))
                  for k in sorted(P.count)}
        with sems:
            with nc.Block() as block:
                @block.sync
                def _(h):
                    for f in P.ops["sync"]:
                        f(h)

                @block.gpsimd
                def _(h):
                    for f in P.ops["gpsimd"]:
                        f(h)

                @block.tensor
                def _(h):
                    for f in P.ops["tensor"]:
                        f(h)

                @block.vector
                def _(h):
                    for f in P.ops["vector"]:
                        f(h)

                @block.scalar
                def _(h):
                    for f in P.ops["scalar"]:
                        f(h)
    return nc


def prepare_host(inputs, T=None):
    f = np.float32
    Wq = np.ascontiguousarray(inputs["L0_attn_Wq"], f)
    Wk = np.ascontiguousarray(inputs["L0_attn_Wk"], f)
    Wv = np.ascontiguousarray(inputs["L0_attn_Wv"], f)
    Wo = np.ascontiguousarray(inputs["L0_attn_Wo"], f)
    bq = np.ascontiguousarray(inputs["L0_attn_bq"], f)
    bk = np.ascontiguousarray(inputs["L0_attn_bk"], f)
    bv = np.ascontiguousarray(inputs["L0_attn_bv"], f)
    bo = np.ascontiguousarray(inputs["L0_attn_bo"], f)
    Wg = np.ascontiguousarray(inputs["L0_ffn_Wg"], f)
    Weg = np.ascontiguousarray(inputs["L0_ffn_We_gate"], f)
    Weu = np.ascontiguousarray(inputs["L0_ffn_We_up"], f)
    Wed = np.ascontiguousarray(inputs["L0_ffn_We_down"], f)
    Wsg_ = np.ascontiguousarray(inputs["L0_ffn_Ws_gate"], f)
    Wsu = np.ascontiguousarray(inputs["L0_ffn_Ws_up"], f)
    Wsd = np.ascontiguousarray(inputs["L0_ffn_Ws_down"], f)
    Wsig = np.ascontiguousarray(inputs["L0_ffn_Wsg"], f)

    qkvwo = np.zeros((NHP, 128, 5376), f)
    for hp in range(NHP):
        cs = slice(hp * 128, hp * 128 + 128)
        for j, W in enumerate((Wq, Wk, Wv)):
            blk = W[cs, :].reshape(128, 8, 128)
            qkvwo[hp][:, j * 1024:(j + 1) * 1024] = (
                blk.transpose(2, 1, 0).reshape(128, 1024))
        qkvwo[hp][:, 3072:4096] = Wo[:, cs].T
        qkvwo[hp][0, 4096:4224] = bv[cs]
        qkvwo[hp][0, 4224:5248] = bo
        qkvwo[hp][0, 5248:5376] = 1.0

    bqkvT = np.empty((128, 16), f)
    for hp in range(NHP):
        bqkvT[:, hp * 2 + 0] = bq[hp * 128:(hp + 1) * 128]
        bqkvT[:, hp * 2 + 1] = bk[hp * 128:(hp + 1) * 128]

    def img_gud(gate, up, down):
        out = np.empty((128, 6144), f)
        out[:, 0:2048] = gate.reshape(256, 8, 128).transpose(2, 1, 0).reshape(128, 2048)
        out[:, 2048:4096] = up.reshape(256, 8, 128).transpose(2, 1, 0).reshape(128, 2048)
        out[:, 4096:6144] = down.reshape(1024, 2, 128).transpose(2, 1, 0).reshape(128, 2048)
        return out

    wunits = np.empty((NU, 128, 6144), f)
    for e in range(NE):
        for half in range(2):
            hs = slice(half * 256, half * 256 + 256)
            wunits[2 * e + half] = img_gud(Weg[e][hs, :], Weu[e][hs, :], Wed[e][:, hs])
    for s in range(NSH):
        for half in range(2):
            o = s * 512 + half * 256
            hs = slice(o, o + 256)
            wunits[2 * NE + 2 * s + half] = img_gud(Wsg_[hs, :], Wsu[hs, :], Wsd[:, hs])

    WgA = np.concatenate([Wg, Wsig], axis=0)
    wgimg = np.ascontiguousarray(
        WgA.reshape(17, 8, 128).transpose(2, 1, 0).reshape(128, 136))

    if T is None:
        T = int(np.asarray(inputs["L0_attn_x"]).shape[1])
    CAP = T // 4
    NB = CAP // 128
    rout = np.zeros((128, 280), f)
    rout[:, 0:128] = np.tril(np.ones((128, 128), f), -1).T
    rout[:, 128] = 1.0
    rout[:, 129] = np.arange(128, dtype=f)
    rout[:, 130:146] = (np.arange(16, dtype=f) * NB)[None, :]
    rout[0, 148:276] = 1.0
    fill = np.zeros((128, 64 * NB), f)
    fill[:, 1::4] = float(2 * T)
    return {
        "qkvwo": qkvwo,
        "wunits": wunits,
        "bqkvT_d": bqkvT,
        "wgimg": wgimg,
        "ident": np.eye(128, dtype=f),
        "routimg": rout,
        "fillimg": fill,
    }


def kernel(**inputs):
    x = np.ascontiguousarray(np.asarray(inputs["L0_attn_x"], np.float32))
    B, T, d = x.shape
    assert d == D
    common = prepare_host(inputs)
    nc = build(T)
    in_maps = []
    for b in range(B):
        m = dict(common)
        m["x"] = np.ascontiguousarray(x[b])
        in_maps.append(m)
    res = run_bass_kernel_spmd(nc, in_maps, core_ids=list(range(B)))
    return np.stack([r["out"] for r in res.results], axis=0)


# version 08
# speedup vs baseline: 1.0949x; 1.0949x over previous
import numpy as np
import ml_dtypes
import concourse.bass as bass
import concourse.mybir as mybir
from concourse.bass_utils import run_bass_kernel_spmd
from contextlib import ExitStack
from concourse.bass_core_rust import IndirectOffsetOnAxis

F32 = mybir.dt.float32
BF16 = mybir.dt.bfloat16
I32 = mybir.dt.int32
AF = mybir.ActivationFunctionType
ALU = mybir.AluOpType
AX = mybir.AxisListType

D = 1024
NHP = 8
NE = 16
NSH = 4
NU = 2 * NE + 2 * NSH
EPS = 1e-5
SIN_SCALE = 30.0 / 8.0
BIG = 1e30


class Prog:
    ENGINES = ("sync", "gpsimd", "tensor", "vector", "scalar")
    ENG_SEM = {"tensor": "pe", "vector": "dve", "scalar": "act",
               "sync": "sq", "gpsimd": "gq"}

    def __init__(self):
        self.ops = {e: [] for e in self.ENGINES}
        self.count = {"pe": 0, "dve": 0, "act": 0}
        self.lastwait = {e: {} for e in self.ENGINES}
        self.regions = {}
        self.sems = None

    def _region(self, key):
        r = self.regions.get(key)
        if r is None:
            r = {"w": None, "r": {}}
            self.regions[key] = r
        return r

    def _wait(self, eng, sem, val):
        if val is None or val <= 0:
            return
        if eng == "tensor" and sem == "pe":
            return
        lw = self.lastwait[eng]
        if lw.get(sem, 0) >= val:
            return
        lw[sem] = val

        def f(h, self=self, sem=sem, val=val):
            h.wait_ge(self.sems[sem], val)
        self.ops[eng].append(f)

    def grp(self, eng, body, reads=(), writes=(), dsem=None):
        if eng in ("sync", "gpsimd"):
            assert dsem is not None
        edges = {}
        for key in reads:
            r = self._region(key)
            if r["w"] is not None:
                s, v = r["w"]
                edges[s] = max(edges.get(s, 0), v)
        for key in writes:
            r = self._region(key)
            if r["w"] is not None:
                s, v = r["w"]
                edges[s] = max(edges.get(s, 0), v)
            for s, v in r["r"].items():
                edges[s] = max(edges.get(s, 0), v)
        for s in sorted(edges):
            self._wait(eng, s, edges[s])
        sem = dsem if dsem is not None else self.ENG_SEM[eng]
        amt = 16 if dsem is not None else 1
        self.count.setdefault(sem, 0)
        self.count[sem] += amt
        val = self.count[sem]

        def f(h, self=self, body=body, sem=sem, amt=amt):
            body(h).then_inc(self.sems[sem], amt)
        self.ops[eng].append(f)
        for key in reads:
            r = self._region(key)
            r["r"][sem] = max(r["r"].get(sem, 0), val)
        for key in writes:
            self.regions[key] = {"w": (sem, val), "r": {}}
        return val

    def barrier_vals(self):
        return dict(self.count)

    def barrier_wait(self, eng, vals, sems=("pe", "dve", "act")):
        for s in sems:
            self._wait(eng, s, vals.get(s, 0))


def _pbanks(nc, handle, shape, dtype):
    ml = nc.lookup_mloc(handle)
    per_part = int(np.prod(shape[1:])) * mybir.dt.size(dtype)
    nb = (per_part + 2047) // 2048
    return tuple(("ps", ml.bank + i) for i in range(max(nb, 1)))


def build(T, debug=False):
    assert T % 512 == 0
    nT = T // 128
    nQC = T // 512
    NCH = T // 512
    nc = bass.Bass("TRN2", target_bir_lowering=False, dynamic_dma_scratch_size=16384)

    x_d = nc.dram_tensor("x", [T, D], F32, kind="ExternalInput").ap()
    out_d = nc.dram_tensor("out", [T, D], F32, kind="ExternalOutput").ap()
    qkvwo_d = nc.dram_tensor("qkvwo", [NHP, 128, 5376], F32, kind="ExternalInput").ap()
    wunits_d = nc.dram_tensor("wunits", [NU, 128, 6144], F32, kind="ExternalInput").ap()
    bqkv_d = nc.dram_tensor("bqkvT_d", [128, 16], F32, kind="ExternalInput").ap()
    CAP = T // 4
    NB = CAP // 128
    SC = min(128, CAP)
    SCB = SC // 128
    rout_d = nc.dram_tensor("routimg", [128, 280], F32, kind="ExternalInput").ap()
    fill_d = nc.dram_tensor("fillimg", [128, 64 * NB], F32, kind="ExternalInput").ap()
    glw_d = nc.dram_tensor("glw", [16 * CAP + 8, 4], F32, kind="Internal").ap()
    y2_d = nc.dram_tensor("y2", [2 * T + 128, 1024], BF16, kind="Internal").ap()
    xn_d = nc.dram_tensor("xnhbm", [T, 1024], BF16, kind="Internal").ap()
    wg_d = nc.dram_tensor("wgimg", [128, 136], F32, kind="ExternalInput").ap()
    id_d = nc.dram_tensor("ident", [128, 128], F32, kind="ExternalInput").ap()

    P = Prog()

    def tap(name, ap, reads, bf=True):
        if not debug or (isinstance(debug, (list, tuple, set)) and name not in debug):
            return
        d = nc.dram_tensor(f"dbg_{name}", list(ap.shape), F32, kind="ExternalOutput").ap()
        eng = "gpsimd" if bf else "sync"
        P.grp(eng, lambda h: h.dma_start(out=d[:], in_=ap), reads=tuple(reads), writes=(),
              dsem="d_tapg" if bf else "d_tap")

    def xn_keys(t0, t1):
        return tuple(f"xnT{i}" for i in range(t0 // 128, (t1 + 127) // 128))

    es = ExitStack()
    ctx = es.enter_context
    with es:
        acc = ctx(nc.sbuf_tensor("acc", [128, nT * 1024], BF16))
        xnT = ctx(nc.sbuf_tensor("xnT", [128, 8 * T], BF16))
        wg_bf = ctx(nc.sbuf_tensor("wg_bf", [128, 136], BF16))
        bqkvT = ctx(nc.sbuf_tensor("bqkvT", [128, 16], F32))
        stats = ctx(nc.sbuf_tensor("stats", [128, 16], F32))
        sig_all = ctx(nc.sbuf_tensor("sig_all", [128, T // 128], F32))
        idxs_i = ctx(nc.sbuf_tensor("idxs_i", [128, 32 * NB], I32))
        wext = ctx(nc.sbuf_tensor("wext", [128, 16 * NB], F32))
        xg_tok = [ctx(nc.sbuf_tensor(f"xg_tok{i}", [128, SCB * 1024], BF16)) for i in range(2)]
        assert SCB == 1
        ysc = ctx(nc.sbuf_tensor("ysc0", [128, 512], BF16))
        qkvb = ctx(nc.sbuf_tensor("qkvb", [128, 5376], BF16))
        if T >= 4096:
            sxb = ExitStack()
            xbuf = [sxb.enter_context(nc.sbuf_tensor(f"xbuf{i}", [128, 1024], F32)) for i in range(2)]
            sxb.close()
            QT = ctx(nc.sbuf_tensor("QT", [128, T], BF16))
            assert nc.lookup_mloc(QT).addr == nc.lookup_mloc(xbuf[0]).addr
        else:
            xbuf = [ctx(nc.sbuf_tensor(f"xbuf{i}", [128, 1024], F32)) for i in range(2)]
            QT = ctx(nc.sbuf_tensor("QT", [128, T], BF16))
        KT = ctx(nc.sbuf_tensor("KT", [128, T], BF16))
        V_sb = ctx(nc.sbuf_tensor("V_sb", [128, T], BF16))
        st_sb = ctx(nc.sbuf_tensor("stsb", [128, 1024], BF16))
        ctx_sb = ctx(nc.sbuf_tensor("ctxsb", [128, 512], BF16))
        id_bf = ctx(nc.sbuf_tensor("id_perm", [128, 128], BF16))
        sg_sb = ctx(nc.sbuf_tensor("sg_sb", [128, 512], BF16))
        srt = ExitStack()
        gat = srt.enter_context(nc.sbuf_tensor("gat", [128, 256], F32))
        rout_sb = srt.enter_context(nc.sbuf_tensor("rout_sb", [128, 280], F32))
        carry = srt.enter_context(nc.sbuf_tensor("carry", [1, 16], F32))
        glw_sb = srt.enter_context(nc.sbuf_tensor("glw_sb", [128, 64 * NB], F32))
        quad = srt.enter_context(nc.sbuf_tensor("quad", [128, 8], F32))
        idx_i = srt.enter_context(nc.sbuf_tensor("idx_i", [128, 2], I32))
        pos_i = srt.enter_context(nc.sbuf_tensor("pos_i", [128, 32], I32))
        hole_start = nc.lookup_mloc(gat).addr
        hole_end = nc.lookup_mloc(pos_i).addr + 128
        srt.close()
        h_sb = ctx(nc.sbuf_tensor("h_sb", [128, 2048], BF16))
        assert nc.lookup_mloc(h_sb).addr == hole_start
        spacer = hole_end - (hole_start + 4096)
        if spacer > 0:
            ctx(nc.sbuf_tensor("h_spacer", [128, spacer], mybir.dt.uint8))
        mbuf = [ctx(nc.sbuf_tensor(f"mbuf{i}", [128, 6144], BF16)) for i in range(2)]
        assert nc.lookup_mloc(mbuf[0]).addr >= hole_end

        if T >= 2048:
            sq_sb = KT[:, 1024:2048]
            xnbf = KT[:, 0:1024]
            sq_key, xnbf_key = "KT", "KT"
        else:
            sq_t = ctx(nc.sbuf_tensor("sq_t", [128, 1024], F32))
            xnbf_t = ctx(nc.sbuf_tensor("xnbf_t", [128, 1024], BF16))
            sq_sb, xnbf = sq_t[:], xnbf_t[:]
            sq_key, xnbf_key = "sqt", "xnbft"
        ps0s = ExitStack()
        psTR = [ps0s.enter_context(nc.psum_tensor(f"psTR{i}", [128, 128], BF16)) for i in range(2)]
        trb = [_pbanks(nc, psTR[i], [128, 128], BF16) for i in range(2)]

        P.grp("sync", lambda h: h.dma_start(out=bqkvT[:], in_=bqkv_d[:]),
              writes=("bqkv",), dsem="d_cst")
        P.grp("gpsimd", lambda h: h.dma_start(out=id_bf[:], in_=id_d[:]),
              writes=("idp",), dsem="d_cstg")
        P.grp("gpsimd", lambda h: h.dma_start(out=wg_bf[:], in_=wg_d[:]),
              writes=("wgbf",), dsem="d_cstg")
        P.grp("sync", lambda h: h.dma_start(out=rout_sb[:], in_=rout_d[:]),
              writes=("rout",), dsem="d_cst")
        P.grp("vector", lambda h: h.memset(carry[:], 0.0), writes=("carry",))
        P.grp("vector", lambda h: h.memset(quad[:], 0.0), writes=("quad",))
        P.grp("sync", lambda h: h.dma_start(out=glw_sb[:], in_=fill_d[:]),
              writes=("glwsb",), dsem="d_cst")
        P.grp("sync",
              lambda h: h.dma_start(
                  out=glw_d[0:16 * CAP, :].rearrange("(p r) c -> p (r c)", p=128),
                  in_=glw_sb[:]),
              reads=("glwsb",), writes=("glwd",), dsem="d_cst")
        for _k in ("bqkv", "rout", "glwsb", "glwd"):
            P.regions[_k]["w"] = ("d_cst", P.count["d_cst"])
        for _k in ("idp", "wgbf"):
            P.regions[_k]["w"] = ("d_cstg", P.count["d_cstg"])
        acc_keys = tuple(f"acc{tt}" for tt in range(nT))
        P.grp("vector", lambda h: h.memset(acc[:], 0.0), writes=acc_keys)

        inv_d = 1.0 / D
        for i in range(nT):
            par = i % 2
            xt = xbuf[par]
            st = par * 8
            P.grp("sync",
                  lambda h, o=xt, i=i: h.dma_start(out=o[:], in_=x_d[i * 128:(i + 1) * 128, :]),
                  writes=(f"xb{par}",), dsem=f"d_xb{par}")
            P.grp("scalar",
                  lambda h, xt=xt, st=st: h.activation(
                      out=sq_sb, in_=xt[:], func=AF.Square,
                      accum_out=stats[:, st + 1:st + 2]),
                  reads=(f"xb{par}",), writes=(sq_key, f"s2_{par}"))
            P.grp("vector",
                  lambda h, xt=xt, st=st: h.reduce_sum(stats[:, st:st + 1], xt[:], axis=AX.X),
                  reads=(f"xb{par}",), writes=(f"s1_{par}",))
            P.grp("vector",
                  lambda h, st=st: h.tensor_scalar(
                      out=stats[:, st + 2:st + 3], in0=stats[:, st:st + 1],
                      scalar1=inv_d, scalar2=None, op0=ALU.mult),
                  reads=(f"s1_{par}",), writes=(f"mean_{par}",))
            P.grp("vector",
                  lambda h, st=st: h.tensor_scalar(
                      out=stats[:, st + 3:st + 4], in0=stats[:, st + 1:st + 2],
                      scalar1=inv_d, scalar2=None, op0=ALU.mult),
                  reads=(f"s2_{par}",), writes=(f"e2_{par}",))
            P.grp("vector",
                  lambda h, st=st: h.tensor_mul(
                      stats[:, st + 4:st + 5], stats[:, st + 2:st + 3], stats[:, st + 2:st + 3]),
                  reads=(f"mean_{par}",), writes=(f"msq_{par}",))
            P.grp("vector",
                  lambda h, st=st: h.tensor_scalar(
                      out=stats[:, st + 5:st + 6], in0=stats[:, st + 3:st + 4],
                      scalar1=stats[:, st + 4:st + 5], scalar2=EPS,
                      op0=ALU.subtract, op1=ALU.add),
                  reads=(f"e2_{par}", f"msq_{par}"), writes=(f"var_{par}",))
            P.grp("scalar",
                  lambda h, st=st: h.activation(
                      out=stats[:, st + 6:st + 7], in_=stats[:, st + 5:st + 6],
                      func=AF.Sqrt),
                  reads=(f"var_{par}",), writes=(f"std_{par}",))
            P.grp("vector",
                  lambda h, st=st: h.reciprocal(stats[:, st + 7:st + 8], stats[:, st + 6:st + 7]),
                  reads=(f"std_{par}",), writes=(f"rstd_{par}",))
            P.grp("vector",
                  lambda h, xt=xt, st=st: h.tensor_scalar(
                      out=xnbf, in0=xt[:],
                      scalar1=stats[:, st + 2:st + 3], scalar2=stats[:, st + 7:st + 8],
                      op0=ALU.subtract, op1=ALU.mult),
                  reads=(f"xb{par}", f"mean_{par}", f"rstd_{par}"),
                  writes=(xnbf_key,))
            P.grp("sync",
                  lambda h, i=i: h.dma_start(
                      out=xn_d[i * 128:(i + 1) * 128, :], in_=xnbf),
                  reads=(xnbf_key,), writes=("xnhbm",), dsem="d_xn")
            for dc in range(8):
                tr = dc % 2
                P.grp("tensor",
                      lambda h, dc=dc, tr=tr: h.transpose(
                          psTR[tr][:], xnbf[:, dc * 128:(dc + 1) * 128], id_bf[:]),
                      reads=(xnbf_key, "idp"), writes=trb[tr])
                P.grp("vector",
                      lambda h, i=i, dc=dc, tr=tr: h.tensor_copy(
                          xnT[:, dc * T + i * 128: dc * T + (i + 1) * 128], psTR[tr][:]),
                      reads=trb[tr], writes=(f"xnT{i}",))
        all_xn = xn_keys(0, T)
        tap("xnT", xnT[:], all_xn)
        vals0 = P.barrier_vals()
        ps0s.close()

        psM = ExitStack()
        psL = psM.enter_context(nc.psum_tensor("psL", [128, 1024], F32))
        psCxA = psM.enter_context(nc.psum_tensor("psCxA", [128, 512], F32))
        psCxB = psM.enter_context(nc.psum_tensor("psCxB", [128, 512], F32))
        pQO = psM.enter_context(nc.psum_tensor("pQO", [128, 512], F32))
        pg = psM.enter_context(nc.psum_tensor("pg", [128, 512], F32))
        pu = psM.enter_context(nc.psum_tensor("pu", [128, 512], F32))
        spx = ExitStack()
        psTX = spx.enter_context(nc.psum_tensor("psTX", [128, 1024], BF16))
        spx.close()
        psY = psM.enter_context(nc.psum_tensor("psY", [128, 512], F32))
        assert nc.lookup_mloc(psTX).bank == nc.lookup_mloc(psY).bank
        psLb = _pbanks(nc, psL, [128, 1024], F32)
        psCxAb = _pbanks(nc, psCxA, [128, 512], F32)
        psCxBb = _pbanks(nc, psCxB, [128, 512], F32)
        pQOb = _pbanks(nc, pQO, [128, 512], F32)
        pgb = _pbanks(nc, pg, [128, 512], F32)
        pub = _pbanks(nc, pu, [128, 512], F32)
        psYb = _pbanks(nc, psY, [128, 512], F32)

        g_eg = gat[:, 0:16]
        g_msk = gat[:, 16:32]
        g_tmp = gat[:, 32:48]
        g_ind = gat[:, 48:64]
        g_wsel = gat[:, 64:80]
        g_nm = gat[:, 80:81]
        g_rmin = gat[:, 81:82]
        g_s2 = gat[:, 82:83]
        g_r2 = gat[:, 83:84]
        g_m1n = gat[:, 84:85]

        def genA():
            for hp in range(NHP):
                P.grp("gpsimd",
                      lambda h, hp=hp: h.dma_start(out=qkvb[:], in_=qkvwo_d[hp, :, :]),
                      writes=("qkvb",), dsem="d_qkvb")
                yield
                for j, dkey in ((0, "QT"), (1, "KT")):
                    dst = QT if j == 0 else KT
                    for tc in range(nQC):
                        def qkbody(h, j=j, tc=tc):
                            last = None
                            for dc in range(8):
                                last = h.matmul(
                                    pQO[:], qkvb[:, j * 1024 + dc * 128: j * 1024 + (dc + 1) * 128],
                                    xnT[:, dc * T + tc * 512: dc * T + tc * 512 + 512],
                                    start=(dc == 0), stop=(dc == 7))
                            return last
                        P.grp("tensor", qkbody,
                              reads=("qkvb",) + xn_keys(tc * 512, tc * 512 + 512), writes=pQOb)
                        P.grp("vector",
                              lambda h, dst=dst, tc=tc, hp=hp, j=j: h.tensor_scalar(
                                  out=dst[:, tc * 512: tc * 512 + 512], in0=pQO[:],
                                  scalar1=bqkvT[:, hp * 2 + j: hp * 2 + j + 1], scalar2=None,
                                  op0=ALU.add),
                              reads=pQOb + ("bqkv",), writes=(dkey,))
                        yield
                for tt in range(nT):
                    def vbody(h, tt=tt, hp=hp):
                        for dc in range(8):
                            h.matmul(pQO[:, 0:128],
                                     xnT[:, dc * T + tt * 128: dc * T + (tt + 1) * 128],
                                     qkvb[:, 2048 + dc * 128: 2048 + (dc + 1) * 128],
                                     start=(dc == 0), stop=False)
                        return h.matmul(pQO[:, 0:128], qkvb[0:1, 5248:5376],
                                        qkvb[0:1, 4096:4224],
                                        start=False, stop=True)
                    P.grp("tensor", vbody,
                          reads=("qkvb",) + xn_keys(tt * 128, tt * 128 + 128),
                          writes=pQOb)
                    P.grp("vector",
                          lambda h, tt=tt: h.tensor_copy(
                              V_sb[:, tt * 128:(tt + 1) * 128], pQO[:, 0:128]),
                          reads=pQOb, writes=("V",))
                    yield
                if hp == 0:
                    tap("QT", QT[:], ("QT",))
                    tap("KT", KT[:], ("KT",))
                    tap("V", V_sb[:], ("V",))
                for qc in range(nQC):
                    def emit_L(kt):
                        def lbody(h, qc=qc, kt=kt):
                            h.matmul(psL[:, 0:512], KT[0:64, kt * 128:(kt + 1) * 128],
                                     QT[0:64, qc * 512: qc * 512 + 512],
                                     start=True, stop=True, tile_position=(0, 0))
                            return h.matmul(psL[:, 512:1024], KT[64:128, kt * 128:(kt + 1) * 128],
                                            QT[64:128, qc * 512: qc * 512 + 512],
                                            start=True, stop=True, tile_position=(64, 0))
                        P.grp("tensor", lbody, reads=("QT", "KT"), writes=psLb)
                        P.grp("scalar",
                              lambda h: h.activation(
                                  out=st_sb[:], in_=psL[:], func=AF.Sin, scale=SIN_SCALE),
                              reads=psLb, writes=("stsb",))

                    def emit_SV(kt):
                        def svbody(h, kt=kt):
                            h.matmul(psCxA[0:64, :], V_sb[:, kt * 128: kt * 128 + 64],
                                     st_sb[:, 0:512],
                                     start=(kt == 0), stop=(kt == nT - 1), tile_position=(0, 0))
                            return h.matmul(psCxB[64:128, :], V_sb[:, kt * 128 + 64: kt * 128 + 128],
                                            st_sb[:, 512:1024],
                                            start=(kt == 0), stop=(kt == nT - 1), tile_position=(0, 64))
                        P.grp("tensor", svbody, reads=("stsb", "V"),
                              writes=psCxAb + psCxBb)

                    emit_L(0)
                    yield
                    for kt in range(1, nT):
                        emit_SV(kt - 1)
                        emit_L(kt)
                        yield
                    emit_SV(nT - 1)

                    def cxbody(h):
                        h.tensor_copy(ctx_sb[0:64, :], psCxA[0:64, :])
                        return h.tensor_copy(ctx_sb[64:128, :], psCxB[64:128, :])
                    P.grp("vector", cxbody, reads=psCxAb + psCxBb, writes=("ctx",))
                    yield
                    for st4 in range(4):
                        for n in range(2):
                            def obody(h, st4=st4, n=n, hp=hp):
                                last = h.matmul(pQO[:], ctx_sb[:, st4 * 128:(st4 + 1) * 128],
                                                qkvb[:, 3072 + n * 512: 3072 + n * 512 + 512],
                                                start=True, stop=(hp != 0))
                                if hp == 0:
                                    last = h.matmul(pQO[:], qkvb[0:1, 5248:5376],
                                                    qkvb[0:1, 4224 + n * 512: 4224 + n * 512 + 512],
                                                    start=False, stop=True)
                                return last
                            P.grp("tensor", obody,
                                  reads=("ctx", "qkvb"), writes=pQOb)
                            tt = qc * 4 + st4
                            P.grp("vector",
                                  lambda h, tt=tt, n=n: h.tensor_tensor(
                                      out=acc[:, tt * 1024 + n * 512: tt * 1024 + n * 512 + 512],
                                      in0=pQO[:],
                                      in1=acc[:, tt * 1024 + n * 512: tt * 1024 + n * 512 + 512],
                                      op=ALU.add),
                                  reads=pQOb + (f"acc{tt}",), writes=(f"acc{tt}",))
                            yield

        r_ind = gat[:, 96:112]
        r_pos = gat[:, 112:128]
        r_q = gat[:, 128:144]
        r_t1 = gat[:, 144:160]
        r_t2 = gat[:, 160:176]
        r_slot = gat[:, 176:192]
        r_w = gat[:, 192:208]
        r_eAv = gat[:, 208:209]
        r_sA = gat[:, 209:210]
        r_sB = gat[:, 210:211]
        r_wA = gat[:, 211:212]
        r_wB = gat[:, 212:213]
        r_tok = gat[:, 213:214]
        r_indA = gat[:, 224:240]
        r_indB = gat[:, 240:256]
        LTm = rout_sb[:, 0:128]
        ONEC = rout_sb[:, 128:129]
        PIDX = rout_sb[:, 129:130]
        EOFF = rout_sb[:, 130:146]
        BIGI = float(1 << 28)

        def genB():
            for tt in range(nT):
                def gbody(h, tt=tt):
                    last = None
                    for dc in range(8):
                        last = h.matmul(psG[:, 0:17],
                                        xnT[:, dc * T + tt * 128: dc * T + (tt + 1) * 128],
                                        wg_bf[:, dc * 17:(dc + 1) * 17],
                                        start=(dc == 0), stop=(dc == 7))
                    return last
                P.grp("tensor", gbody,
                      reads=("wgbf",) + xn_keys(tt * 128, tt * 128 + 128), writes=pgb)
                P.grp("vector",
                      lambda h: h.reduce_max(g_nm, psG[:, 0:16], axis=AX.X, negate=True),
                      reads=pgb, writes=("g_nm",))
                P.grp("scalar",
                      lambda h: h.activation(out=g_eg, in_=psG[:, 0:16], func=AF.Exp, bias=g_nm),
                      reads=pgb + ("g_nm",), writes=("g_eg",))
                P.grp("scalar",
                      lambda h, tt=tt: h.activation(
                          out=sig_all[:, tt:tt + 1], in_=psG[:, 16:17], func=AF.Sigmoid),
                      reads=pgb, writes=(f"sig{tt}",))
                P.grp("vector", lambda h: h.reduce_max(g_m1n, g_eg, axis=AX.X, negate=True),
                      reads=("g_eg",), writes=("g_m1n",))
                P.grp("vector",
                      lambda h: h.tensor_scalar(out=g_msk, in0=g_eg, scalar1=g_m1n,
                                                scalar2=0.0, op0=ALU.add, op1=ALU.is_ge),
                      reads=("g_eg", "g_m1n"), writes=("g_msk",))
                P.grp("vector",
                      lambda h: h.scalar_tensor_tensor(out=g_tmp, in0=g_msk, scalar=BIG,
                                                       in1=g_eg, op0=ALU.mult, op1=ALU.subtract),
                      reads=("g_msk", "g_eg"), writes=("g_tmp",))
                P.grp("vector",
                      lambda h: h.tensor_reduce(g_rmin, g_tmp, axis=AX.X, op=ALU.min),
                      reads=("g_tmp",), writes=("g_rmin",))
                P.grp("vector",
                      lambda h: h.tensor_scalar(out=g_ind, in0=g_eg, scalar1=g_rmin,
                                                scalar2=0.0, op0=ALU.add, op1=ALU.is_ge),
                      reads=("g_eg", "g_rmin"), writes=("g_ind",))
                P.grp("vector", lambda h: h.tensor_mul(g_wsel, g_eg, g_ind),
                      reads=("g_eg", "g_ind"), writes=("g_wsel",))
                P.grp("vector", lambda h: h.reduce_sum(g_s2, g_wsel, axis=AX.X),
                      reads=("g_wsel",), writes=("g_s2",))
                P.grp("vector", lambda h: h.reciprocal(g_r2, g_s2),
                      reads=("g_s2",), writes=("g_r2",))
                P.grp("vector",
                      lambda h: h.tensor_scalar(out=r_w, in0=g_wsel,
                                                scalar1=g_r2, scalar2=None, op0=ALU.mult),
                      reads=("g_wsel", "g_r2"), writes=("r_w",))
                P.grp("vector",
                      lambda h: h.tensor_scalar(out=r_ind, in0=r_w, scalar1=0.0,
                                                scalar2=None, op0=ALU.is_gt),
                      reads=("r_w",), writes=("r_ind",))

                def cums(h):
                    h.matmul(pu[:, 0:16], LTm, r_ind, start=True, stop=False)
                    h.matmul(pu[:, 0:16], rout_sb[0:1, 148:276], carry[:], start=False, stop=True)
                    return h.matmul(pu[0:1, 16:32], ONEC, r_ind, start=True, stop=True)
                P.grp("tensor", cums, reads=("rout", "carry", "r_ind"), writes=pub)
                P.grp("vector", lambda h: h.tensor_copy(r_pos, pu[:, 0:16]),
                      reads=pub, writes=("r_pos",))
                P.grp("vector", lambda h: h.tensor_add(carry[:], pu[0:1, 16:32], carry[:]),
                      reads=pub + ("carry",), writes=("carry",))
                P.grp("vector", lambda h: h.tensor_copy(pos_i[:, 0:16], r_pos),
                      reads=("r_pos",), writes=("posi",))
                P.grp("vector",
                      lambda h: h.tensor_scalar(out=pos_i[:, 16:32], in0=pos_i[:, 0:16],
                                                scalar1=127, scalar2=None,
                                                op0=ALU.bitwise_and),
                      reads=("posi",), writes=("posq",))
                P.grp("vector", lambda h: h.tensor_copy(r_q, pos_i[:, 16:32]),
                      reads=("posq",), writes=("r_q",))
                P.grp("vector",
                      lambda h: h.scalar_tensor_tensor(out=r_t1, in0=r_q, scalar=float(16 * NB),
                                                       in1=EOFF, op0=ALU.mult, op1=ALU.add),
                      reads=("r_q", "rout"), writes=("r_t1",))
                P.grp("vector", lambda h: h.tensor_sub(r_t2, r_pos, r_q),
                      reads=("r_pos", "r_q"), writes=("r_t2",))
                P.grp("vector",
                      lambda h: h.scalar_tensor_tensor(out=r_slot, in0=r_t2, scalar=1.0 / 128.0,
                                                       in1=r_t1, op0=ALU.mult, op1=ALU.add),
                      reads=("r_t2", "r_t1"), writes=("r_slot",))
                P.grp("vector",
                      lambda h: h.tensor_scalar(out=r_t1, in0=r_pos, scalar1=float(CAP),
                                                scalar2=BIGI, op0=ALU.is_ge, op1=ALU.mult),
                      reads=("r_pos",), writes=("r_t1",))
                P.grp("vector", lambda h: h.tensor_add(r_slot, r_slot, r_t1),
                      reads=("r_slot", "r_t1"), writes=("r_slot",))
                P.grp("vector",
                      lambda h: h.tensor_scalar(
                          out=r_slot, in0=r_slot, scalar1=float(16 * CAP),
                          scalar2=None, op0=ALU.min),
                      reads=("r_slot",), writes=("r_slot",))
                P.grp("vector",
                      lambda h: h.tensor_scalar(out=r_t2, in0=r_ind, scalar1=-BIGI,
                                                scalar2=BIGI, op0=ALU.mult, op1=ALU.add),
                      reads=("r_ind",), writes=("r_t2",))
                P.grp("vector",
                      lambda h: h.scalar_tensor_tensor(out=r_t1, in0=EOFF, scalar=1.0,
                                                       in1=r_t2, op0=ALU.mult, op1=ALU.add),
                      reads=("rout", "r_t2"), writes=("r_t1",))
                P.grp("vector", lambda h: h.tensor_reduce(r_eAv, r_t1, axis=AX.X, op=ALU.min),
                      reads=("r_t1",), writes=("r_eAv",))
                P.grp("vector",
                      lambda h: h.tensor_scalar(out=r_indA, in0=r_t1, scalar1=r_eAv,
                                                scalar2=None, op0=ALU.is_equal),
                      reads=("r_t1", "r_eAv"), writes=("r_indA",))
                P.grp("vector", lambda h: h.tensor_sub(r_indB, r_ind, r_indA),
                      reads=("r_ind", "r_indA"), writes=("r_indB",))
                for nm, indv, sdst, wdst in (("A", r_indA, r_sA, r_wA), ("B", r_indB, r_sB, r_wB)):
                    P.grp("vector",
                          lambda h, indv=indv: h.tensor_mul(r_t1, r_slot, indv),
                          reads=("r_slot", f"r_ind{nm}"), writes=("r_t1",))
                    P.grp("vector",
                          lambda h, sdst=sdst: h.reduce_sum(sdst, r_t1, axis=AX.X),
                          reads=("r_t1",), writes=(f"r_s{nm}",))
                    P.grp("vector",
                          lambda h, indv=indv: h.tensor_mul(r_t1, r_w, indv),
                          reads=("r_w", f"r_ind{nm}"), writes=("r_t1",))
                    P.grp("vector",
                          lambda h, wdst=wdst: h.reduce_sum(wdst, r_t1, axis=AX.X),
                          reads=("r_t1",), writes=(f"r_w{nm}",))
                P.grp("vector",
                      lambda h, tt=tt: h.tensor_scalar(out=r_tok, in0=PIDX,
                                                       scalar1=float(tt * 128),
                                                       scalar2=None, op0=ALU.add),
                      reads=("rout",), writes=("r_tok",))
                P.grp("vector", lambda h: h.tensor_copy(quad[:, 0:1], r_tok),
                      reads=("r_tok",), writes=("quad",))
                P.grp("vector", lambda h: h.tensor_copy(quad[:, 1:2], r_tok),
                      reads=("r_tok", "quad"), writes=("quad",))
                P.grp("vector", lambda h: h.tensor_copy(quad[:, 2:3], r_wA),
                      reads=("r_wA", "quad"), writes=("quad",))
                P.grp("vector", lambda h: h.tensor_copy(quad[:, 4:5], r_tok),
                      reads=("r_tok", "quad"), writes=("quad",))
                P.grp("vector",
                      lambda h: h.tensor_scalar(out=quad[:, 5:6], in0=r_tok,
                                                scalar1=float(T), scalar2=None, op0=ALU.add),
                      reads=("r_tok", "quad"), writes=("quad",))
                P.grp("vector", lambda h: h.tensor_copy(quad[:, 6:7], r_wB),
                      reads=("r_wB", "quad"), writes=("quad",))
                P.grp("vector", lambda h: h.tensor_copy(idx_i[:, 0:1], r_sA),
                      reads=("r_sA",), writes=("idxi",))
                P.grp("vector", lambda h: h.tensor_copy(idx_i[:, 1:2], r_sB),
                      reads=("r_sB", "idxi"), writes=("idxi",))
                P.grp("gpsimd",
                      lambda h: h.indirect_dma_start(
                          out=glw_d[:],
                          out_offset=IndirectOffsetOnAxis(ap=idx_i[:, 0:1], axis=0),
                          in_=quad[:, 0:4], in_offset=None),
                      reads=("quad", "idxi", "glwd"), writes=(), dsem="d_sc")
                P.grp("gpsimd",
                      lambda h: h.indirect_dma_start(
                          out=glw_d[:],
                          out_offset=IndirectOffsetOnAxis(ap=idx_i[:, 1:2], axis=0),
                          in_=quad[:, 4:8], in_offset=None),
                      reads=("quad", "idxi", "glwd"), writes=(), dsem="d_sc")
                yield
            P._wait("sync", "d_sc", P.count["d_sc"])
            P.grp("sync",
                  lambda h: h.dma_start(
                      out=glw_sb[:],
                      in_=glw_d[0:16 * CAP, :].rearrange("(p r) c -> p (r c)", p=128)),
                  reads=("glwd",), writes=("glwsb",), dsem="d_glwr")
            P.grp("vector",
                  lambda h: h.tensor_copy(
                      idxs_i[:, 0:16 * NB],
                      glw_sb[:].rearrange("p (r c) -> p r c", c=4)[:, :, 0]),
                  reads=("glwsb",), writes=("idxs",))
            P.grp("vector",
                  lambda h: h.tensor_copy(
                      idxs_i[:, 16 * NB:32 * NB],
                      glw_sb[:].rearrange("p (r c) -> p r c", c=4)[:, :, 1]),
                  reads=("glwsb", "idxs"), writes=("idxs",))
            P.grp("vector",
                  lambda h: h.tensor_copy(
                      wext[:], glw_sb[:].rearrange("p (r c) -> p r c", c=4)[:, :, 2]),
                  reads=("glwsb",), writes=("wext",))
            uctr = 0
            for e in range(NE):
                bufs = []
                bkeys = []
                for half in range(2):
                    mpar = uctr % 2
                    uctr += 1
                    P.grp("gpsimd",
                          lambda h, u=2 * e + half, mpar=mpar: h.dma_start(
                              out=mbuf[mpar][:], in_=wunits_d[u, :, :]),
                          writes=(f"mbuf{mpar}",), dsem=f"d_mb{mpar}")
                    bufs.append(mbuf[mpar])
                    bkeys.append(f"mbuf{mpar}")
                def emit_gather(scj, e=e):
                    gp = scj % 2
                    P.grp("gpsimd",
                          lambda h, e=e, scj=scj, gp=gp: h.indirect_dma_start(
                              out=xg_tok[gp][:, 0:1024],
                              out_offset=None,
                              in_=xn_d[:],
                              in_offset=IndirectOffsetOnAxis(
                                  ap=idxs_i[:, e * NB + scj: e * NB + scj + 1],
                                  axis=0)),
                          reads=("idxs", "xnhbm"), writes=(f"xgk{gp}",),
                          dsem=f"d_xg{gp}")
                emit_gather(0)
                for scj in range(CAP // SC):
                    gp = scj % 2
                    xgT = xg_tok[gp]
                    if scj + 1 < CAP // SC:
                        emit_gather(scj + 1)

                    def twav(h, xgT=xgT):
                        last = None
                        for k in range(8):
                            last = h.transpose(
                                psTX[:, k * 128:(k + 1) * 128],
                                xgT[:, k * 128: k * 128 + 128],
                                id_bf[:])
                        return last
                    P.grp("tensor", twav, reads=(f"xgk{gp}", "idp"), writes=psYb)

                    def tcop(h, xgT=xgT):
                        last = None
                        for k in range(8):
                            last = h.tensor_copy(
                                xgT[:, k * 128: k * 128 + 128],
                                psTX[:, k * 128:(k + 1) * 128])
                        return last
                    P.grp("vector", tcop, reads=psYb, writes=(f"xgk{gp}",))
                    yield
                    for hc in range(4):
                        hf, hc2 = hc // 2, hc % 2
                        bufh = bufs[hf]
                        bkey = bkeys[hf]

                        def gfill(h, bufh=bufh, hc2=hc2, xgT=xgT):
                            last = None
                            for dc in range(8):
                                last = h.matmul(
                                    pg[:, 0:SC],
                                    bufh[:, dc * 256 + hc2 * 128: dc * 256 + hc2 * 128 + 128],
                                    xgT[:, dc * SC: dc * SC + SC],
                                    start=(dc == 0), stop=(dc == 7))
                            return last
                        P.grp("tensor", gfill, reads=(bkey, f"xgk{gp}"), writes=pgb)
                        P.grp("scalar",
                              lambda h: h.activation(out=sg_sb[:, 0:SC], in_=pg[:, 0:SC], func=AF.Silu),
                              reads=pgb, writes=("sgk",))
                        yield

                        def ufill(h, bufh=bufh, hc2=hc2, xgT=xgT):
                            last = None
                            for dc in range(8):
                                last = h.matmul(
                                    pu[:, 0:SC],
                                    bufh[:, 2048 + dc * 256 + hc2 * 128: 2048 + dc * 256 + hc2 * 128 + 128],
                                    xgT[:, dc * SC: dc * SC + SC],
                                    start=(dc == 0), stop=(dc == 7))
                            return last
                        P.grp("tensor", ufill, reads=(bkey, f"xgk{gp}"), writes=pub)
                        P.grp("vector",
                              lambda h, hc=hc: h.tensor_mul(
                                  h_sb[:, hc * SC:(hc + 1) * SC], sg_sb[:, 0:SC], pu[:, 0:SC]),
                              reads=("sgk",) + pub, writes=(f"hsb{hc}",))
                        yield
                    for blk in range(SCB):
                        bg = scj * SCB + blk
                        stok_ap = idxs_i[:, 16 * NB + e * NB + bg: 16 * NB + e * NB + bg + 1]
                        wcol = wext[:, e * NB + bg: e * NB + bg + 1]
                        for n in range(2):
                            def dbody(h, blk=blk, n=n, bufs=tuple(bufs)):
                                last = None
                                for hc in range(4):
                                    last = h.matmul(
                                        psY[:],
                                        h_sb[:, hc * SC + blk * 128: hc * SC + blk * 128 + 128],
                                        bufs[hc // 2][:, 4096 + (hc % 2) * 1024 + n * 512:
                                                       4096 + (hc % 2) * 1024 + n * 512 + 512],
                                        start=(hc == 0), stop=(hc == 3))
                                return last
                            P.grp("tensor", dbody,
                                  reads=("hsb0", "hsb1", "hsb2", "hsb3") + tuple(bkeys),
                                  writes=psYb)
                            P.grp("vector",
                                  lambda h, wcol=wcol: h.tensor_scalar(
                                      out=ysc[:], in0=psY[:], scalar1=wcol,
                                      scalar2=None, op0=ALU.mult),
                                  reads=psYb + ("wext",), writes=("ysck",))
                            P.grp("gpsimd",
                                  lambda h, n=n, stok_ap=stok_ap: h.indirect_dma_start(
                                      out=y2_d[:],
                                      out_offset=IndirectOffsetOnAxis(ap=stok_ap, axis=0),
                                      in_=ysc[:], in_offset=None,
                                      element_offset=n * 512),
                                  reads=("ysck", "idxs"), writes=(), dsem="d_y2")
                            yield
            for sl in range(NSH):
                bufs = []
                bkeys = []
                for half in range(2):
                    mpar = uctr % 2
                    uctr += 1
                    P.grp("gpsimd",
                          lambda h, u=2 * NE + 2 * sl + half, mpar=mpar: h.dma_start(
                              out=mbuf[mpar][:], in_=wunits_d[u, :, :]),
                          writes=(f"mbuf{mpar}",), dsem=f"d_mb{mpar}")
                    bufs.append(mbuf[mpar])
                    bkeys.append(f"mbuf{mpar}")
                for c in range(NCH):
                    ck = xn_keys(c * 512, c * 512 + 512)
                    for hc in range(4):
                        hf, hc2 = hc // 2, hc % 2
                        bufh = bufs[hf]
                        bkey = bkeys[hf]

                        def gfill(h, bufh=bufh, hc2=hc2, c=c):
                            last = None
                            for dc in range(8):
                                last = h.matmul(
                                    pg[:], bufh[:, dc * 256 + hc2 * 128: dc * 256 + hc2 * 128 + 128],
                                    xnT[:, dc * T + c * 512: dc * T + c * 512 + 512],
                                    start=(dc == 0), stop=(dc == 7))
                            return last
                        P.grp("tensor", gfill, reads=(bkey,) + ck, writes=pgb)
                        P.grp("scalar",
                              lambda h: h.activation(out=sg_sb[:], in_=pg[:], func=AF.Silu),
                              reads=pgb, writes=("sgk",))
                        yield

                        def ufill(h, bufh=bufh, hc2=hc2, c=c):
                            last = None
                            for dc in range(8):
                                last = h.matmul(
                                    pu[:], bufh[:, 2048 + dc * 256 + hc2 * 128: 2048 + dc * 256 + hc2 * 128 + 128],
                                    xnT[:, dc * T + c * 512: dc * T + c * 512 + 512],
                                    start=(dc == 0), stop=(dc == 7))
                            return last
                        P.grp("tensor", ufill, reads=(bkey,) + ck, writes=pub)
                        P.grp("vector",
                              lambda h, hc=hc: h.tensor_mul(
                                  h_sb[:, hc * 512:(hc + 1) * 512], sg_sb[:], pu[:]),
                              reads=("sgk",) + pub, writes=(f"hsb{hc}",))
                        yield
                    for st4 in range(4):
                        tt = c * 4 + st4
                        col = sig_all[:, tt:tt + 1]
                        ckey = f"sig{tt}"
                        for n in range(2):
                            def dbody(h, st4=st4, n=n, bufs=tuple(bufs)):
                                last = None
                                for hc in range(4):
                                    last = h.matmul(
                                        psY[:],
                                        h_sb[:, hc * 512 + st4 * 128: hc * 512 + st4 * 128 + 128],
                                        bufs[hc // 2][:, 4096 + (hc % 2) * 1024 + n * 512:
                                                       4096 + (hc % 2) * 1024 + n * 512 + 512],
                                        start=(hc == 0), stop=(hc == 3))
                                return last
                            P.grp("tensor", dbody,
                                  reads=("hsb0", "hsb1", "hsb2", "hsb3") + tuple(bkeys),
                                  writes=psYb)
                            P.grp("vector",
                                  lambda h, tt=tt, n=n, col=col: h.scalar_tensor_tensor(
                                      out=acc[:, tt * 1024 + n * 512: tt * 1024 + n * 512 + 512],
                                      in0=psY[:], scalar=col,
                                      in1=acc[:, tt * 1024 + n * 512: tt * 1024 + n * 512 + 512],
                                      op0=ALU.mult, op1=ALU.add),
                                  reads=psYb + (ckey, f"acc{tt}"), writes=(f"acc{tt}",))
                            yield

        psG = pg

        nA = NHP * (2 * nQC + nT + nQC * (nT + 1 + 8))
        nB = nT + NE * (CAP // SC) * (2 * SCB + 8 + 2 * SCB * 2) + NSH * NCH * 16
        ratio = nB / nA
        ga, gb = genA(), genB()
        a_alive = b_alive = True
        mcarry = 0.0
        while a_alive or b_alive:
            if a_alive:
                try:
                    next(ga)
                except StopIteration:
                    a_alive = False
            mcarry += ratio if a_alive else 1.0
            while mcarry >= 1.0 and b_alive:
                mcarry -= 1.0
                try:
                    next(gb)
                except StopIteration:
                    b_alive = False
            if not a_alive and not b_alive:
                break

        P.barrier_wait("sync", P.barrier_vals())
        P._wait("sync", "d_y2", P.count.get("d_y2", 0))
        if T >= 4096:
            ybuf = [(KT[:, 2048 * i:2048 * i + 1024], KT[:, 2048 * i + 1024:2048 * i + 2048])
                    for i in range(2)]
            ykeys = [("KT", "KT"), ("KT", "KT")]
        else:
            ybt = [ctx(nc.sbuf_tensor(f"ybt{i}", [128, 1024], BF16)) for i in range(4)]
            ybuf = [(ybt[0][:], ybt[1][:]), (ybt[2][:], ybt[3][:])]
        for tt in range(nT):
            fp = tt % 2
            ya, yb = ybuf[fp]
            P.grp("sync",
                  lambda h, tt=tt, fp=fp: h.dma_start(
                      out=xbuf[fp][:], in_=x_d[tt * 128:(tt + 1) * 128, :]),
                  writes=(f"xb{fp}",), dsem=f"d_xb{fp}")
            P.grp("sync",
                  lambda h, tt=tt, ya=ya: h.dma_start(
                      out=ya, in_=y2_d[tt * 128:(tt + 1) * 128, :]),
                  writes=(f"ya{fp}",), dsem=f"d_ya{fp}")
            P.grp("sync",
                  lambda h, tt=tt, yb=yb: h.dma_start(
                      out=yb, in_=y2_d[T + tt * 128: T + (tt + 1) * 128, :]),
                  writes=(f"yb{fp}",), dsem=f"d_yb{fp}")
            P.grp("vector",
                  lambda h, tt=tt, fp=fp: h.tensor_add(
                      xbuf[fp][:], acc[:, tt * 1024:(tt + 1) * 1024], xbuf[fp][:]),
                  reads=(f"xb{fp}", f"acc{tt}"), writes=(f"xb{fp}",))
            P.grp("vector",
                  lambda h, fp=fp, ya=ya: h.tensor_add(xbuf[fp][:], ya, xbuf[fp][:]),
                  reads=(f"xb{fp}", f"ya{fp}"), writes=(f"xb{fp}",))
            P.grp("vector",
                  lambda h, fp=fp, yb=yb: h.tensor_add(xbuf[fp][:], yb, xbuf[fp][:]),
                  reads=(f"xb{fp}", f"yb{fp}"), writes=(f"xb{fp}",))
            P.grp("sync",
                  lambda h, tt=tt, fp=fp: h.dma_start(
                      out=out_d[tt * 128:(tt + 1) * 128, :], in_=xbuf[fp][:]),
                  reads=(f"xb{fp}",), writes=(), dsem=f"d_out{fp}")
        P._wait("sync", "d_out0", P.count["d_out0"])
        P._wait("sync", "d_out1", P.count["d_out1"])
        psM.close()

        sems = ExitStack()
        P.sems = {k: sems.enter_context(nc.semaphore(f"sem_{k}"))
                  for k in sorted(P.count)}
        with sems:
            with nc.Block() as block:
                @block.sync
                def _(h):
                    for f in P.ops["sync"]:
                        f(h)

                @block.gpsimd
                def _(h):
                    for f in P.ops["gpsimd"]:
                        f(h)

                @block.tensor
                def _(h):
                    for f in P.ops["tensor"]:
                        f(h)

                @block.vector
                def _(h):
                    for f in P.ops["vector"]:
                        f(h)

                @block.scalar
                def _(h):
                    for f in P.ops["scalar"]:
                        f(h)
    return nc


def prepare_host(inputs, T=None):
    f = np.float32
    Wq = np.ascontiguousarray(inputs["L0_attn_Wq"], f)
    Wk = np.ascontiguousarray(inputs["L0_attn_Wk"], f)
    Wv = np.ascontiguousarray(inputs["L0_attn_Wv"], f)
    Wo = np.ascontiguousarray(inputs["L0_attn_Wo"], f)
    bq = np.ascontiguousarray(inputs["L0_attn_bq"], f)
    bk = np.ascontiguousarray(inputs["L0_attn_bk"], f)
    bv = np.ascontiguousarray(inputs["L0_attn_bv"], f)
    bo = np.ascontiguousarray(inputs["L0_attn_bo"], f)
    Wg = np.ascontiguousarray(inputs["L0_ffn_Wg"], f)
    Weg = np.ascontiguousarray(inputs["L0_ffn_We_gate"], f)
    Weu = np.ascontiguousarray(inputs["L0_ffn_We_up"], f)
    Wed = np.ascontiguousarray(inputs["L0_ffn_We_down"], f)
    Wsg_ = np.ascontiguousarray(inputs["L0_ffn_Ws_gate"], f)
    Wsu = np.ascontiguousarray(inputs["L0_ffn_Ws_up"], f)
    Wsd = np.ascontiguousarray(inputs["L0_ffn_Ws_down"], f)
    Wsig = np.ascontiguousarray(inputs["L0_ffn_Wsg"], f)

    qkvwo = np.zeros((NHP, 128, 5376), f)
    for hp in range(NHP):
        cs = slice(hp * 128, hp * 128 + 128)
        for j, W in enumerate((Wq, Wk, Wv)):
            blk = W[cs, :].reshape(128, 8, 128)
            qkvwo[hp][:, j * 1024:(j + 1) * 1024] = (
                blk.transpose(2, 1, 0).reshape(128, 1024))
        qkvwo[hp][:, 3072:4096] = Wo[:, cs].T
        qkvwo[hp][0, 4096:4224] = bv[cs]
        qkvwo[hp][0, 4224:5248] = bo
        qkvwo[hp][0, 5248:5376] = 1.0

    bqkvT = np.empty((128, 16), f)
    for hp in range(NHP):
        bqkvT[:, hp * 2 + 0] = bq[hp * 128:(hp + 1) * 128]
        bqkvT[:, hp * 2 + 1] = bk[hp * 128:(hp + 1) * 128]

    def img_gud(gate, up, down):
        out = np.empty((128, 6144), f)
        out[:, 0:2048] = gate.reshape(256, 8, 128).transpose(2, 1, 0).reshape(128, 2048)
        out[:, 2048:4096] = up.reshape(256, 8, 128).transpose(2, 1, 0).reshape(128, 2048)
        out[:, 4096:6144] = down.reshape(1024, 2, 128).transpose(2, 1, 0).reshape(128, 2048)
        return out

    wunits = np.empty((NU, 128, 6144), f)
    for e in range(NE):
        for half in range(2):
            hs = slice(half * 256, half * 256 + 256)
            wunits[2 * e + half] = img_gud(Weg[e][hs, :], Weu[e][hs, :], Wed[e][:, hs])
    for s in range(NSH):
        for half in range(2):
            o = s * 512 + half * 256
            hs = slice(o, o + 256)
            wunits[2 * NE + 2 * s + half] = img_gud(Wsg_[hs, :], Wsu[hs, :], Wsd[:, hs])

    WgA = np.concatenate([Wg, Wsig], axis=0)
    wgimg = np.ascontiguousarray(
        WgA.reshape(17, 8, 128).transpose(2, 1, 0).reshape(128, 136))

    if T is None:
        T = int(np.asarray(inputs["L0_attn_x"]).shape[1])
    CAP = T // 4
    NB = CAP // 128
    rout = np.zeros((128, 280), f)
    rout[:, 0:128] = np.tril(np.ones((128, 128), f), -1).T
    rout[:, 128] = 1.0
    rout[:, 129] = np.arange(128, dtype=f)
    rout[:, 130:146] = (np.arange(16, dtype=f) * NB)[None, :]
    rout[0, 148:276] = 1.0
    fill = np.zeros((128, 64 * NB), f)
    fill[:, 1::4] = float(2 * T)
    return {
        "qkvwo": qkvwo,
        "wunits": wunits,
        "bqkvT_d": bqkvT,
        "wgimg": wgimg,
        "ident": np.eye(128, dtype=f),
        "routimg": rout,
        "fillimg": fill,
    }


def kernel(**inputs):
    x = np.ascontiguousarray(np.asarray(inputs["L0_attn_x"], np.float32))
    B, T, d = x.shape
    assert d == D
    common = prepare_host(inputs)
    nc = build(T)
    in_maps = []
    for b in range(B):
        m = dict(common)
        m["x"] = np.ascontiguousarray(x[b])
        in_maps.append(m)
    res = run_bass_kernel_spmd(nc, in_maps, core_ids=list(range(B)))
    return np.stack([r["out"] for r in res.results], axis=0)


# version 09
# speedup vs baseline: 1.1174x; 1.0205x over previous
import numpy as np
import ml_dtypes
import concourse.bass as bass
import concourse.mybir as mybir
from concourse.bass_utils import run_bass_kernel_spmd
from contextlib import ExitStack
from concourse.bass_core_rust import IndirectOffsetOnAxis

F32 = mybir.dt.float32
BF16 = mybir.dt.bfloat16
I32 = mybir.dt.int32
AF = mybir.ActivationFunctionType
ALU = mybir.AluOpType
AX = mybir.AxisListType

D = 1024
NHP = 8
NE = 16
NSH = 4
NU = 2 * NE + 2 * NSH
EPS = 1e-5
SIN_SCALE = 30.0 / 8.0
BIG = 1e30


class Prog:
    ENGINES = ("sync", "gpsimd", "tensor", "vector", "scalar")
    ENG_SEM = {"tensor": "pe", "vector": "dve", "scalar": "act",
               "sync": "sq", "gpsimd": "gq"}

    def __init__(self):
        self.ops = {e: [] for e in self.ENGINES}
        self.count = {"pe": 0, "dve": 0, "act": 0}
        self.lastwait = {e: {} for e in self.ENGINES}
        self.regions = {}
        self.sems = None

    def _region(self, key):
        r = self.regions.get(key)
        if r is None:
            r = {"w": None, "r": {}}
            self.regions[key] = r
        return r

    def _wait(self, eng, sem, val):
        if val is None or val <= 0:
            return
        if eng == "tensor" and sem == "pe":
            return
        lw = self.lastwait[eng]
        if lw.get(sem, 0) >= val:
            return
        lw[sem] = val

        def f(h, self=self, sem=sem, val=val):
            h.wait_ge(self.sems[sem], val)
        self.ops[eng].append(f)

    def grp(self, eng, body, reads=(), writes=(), dsem=None):
        if eng in ("sync", "gpsimd"):
            assert dsem is not None
        edges = {}
        for key in reads:
            r = self._region(key)
            if r["w"] is not None:
                s, v = r["w"]
                edges[s] = max(edges.get(s, 0), v)
        for key in writes:
            r = self._region(key)
            if r["w"] is not None:
                s, v = r["w"]
                edges[s] = max(edges.get(s, 0), v)
            for s, v in r["r"].items():
                edges[s] = max(edges.get(s, 0), v)
        for s in sorted(edges):
            self._wait(eng, s, edges[s])
        sem = dsem if dsem is not None else self.ENG_SEM[eng]
        amt = 16 if dsem is not None else 1
        self.count.setdefault(sem, 0)
        self.count[sem] += amt
        val = self.count[sem]

        def f(h, self=self, body=body, sem=sem, amt=amt):
            body(h).then_inc(self.sems[sem], amt)
        self.ops[eng].append(f)
        for key in reads:
            r = self._region(key)
            r["r"][sem] = max(r["r"].get(sem, 0), val)
        for key in writes:
            self.regions[key] = {"w": (sem, val), "r": {}}
        return val

    def barrier_vals(self):
        return dict(self.count)

    def barrier_wait(self, eng, vals, sems=("pe", "dve", "act")):
        for s in sems:
            self._wait(eng, s, vals.get(s, 0))


def _pbanks(nc, handle, shape, dtype):
    ml = nc.lookup_mloc(handle)
    per_part = int(np.prod(shape[1:])) * mybir.dt.size(dtype)
    nb = (per_part + 2047) // 2048
    return tuple(("ps", ml.bank + i) for i in range(max(nb, 1)))


def build(T, debug=False):
    assert T % 512 == 0
    nT = T // 128
    nQC = T // 512
    NCH = T // 512
    nc = bass.Bass("TRN2", target_bir_lowering=False, dynamic_dma_scratch_size=16384)

    x_d = nc.dram_tensor("x", [T, D], F32, kind="ExternalInput").ap()
    out_d = nc.dram_tensor("out", [T, D], F32, kind="ExternalOutput").ap()
    qkvwo_d = nc.dram_tensor("qkvwo", [NHP, 128, 5376], F32, kind="ExternalInput").ap()
    wunits_d = nc.dram_tensor("wunits", [NU, 128, 6144], F32, kind="ExternalInput").ap()
    bqkv_d = nc.dram_tensor("bqkvT_d", [128, 16], F32, kind="ExternalInput").ap()
    CAP = T // 4
    NB = CAP // 128
    SC = min(128, CAP)
    SCB = SC // 128
    rout_d = nc.dram_tensor("routimg", [128, 280], F32, kind="ExternalInput").ap()
    fill_d = nc.dram_tensor("fillimg", [128, 64 * NB], F32, kind="ExternalInput").ap()
    glw_d = nc.dram_tensor("glw", [16 * CAP + 8, 4], F32, kind="Internal").ap()
    y2_d = nc.dram_tensor("y2", [2 * T + 128, 1024], BF16, kind="Internal").ap()
    xn_d = nc.dram_tensor("xnhbm", [T, 1024], BF16, kind="Internal").ap()
    wg_d = nc.dram_tensor("wgimg", [128, 136], F32, kind="ExternalInput").ap()
    id_d = nc.dram_tensor("ident", [128, 128], F32, kind="ExternalInput").ap()

    P = Prog()

    def tap(name, ap, reads, bf=True):
        if not debug or (isinstance(debug, (list, tuple, set)) and name not in debug):
            return
        d = nc.dram_tensor(f"dbg_{name}", list(ap.shape), F32, kind="ExternalOutput").ap()
        eng = "gpsimd" if bf else "sync"
        P.grp(eng, lambda h: h.dma_start(out=d[:], in_=ap), reads=tuple(reads), writes=(),
              dsem="d_tapg" if bf else "d_tap")

    def xn_keys(t0, t1):
        return tuple(f"xnT{i}" for i in range(t0 // 128, (t1 + 127) // 128))

    es = ExitStack()
    ctx = es.enter_context
    with es:
        acc = ctx(nc.sbuf_tensor("acc", [128, nT * 1024], BF16))
        xnT = ctx(nc.sbuf_tensor("xnT", [128, 8 * T], BF16))
        wg_bf = ctx(nc.sbuf_tensor("wg_bf", [128, 136], BF16))
        bqkvT = ctx(nc.sbuf_tensor("bqkvT", [128, 16], F32))
        stats = ctx(nc.sbuf_tensor("stats", [128, 16], F32))
        sig_all = ctx(nc.sbuf_tensor("sig_all", [128, T // 128], F32))
        idxs_i = ctx(nc.sbuf_tensor("idxs_i", [128, 32 * NB], I32))
        wext = ctx(nc.sbuf_tensor("wext", [128, 16 * NB], F32))
        xg_tok = [ctx(nc.sbuf_tensor(f"xg_tok{i}", [128, SCB * 1024], BF16)) for i in range(2)]
        assert SCB == 1
        ysc = ctx(nc.sbuf_tensor("ysc0", [128, 512], BF16))
        qkvb = ctx(nc.sbuf_tensor("qkvb", [128, 5376], BF16))
        if T >= 4096:
            sxb = ExitStack()
            xbuf = [sxb.enter_context(nc.sbuf_tensor(f"xbuf{i}", [128, 1024], F32)) for i in range(2)]
            sxb.close()
            QT = ctx(nc.sbuf_tensor("QT", [128, T], BF16))
            assert nc.lookup_mloc(QT).addr == nc.lookup_mloc(xbuf[0]).addr
        else:
            xbuf = [ctx(nc.sbuf_tensor(f"xbuf{i}", [128, 1024], F32)) for i in range(2)]
            QT = ctx(nc.sbuf_tensor("QT", [128, T], BF16))
        KT = ctx(nc.sbuf_tensor("KT", [128, T], BF16))
        V_sb = ctx(nc.sbuf_tensor("V_sb", [128, T], BF16))
        st_sb = ctx(nc.sbuf_tensor("stsb", [128, 1024], BF16))
        ctx_sb = ctx(nc.sbuf_tensor("ctxsb", [128, 512], BF16))
        id_bf = ctx(nc.sbuf_tensor("id_perm", [128, 128], BF16))
        sg_sb = ctx(nc.sbuf_tensor("sg_sb", [128, 512], BF16))
        srt = ExitStack()
        gat = srt.enter_context(nc.sbuf_tensor("gat", [128, 256], F32))
        rout_sb = srt.enter_context(nc.sbuf_tensor("rout_sb", [128, 280], F32))
        carry = srt.enter_context(nc.sbuf_tensor("carry", [1, 16], F32))
        glw_sb = srt.enter_context(nc.sbuf_tensor("glw_sb", [128, 64 * NB], F32))
        quad = srt.enter_context(nc.sbuf_tensor("quad", [128, 8], F32))
        idx_i = srt.enter_context(nc.sbuf_tensor("idx_i", [128, 2], I32))
        pos_i = srt.enter_context(nc.sbuf_tensor("pos_i", [128, 32], I32))
        hole_start = nc.lookup_mloc(gat).addr
        hole_end = nc.lookup_mloc(pos_i).addr + 128
        srt.close()
        h_sb = ctx(nc.sbuf_tensor("h_sb", [128, 2048], BF16))
        assert nc.lookup_mloc(h_sb).addr == hole_start
        spacer = hole_end - (hole_start + 4096)
        if spacer > 0:
            ctx(nc.sbuf_tensor("h_spacer", [128, spacer], mybir.dt.uint8))
        mbuf = [ctx(nc.sbuf_tensor(f"mbuf{i}", [128, 6144], BF16)) for i in range(2)]
        assert nc.lookup_mloc(mbuf[0]).addr >= hole_end

        if T >= 2048:
            sq_sb = KT[:, 1024:2048]
            xnbf = KT[:, 0:1024]
            sq_key, xnbf_key = "KT", "KT"
        else:
            sq_t = ctx(nc.sbuf_tensor("sq_t", [128, 1024], F32))
            xnbf_t = ctx(nc.sbuf_tensor("xnbf_t", [128, 1024], BF16))
            sq_sb, xnbf = sq_t[:], xnbf_t[:]
            sq_key, xnbf_key = "sqt", "xnbft"
        ps0s = ExitStack()
        psTR = [ps0s.enter_context(nc.psum_tensor(f"psTR{i}", [128, 128], BF16)) for i in range(2)]
        trb = [_pbanks(nc, psTR[i], [128, 128], BF16) for i in range(2)]

        P.grp("sync", lambda h: h.dma_start(out=bqkvT[:], in_=bqkv_d[:]),
              writes=("bqkv",), dsem="d_cst")
        P.grp("gpsimd", lambda h: h.dma_start(out=id_bf[:], in_=id_d[:]),
              writes=("idp",), dsem="d_cstg")
        P.grp("gpsimd", lambda h: h.dma_start(out=wg_bf[:], in_=wg_d[:]),
              writes=("wgbf",), dsem="d_cstg")
        P.grp("sync", lambda h: h.dma_start(out=rout_sb[:], in_=rout_d[:]),
              writes=("rout",), dsem="d_cst")
        P.grp("vector", lambda h: h.memset(carry[:], 0.0), writes=("carry",))
        P.grp("vector", lambda h: h.memset(quad[:], 0.0), writes=("quad",))
        P.grp("sync", lambda h: h.dma_start(out=glw_sb[:], in_=fill_d[:]),
              writes=("glwsb",), dsem="d_cst")
        P.grp("sync",
              lambda h: h.dma_start(
                  out=glw_d[0:16 * CAP, :].rearrange("(p r) c -> p (r c)", p=128),
                  in_=glw_sb[:]),
              reads=("glwsb",), writes=("glwd",), dsem="d_cst")
        for _k in ("bqkv", "rout", "glwsb", "glwd"):
            P.regions[_k]["w"] = ("d_cst", P.count["d_cst"])
        for _k in ("idp", "wgbf"):
            P.regions[_k]["w"] = ("d_cstg", P.count["d_cstg"])
        acc_keys = tuple(f"acc{tt}" for tt in range(nT))
        P.grp("vector", lambda h: h.memset(acc[:], 0.0), writes=acc_keys)

        inv_d = 1.0 / D
        for i in range(nT):
            par = i % 2
            xt = xbuf[par]
            st = par * 8
            P.grp("sync",
                  lambda h, o=xt, i=i: h.dma_start(out=o[:], in_=x_d[i * 128:(i + 1) * 128, :]),
                  writes=(f"xb{par}",), dsem=f"d_xb{par}")
            P.grp("scalar",
                  lambda h, xt=xt, st=st: h.activation(
                      out=sq_sb, in_=xt[:], func=AF.Square,
                      accum_out=stats[:, st + 1:st + 2]),
                  reads=(f"xb{par}",), writes=(sq_key, f"s2_{par}"))
            P.grp("vector",
                  lambda h, xt=xt, st=st: h.reduce_sum(stats[:, st:st + 1], xt[:], axis=AX.X),
                  reads=(f"xb{par}",), writes=(f"s1_{par}",))
            P.grp("vector",
                  lambda h, st=st: h.tensor_scalar(
                      out=stats[:, st + 2:st + 3], in0=stats[:, st:st + 1],
                      scalar1=inv_d, scalar2=None, op0=ALU.mult),
                  reads=(f"s1_{par}",), writes=(f"mean_{par}",))
            P.grp("vector",
                  lambda h, st=st: h.tensor_scalar(
                      out=stats[:, st + 3:st + 4], in0=stats[:, st + 1:st + 2],
                      scalar1=inv_d, scalar2=None, op0=ALU.mult),
                  reads=(f"s2_{par}",), writes=(f"e2_{par}",))
            P.grp("vector",
                  lambda h, st=st: h.tensor_mul(
                      stats[:, st + 4:st + 5], stats[:, st + 2:st + 3], stats[:, st + 2:st + 3]),
                  reads=(f"mean_{par}",), writes=(f"msq_{par}",))
            P.grp("vector",
                  lambda h, st=st: h.tensor_scalar(
                      out=stats[:, st + 5:st + 6], in0=stats[:, st + 3:st + 4],
                      scalar1=stats[:, st + 4:st + 5], scalar2=EPS,
                      op0=ALU.subtract, op1=ALU.add),
                  reads=(f"e2_{par}", f"msq_{par}"), writes=(f"var_{par}",))
            P.grp("scalar",
                  lambda h, st=st: h.activation(
                      out=stats[:, st + 6:st + 7], in_=stats[:, st + 5:st + 6],
                      func=AF.Sqrt),
                  reads=(f"var_{par}",), writes=(f"std_{par}",))
            P.grp("vector",
                  lambda h, st=st: h.reciprocal(stats[:, st + 7:st + 8], stats[:, st + 6:st + 7]),
                  reads=(f"std_{par}",), writes=(f"rstd_{par}",))
            P.grp("vector",
                  lambda h, xt=xt, st=st: h.tensor_scalar(
                      out=xnbf, in0=xt[:],
                      scalar1=stats[:, st + 2:st + 3], scalar2=stats[:, st + 7:st + 8],
                      op0=ALU.subtract, op1=ALU.mult),
                  reads=(f"xb{par}", f"mean_{par}", f"rstd_{par}"),
                  writes=(xnbf_key,))
            P.grp("sync",
                  lambda h, i=i: h.dma_start(
                      out=xn_d[i * 128:(i + 1) * 128, :], in_=xnbf),
                  reads=(xnbf_key,), writes=("xnhbm",), dsem="d_xn")
            for dc in range(8):
                tr = dc % 2
                P.grp("tensor",
                      lambda h, dc=dc, tr=tr: h.transpose(
                          psTR[tr][:], xnbf[:, dc * 128:(dc + 1) * 128], id_bf[:]),
                      reads=(xnbf_key, "idp"), writes=trb[tr])
                P.grp("vector",
                      lambda h, i=i, dc=dc, tr=tr: h.tensor_copy(
                          xnT[:, dc * T + i * 128: dc * T + (i + 1) * 128], psTR[tr][:]),
                      reads=trb[tr], writes=(f"xnT{i}",))
        all_xn = xn_keys(0, T)
        tap("xnT", xnT[:], all_xn)
        vals0 = P.barrier_vals()
        ps0s.close()

        psM = ExitStack()
        psL = psM.enter_context(nc.psum_tensor("psL", [128, 1024], F32))
        psCxA = psM.enter_context(nc.psum_tensor("psCxA", [128, 512], F32))
        psCxB = psM.enter_context(nc.psum_tensor("psCxB", [128, 512], F32))
        pQO = psM.enter_context(nc.psum_tensor("pQO", [128, 512], F32))
        pg = psM.enter_context(nc.psum_tensor("pg", [128, 512], F32))
        pu = psM.enter_context(nc.psum_tensor("pu", [128, 512], F32))
        spx = ExitStack()
        psTX = spx.enter_context(nc.psum_tensor("psTX", [128, 1024], BF16))
        spx.close()
        psY = psM.enter_context(nc.psum_tensor("psY", [128, 512], F32))
        assert nc.lookup_mloc(psTX).bank == nc.lookup_mloc(psY).bank
        psLb = _pbanks(nc, psL, [128, 1024], F32)
        psCxAb = _pbanks(nc, psCxA, [128, 512], F32)
        psCxBb = _pbanks(nc, psCxB, [128, 512], F32)
        pQOb = _pbanks(nc, pQO, [128, 512], F32)
        pgb = _pbanks(nc, pg, [128, 512], F32)
        pub = _pbanks(nc, pu, [128, 512], F32)
        psYb = _pbanks(nc, psY, [128, 512], F32)

        g_eg = gat[:, 0:16]
        g_msk = gat[:, 16:32]
        g_tmp = gat[:, 32:48]
        g_ind = gat[:, 48:64]
        g_wsel = gat[:, 64:80]
        g_nm = gat[:, 80:81]
        g_rmin = gat[:, 81:82]
        g_s2 = gat[:, 82:83]
        g_r2 = gat[:, 83:84]
        g_m1n = gat[:, 84:85]

        def genA():
            for hp in range(NHP):
                P.grp("gpsimd",
                      lambda h, hp=hp: h.dma_start(out=qkvb[:], in_=qkvwo_d[hp, :, :]),
                      writes=("qkvb",), dsem="d_qkvb")
                yield
                for j, dkey in ((0, "QT"), (1, "KT")):
                    dst = QT if j == 0 else KT
                    for tc in range(nQC):
                        def qkbody(h, j=j, tc=tc):
                            last = None
                            for dc in range(8):
                                last = h.matmul(
                                    pQO[:], qkvb[:, j * 1024 + dc * 128: j * 1024 + (dc + 1) * 128],
                                    xnT[:, dc * T + tc * 512: dc * T + tc * 512 + 512],
                                    start=(dc == 0), stop=(dc == 7))
                            return last
                        P.grp("tensor", qkbody,
                              reads=("qkvb",) + xn_keys(tc * 512, tc * 512 + 512), writes=pQOb)
                        P.grp("vector",
                              lambda h, dst=dst, tc=tc, hp=hp, j=j: h.tensor_scalar(
                                  out=dst[:, tc * 512: tc * 512 + 512], in0=pQO[:],
                                  scalar1=bqkvT[:, hp * 2 + j: hp * 2 + j + 1], scalar2=None,
                                  op0=ALU.add),
                              reads=pQOb + ("bqkv",), writes=(dkey,))
                        yield
                for tt in range(nT):
                    def vbody(h, tt=tt, hp=hp):
                        for dc in range(8):
                            h.matmul(pQO[:, 0:128],
                                     xnT[:, dc * T + tt * 128: dc * T + (tt + 1) * 128],
                                     qkvb[:, 2048 + dc * 128: 2048 + (dc + 1) * 128],
                                     start=(dc == 0), stop=False)
                        return h.matmul(pQO[:, 0:128], qkvb[0:1, 5248:5376],
                                        qkvb[0:1, 4096:4224],
                                        start=False, stop=True)
                    P.grp("tensor", vbody,
                          reads=("qkvb",) + xn_keys(tt * 128, tt * 128 + 128),
                          writes=pQOb)
                    P.grp("vector",
                          lambda h, tt=tt: h.tensor_copy(
                              V_sb[:, tt * 128:(tt + 1) * 128], pQO[:, 0:128]),
                          reads=pQOb, writes=("V",))
                    yield
                if hp == 0:
                    tap("QT", QT[:], ("QT",))
                    tap("KT", KT[:], ("KT",))
                    tap("V", V_sb[:], ("V",))
                for qc in range(nQC):
                    def emit_L(kt):
                        def lbody(h, qc=qc, kt=kt):
                            h.matmul(psL[:, 0:512], KT[0:64, kt * 128:(kt + 1) * 128],
                                     QT[0:64, qc * 512: qc * 512 + 512],
                                     start=True, stop=True, tile_position=(0, 0))
                            return h.matmul(psL[:, 512:1024], KT[64:128, kt * 128:(kt + 1) * 128],
                                            QT[64:128, qc * 512: qc * 512 + 512],
                                            start=True, stop=True, tile_position=(64, 0))
                        P.grp("tensor", lbody, reads=("QT", "KT"), writes=psLb)
                        P.grp("scalar",
                              lambda h: h.activation(
                                  out=st_sb[:], in_=psL[:], func=AF.Sin, scale=SIN_SCALE),
                              reads=psLb, writes=("stsb",))

                    def emit_SV(kt):
                        def svbody(h, kt=kt):
                            h.matmul(psCxA[0:64, :], V_sb[:, kt * 128: kt * 128 + 64],
                                     st_sb[:, 0:512],
                                     start=(kt == 0), stop=(kt == nT - 1), tile_position=(0, 0))
                            return h.matmul(psCxB[64:128, :], V_sb[:, kt * 128 + 64: kt * 128 + 128],
                                            st_sb[:, 512:1024],
                                            start=(kt == 0), stop=(kt == nT - 1), tile_position=(0, 64))
                        P.grp("tensor", svbody, reads=("stsb", "V"),
                              writes=psCxAb + psCxBb)

                    emit_L(0)
                    yield
                    for kt in range(1, nT):
                        emit_SV(kt - 1)
                        emit_L(kt)
                        yield
                    emit_SV(nT - 1)

                    def cxbody(h):
                        h.tensor_copy(ctx_sb[0:64, :], psCxA[0:64, :])
                        return h.tensor_copy(ctx_sb[64:128, :], psCxB[64:128, :])
                    P.grp("vector", cxbody, reads=psCxAb + psCxBb, writes=("ctx",))
                    yield
                    for st4 in range(4):
                        for n in range(2):
                            def obody(h, st4=st4, n=n, hp=hp):
                                last = h.matmul(pQO[:], ctx_sb[:, st4 * 128:(st4 + 1) * 128],
                                                qkvb[:, 3072 + n * 512: 3072 + n * 512 + 512],
                                                start=True, stop=(hp != 0))
                                if hp == 0:
                                    last = h.matmul(pQO[:], qkvb[0:1, 5248:5376],
                                                    qkvb[0:1, 4224 + n * 512: 4224 + n * 512 + 512],
                                                    start=False, stop=True)
                                return last
                            P.grp("tensor", obody,
                                  reads=("ctx", "qkvb"), writes=pQOb)
                            tt = qc * 4 + st4
                            P.grp("vector",
                                  lambda h, tt=tt, n=n: h.tensor_tensor(
                                      out=acc[:, tt * 1024 + n * 512: tt * 1024 + n * 512 + 512],
                                      in0=pQO[:],
                                      in1=acc[:, tt * 1024 + n * 512: tt * 1024 + n * 512 + 512],
                                      op=ALU.add),
                                  reads=pQOb + (f"acc{tt}",), writes=(f"acc{tt}",))
                            yield

        r_ind = gat[:, 96:112]
        r_pos = gat[:, 112:128]
        r_q = gat[:, 128:144]
        r_t1 = gat[:, 144:160]
        r_t2 = gat[:, 160:176]
        r_slot = gat[:, 176:192]
        r_w = gat[:, 192:208]
        r_eAv = gat[:, 208:209]
        r_sA = gat[:, 209:210]
        r_sB = gat[:, 210:211]
        r_wA = gat[:, 211:212]
        r_wB = gat[:, 212:213]
        r_tok = gat[:, 213:214]
        r_indA = gat[:, 224:240]
        r_indB = gat[:, 240:256]
        LTm = rout_sb[:, 0:128]
        ONEC = rout_sb[:, 128:129]
        PIDX = rout_sb[:, 129:130]
        EOFF = rout_sb[:, 130:146]
        BIGI = float(1 << 28)

        def genB():
            for tt in range(nT):
                def gbody(h, tt=tt):
                    last = None
                    for dc in range(8):
                        last = h.matmul(psG[:, 0:17],
                                        xnT[:, dc * T + tt * 128: dc * T + (tt + 1) * 128],
                                        wg_bf[:, dc * 17:(dc + 1) * 17],
                                        start=(dc == 0), stop=(dc == 7))
                    return last
                P.grp("tensor", gbody,
                      reads=("wgbf",) + xn_keys(tt * 128, tt * 128 + 128), writes=pgb)
                P.grp("vector",
                      lambda h: h.reduce_max(g_nm, psG[:, 0:16], axis=AX.X, negate=True),
                      reads=pgb, writes=("g_nm",))
                P.grp("scalar",
                      lambda h: h.activation(out=g_eg, in_=psG[:, 0:16], func=AF.Exp, bias=g_nm),
                      reads=pgb + ("g_nm",), writes=("g_eg",))
                P.grp("scalar",
                      lambda h, tt=tt: h.activation(
                          out=sig_all[:, tt:tt + 1], in_=psG[:, 16:17], func=AF.Sigmoid),
                      reads=pgb, writes=(f"sig{tt}",))
                P.grp("vector", lambda h: h.reduce_max(g_m1n, g_eg, axis=AX.X, negate=True),
                      reads=("g_eg",), writes=("g_m1n",))
                P.grp("vector",
                      lambda h: h.tensor_scalar(out=g_msk, in0=g_eg, scalar1=g_m1n,
                                                scalar2=0.0, op0=ALU.add, op1=ALU.is_ge),
                      reads=("g_eg", "g_m1n"), writes=("g_msk",))
                P.grp("vector",
                      lambda h: h.scalar_tensor_tensor(out=g_tmp, in0=g_msk, scalar=BIG,
                                                       in1=g_eg, op0=ALU.mult, op1=ALU.subtract),
                      reads=("g_msk", "g_eg"), writes=("g_tmp",))
                P.grp("vector",
                      lambda h: h.tensor_reduce(g_rmin, g_tmp, axis=AX.X, op=ALU.min),
                      reads=("g_tmp",), writes=("g_rmin",))
                P.grp("vector",
                      lambda h: h.tensor_scalar(out=g_ind, in0=g_eg, scalar1=g_rmin,
                                                scalar2=0.0, op0=ALU.add, op1=ALU.is_ge),
                      reads=("g_eg", "g_rmin"), writes=("g_ind",))
                P.grp("vector", lambda h: h.tensor_mul(g_wsel, g_eg, g_ind),
                      reads=("g_eg", "g_ind"), writes=("g_wsel",))
                P.grp("vector", lambda h: h.reduce_sum(g_s2, g_wsel, axis=AX.X),
                      reads=("g_wsel",), writes=("g_s2",))
                P.grp("vector", lambda h: h.reciprocal(g_r2, g_s2),
                      reads=("g_s2",), writes=("g_r2",))
                P.grp("vector",
                      lambda h: h.tensor_scalar(out=r_w, in0=g_wsel,
                                                scalar1=g_r2, scalar2=None, op0=ALU.mult),
                      reads=("g_wsel", "g_r2"), writes=("r_w",))
                P.grp("vector",
                      lambda h: h.tensor_scalar(out=r_ind, in0=r_w, scalar1=0.0,
                                                scalar2=None, op0=ALU.is_gt),
                      reads=("r_w",), writes=("r_ind",))

                def cums(h):
                    h.matmul(pu[:, 0:16], LTm, r_ind, start=True, stop=False)
                    h.matmul(pu[:, 0:16], rout_sb[0:1, 148:276], carry[:], start=False, stop=True)
                    return h.matmul(pu[0:1, 16:32], ONEC, r_ind, start=True, stop=True)
                P.grp("tensor", cums, reads=("rout", "carry", "r_ind"), writes=pub)
                P.grp("vector", lambda h: h.tensor_copy(r_pos, pu[:, 0:16]),
                      reads=pub, writes=("r_pos",))
                P.grp("vector", lambda h: h.tensor_add(carry[:], pu[0:1, 16:32], carry[:]),
                      reads=pub + ("carry",), writes=("carry",))
                P.grp("vector", lambda h: h.tensor_copy(pos_i[:, 0:16], r_pos),
                      reads=("r_pos",), writes=("posi",))
                P.grp("vector",
                      lambda h: h.tensor_scalar(out=pos_i[:, 16:32], in0=pos_i[:, 0:16],
                                                scalar1=127, scalar2=None,
                                                op0=ALU.bitwise_and),
                      reads=("posi",), writes=("posq",))
                P.grp("vector", lambda h: h.tensor_copy(r_q, pos_i[:, 16:32]),
                      reads=("posq",), writes=("r_q",))
                P.grp("vector",
                      lambda h: h.scalar_tensor_tensor(out=r_t1, in0=r_q, scalar=float(16 * NB),
                                                       in1=EOFF, op0=ALU.mult, op1=ALU.add),
                      reads=("r_q", "rout"), writes=("r_t1",))
                P.grp("vector", lambda h: h.tensor_sub(r_t2, r_pos, r_q),
                      reads=("r_pos", "r_q"), writes=("r_t2",))
                P.grp("vector",
                      lambda h: h.scalar_tensor_tensor(out=r_slot, in0=r_t2, scalar=1.0 / 128.0,
                                                       in1=r_t1, op0=ALU.mult, op1=ALU.add),
                      reads=("r_t2", "r_t1"), writes=("r_slot",))
                P.grp("vector",
                      lambda h: h.tensor_scalar(out=r_t1, in0=r_pos, scalar1=float(CAP),
                                                scalar2=BIGI, op0=ALU.is_ge, op1=ALU.mult),
                      reads=("r_pos",), writes=("r_t1",))
                P.grp("vector", lambda h: h.tensor_add(r_slot, r_slot, r_t1),
                      reads=("r_slot", "r_t1"), writes=("r_slot",))
                P.grp("vector",
                      lambda h: h.tensor_scalar(
                          out=r_slot, in0=r_slot, scalar1=float(16 * CAP),
                          scalar2=None, op0=ALU.min),
                      reads=("r_slot",), writes=("r_slot",))
                P.grp("vector",
                      lambda h: h.tensor_scalar(out=r_t2, in0=r_ind, scalar1=-BIGI,
                                                scalar2=BIGI, op0=ALU.mult, op1=ALU.add),
                      reads=("r_ind",), writes=("r_t2",))
                P.grp("vector",
                      lambda h: h.scalar_tensor_tensor(out=r_t1, in0=EOFF, scalar=1.0,
                                                       in1=r_t2, op0=ALU.mult, op1=ALU.add),
                      reads=("rout", "r_t2"), writes=("r_t1",))
                P.grp("vector", lambda h: h.tensor_reduce(r_eAv, r_t1, axis=AX.X, op=ALU.min),
                      reads=("r_t1",), writes=("r_eAv",))
                P.grp("vector",
                      lambda h: h.tensor_scalar(out=r_indA, in0=r_t1, scalar1=r_eAv,
                                                scalar2=None, op0=ALU.is_equal),
                      reads=("r_t1", "r_eAv"), writes=("r_indA",))
                P.grp("vector", lambda h: h.tensor_sub(r_indB, r_ind, r_indA),
                      reads=("r_ind", "r_indA"), writes=("r_indB",))
                for nm, indv, sdst, wdst in (("A", r_indA, r_sA, r_wA), ("B", r_indB, r_sB, r_wB)):
                    P.grp("vector",
                          lambda h, indv=indv: h.tensor_mul(r_t1, r_slot, indv),
                          reads=("r_slot", f"r_ind{nm}"), writes=("r_t1",))
                    P.grp("vector",
                          lambda h, sdst=sdst: h.reduce_sum(sdst, r_t1, axis=AX.X),
                          reads=("r_t1",), writes=(f"r_s{nm}",))
                    P.grp("vector",
                          lambda h, indv=indv: h.tensor_mul(r_t1, r_w, indv),
                          reads=("r_w", f"r_ind{nm}"), writes=("r_t1",))
                    P.grp("vector",
                          lambda h, wdst=wdst: h.reduce_sum(wdst, r_t1, axis=AX.X),
                          reads=("r_t1",), writes=(f"r_w{nm}",))
                P.grp("vector",
                      lambda h, tt=tt: h.tensor_scalar(out=r_tok, in0=PIDX,
                                                       scalar1=float(tt * 128),
                                                       scalar2=None, op0=ALU.add),
                      reads=("rout",), writes=("r_tok",))
                P.grp("vector", lambda h: h.tensor_copy(quad[:, 0:1], r_tok),
                      reads=("r_tok",), writes=("quad",))
                P.grp("vector", lambda h: h.tensor_copy(quad[:, 1:2], r_tok),
                      reads=("r_tok", "quad"), writes=("quad",))
                P.grp("vector", lambda h: h.tensor_copy(quad[:, 2:3], r_wA),
                      reads=("r_wA", "quad"), writes=("quad",))
                P.grp("vector", lambda h: h.tensor_copy(quad[:, 4:5], r_tok),
                      reads=("r_tok", "quad"), writes=("quad",))
                P.grp("vector",
                      lambda h: h.tensor_scalar(out=quad[:, 5:6], in0=r_tok,
                                                scalar1=float(T), scalar2=None, op0=ALU.add),
                      reads=("r_tok", "quad"), writes=("quad",))
                P.grp("vector", lambda h: h.tensor_copy(quad[:, 6:7], r_wB),
                      reads=("r_wB", "quad"), writes=("quad",))
                P.grp("vector", lambda h: h.tensor_copy(idx_i[:, 0:1], r_sA),
                      reads=("r_sA",), writes=("idxi",))
                P.grp("vector", lambda h: h.tensor_copy(idx_i[:, 1:2], r_sB),
                      reads=("r_sB", "idxi"), writes=("idxi",))
                P.grp("gpsimd",
                      lambda h: h.indirect_dma_start(
                          out=glw_d[:],
                          out_offset=IndirectOffsetOnAxis(ap=idx_i[:, 0:1], axis=0),
                          in_=quad[:, 0:4], in_offset=None),
                      reads=("quad", "idxi", "glwd"), writes=(), dsem="d_sc")
                P.grp("gpsimd",
                      lambda h: h.indirect_dma_start(
                          out=glw_d[:],
                          out_offset=IndirectOffsetOnAxis(ap=idx_i[:, 1:2], axis=0),
                          in_=quad[:, 4:8], in_offset=None),
                      reads=("quad", "idxi", "glwd"), writes=(), dsem="d_sc")
                yield
            P._wait("sync", "d_sc", P.count["d_sc"])
            P.grp("sync",
                  lambda h: h.dma_start(
                      out=glw_sb[:],
                      in_=glw_d[0:16 * CAP, :].rearrange("(p r) c -> p (r c)", p=128)),
                  reads=("glwd",), writes=("glwsb",), dsem="d_glwr")
            P.grp("vector",
                  lambda h: h.tensor_copy(
                      idxs_i[:, 0:16 * NB],
                      glw_sb[:].rearrange("p (r c) -> p r c", c=4)[:, :, 0]),
                  reads=("glwsb",), writes=("idxs",))
            P.grp("vector",
                  lambda h: h.tensor_copy(
                      idxs_i[:, 16 * NB:32 * NB],
                      glw_sb[:].rearrange("p (r c) -> p r c", c=4)[:, :, 1]),
                  reads=("glwsb", "idxs"), writes=("idxs",))
            P.grp("vector",
                  lambda h: h.tensor_copy(
                      wext[:], glw_sb[:].rearrange("p (r c) -> p r c", c=4)[:, :, 2]),
                  reads=("glwsb",), writes=("wext",))
            uctr = 0
            for e in range(NE):
                def emit_gather(scj, e=e):
                    gp = scj % 2
                    P.grp("gpsimd",
                          lambda h, e=e, scj=scj, gp=gp: h.indirect_dma_start(
                              out=xg_tok[gp][:, 0:1024],
                              out_offset=None,
                              in_=xn_d[:],
                              in_offset=IndirectOffsetOnAxis(
                                  ap=idxs_i[:, e * NB + scj: e * NB + scj + 1],
                                  axis=0)),
                          reads=("idxs", "xnhbm"), writes=(f"xgk{gp}",),
                          dsem=f"d_xg{gp}")
                emit_gather(0)
                bufs = []
                bkeys = []
                for half in range(2):
                    mpar = uctr % 2
                    uctr += 1
                    P.grp("gpsimd",
                          lambda h, u=2 * e + half, mpar=mpar: h.dma_start(
                              out=mbuf[mpar][:], in_=wunits_d[u, :, :]),
                          writes=(f"mbuf{mpar}",), dsem=f"d_mb{mpar}")
                    bufs.append(mbuf[mpar])
                    bkeys.append(f"mbuf{mpar}")
                for scj in range(CAP // SC):
                    gp = scj % 2
                    xgT = xg_tok[gp]
                    if scj + 1 < CAP // SC:
                        emit_gather(scj + 1)

                    def twav(h, xgT=xgT):
                        last = None
                        for k in range(8):
                            last = h.transpose(
                                psTX[:, k * 128:(k + 1) * 128],
                                xgT[:, k * 128: k * 128 + 128],
                                id_bf[:])
                        return last
                    P.grp("tensor", twav, reads=(f"xgk{gp}", "idp"), writes=psYb)

                    def tcop(h, xgT=xgT):
                        last = None
                        for k in range(8):
                            last = h.tensor_copy(
                                xgT[:, k * 128: k * 128 + 128],
                                psTX[:, k * 128:(k + 1) * 128])
                        return last
                    P.grp("vector", tcop, reads=psYb, writes=(f"xgk{gp}",))
                    yield
                    for hc in range(4):
                        hf, hc2 = hc // 2, hc % 2
                        bufh = bufs[hf]
                        bkey = bkeys[hf]

                        def gfill(h, bufh=bufh, hc2=hc2, xgT=xgT):
                            last = None
                            for dc in range(8):
                                last = h.matmul(
                                    pg[:, 0:SC],
                                    bufh[:, dc * 256 + hc2 * 128: dc * 256 + hc2 * 128 + 128],
                                    xgT[:, dc * SC: dc * SC + SC],
                                    start=(dc == 0), stop=(dc == 7))
                            return last
                        P.grp("tensor", gfill, reads=(bkey, f"xgk{gp}"), writes=pgb)
                        P.grp("scalar",
                              lambda h: h.activation(out=sg_sb[:, 0:SC], in_=pg[:, 0:SC], func=AF.Silu),
                              reads=pgb, writes=("sgk",))
                        yield

                        def ufill(h, bufh=bufh, hc2=hc2, xgT=xgT):
                            last = None
                            for dc in range(8):
                                last = h.matmul(
                                    pu[:, 0:SC],
                                    bufh[:, 2048 + dc * 256 + hc2 * 128: 2048 + dc * 256 + hc2 * 128 + 128],
                                    xgT[:, dc * SC: dc * SC + SC],
                                    start=(dc == 0), stop=(dc == 7))
                            return last
                        P.grp("tensor", ufill, reads=(bkey, f"xgk{gp}"), writes=pub)
                        P.grp("vector",
                              lambda h, hc=hc: h.tensor_mul(
                                  h_sb[:, hc * SC:(hc + 1) * SC], sg_sb[:, 0:SC], pu[:, 0:SC]),
                              reads=("sgk",) + pub, writes=(f"hsb{hc}",))
                        yield
                    for blk in range(SCB):
                        bg = scj * SCB + blk
                        stok_ap = idxs_i[:, 16 * NB + e * NB + bg: 16 * NB + e * NB + bg + 1]
                        wcol = wext[:, e * NB + bg: e * NB + bg + 1]
                        for n in range(2):
                            def dbody(h, blk=blk, n=n, bufs=tuple(bufs)):
                                last = None
                                for hc in range(4):
                                    last = h.matmul(
                                        psY[:],
                                        h_sb[:, hc * SC + blk * 128: hc * SC + blk * 128 + 128],
                                        bufs[hc // 2][:, 4096 + (hc % 2) * 1024 + n * 512:
                                                       4096 + (hc % 2) * 1024 + n * 512 + 512],
                                        start=(hc == 0), stop=(hc == 3))
                                return last
                            P.grp("tensor", dbody,
                                  reads=("hsb0", "hsb1", "hsb2", "hsb3") + tuple(bkeys),
                                  writes=psYb)
                            P.grp("vector",
                                  lambda h, wcol=wcol: h.tensor_scalar(
                                      out=ysc[:], in0=psY[:], scalar1=wcol,
                                      scalar2=None, op0=ALU.mult),
                                  reads=psYb + ("wext",), writes=("ysck",))
                            P.grp("gpsimd",
                                  lambda h, n=n, stok_ap=stok_ap: h.indirect_dma_start(
                                      out=y2_d[:],
                                      out_offset=IndirectOffsetOnAxis(ap=stok_ap, axis=0),
                                      in_=ysc[:], in_offset=None,
                                      element_offset=n * 512),
                                  reads=("ysck", "idxs"), writes=(), dsem="d_y2")
                            yield
            for sl in range(NSH):
                bufs = []
                bkeys = []
                for half in range(2):
                    mpar = uctr % 2
                    uctr += 1
                    P.grp("gpsimd",
                          lambda h, u=2 * NE + 2 * sl + half, mpar=mpar: h.dma_start(
                              out=mbuf[mpar][:], in_=wunits_d[u, :, :]),
                          writes=(f"mbuf{mpar}",), dsem=f"d_mb{mpar}")
                    bufs.append(mbuf[mpar])
                    bkeys.append(f"mbuf{mpar}")
                for c in range(NCH):
                    ck = xn_keys(c * 512, c * 512 + 512)
                    for hc in range(4):
                        hf, hc2 = hc // 2, hc % 2
                        bufh = bufs[hf]
                        bkey = bkeys[hf]

                        def gfill(h, bufh=bufh, hc2=hc2, c=c):
                            last = None
                            for dc in range(8):
                                last = h.matmul(
                                    pg[:], bufh[:, dc * 256 + hc2 * 128: dc * 256 + hc2 * 128 + 128],
                                    xnT[:, dc * T + c * 512: dc * T + c * 512 + 512],
                                    start=(dc == 0), stop=(dc == 7))
                            return last
                        P.grp("tensor", gfill, reads=(bkey,) + ck, writes=pgb)
                        P.grp("scalar",
                              lambda h: h.activation(out=sg_sb[:], in_=pg[:], func=AF.Silu),
                              reads=pgb, writes=("sgk",))
                        yield

                        def ufill(h, bufh=bufh, hc2=hc2, c=c):
                            last = None
                            for dc in range(8):
                                last = h.matmul(
                                    pu[:], bufh[:, 2048 + dc * 256 + hc2 * 128: 2048 + dc * 256 + hc2 * 128 + 128],
                                    xnT[:, dc * T + c * 512: dc * T + c * 512 + 512],
                                    start=(dc == 0), stop=(dc == 7))
                            return last
                        P.grp("tensor", ufill, reads=(bkey,) + ck, writes=pub)
                        P.grp("vector",
                              lambda h, hc=hc: h.tensor_mul(
                                  h_sb[:, hc * 512:(hc + 1) * 512], sg_sb[:], pu[:]),
                              reads=("sgk",) + pub, writes=(f"hsb{hc}",))
                        yield
                    for st4 in range(4):
                        tt = c * 4 + st4
                        col = sig_all[:, tt:tt + 1]
                        ckey = f"sig{tt}"
                        for n in range(2):
                            def dbody(h, st4=st4, n=n, bufs=tuple(bufs)):
                                last = None
                                for hc in range(4):
                                    last = h.matmul(
                                        psY[:],
                                        h_sb[:, hc * 512 + st4 * 128: hc * 512 + st4 * 128 + 128],
                                        bufs[hc // 2][:, 4096 + (hc % 2) * 1024 + n * 512:
                                                       4096 + (hc % 2) * 1024 + n * 512 + 512],
                                        start=(hc == 0), stop=(hc == 3))
                                return last
                            P.grp("tensor", dbody,
                                  reads=("hsb0", "hsb1", "hsb2", "hsb3") + tuple(bkeys),
                                  writes=psYb)
                            P.grp("vector",
                                  lambda h, tt=tt, n=n, col=col: h.scalar_tensor_tensor(
                                      out=acc[:, tt * 1024 + n * 512: tt * 1024 + n * 512 + 512],
                                      in0=psY[:], scalar=col,
                                      in1=acc[:, tt * 1024 + n * 512: tt * 1024 + n * 512 + 512],
                                      op0=ALU.mult, op1=ALU.add),
                                  reads=psYb + (ckey, f"acc{tt}"), writes=(f"acc{tt}",))
                            yield

        psG = pg

        nA = NHP * (2 * nQC + nT + nQC * (nT + 1 + 8))
        nB = nT + NE * (CAP // SC) * (2 * SCB + 8 + 2 * SCB * 2) + NSH * NCH * 16
        ratio = nB / nA
        ga, gb = genA(), genB()
        a_alive = b_alive = True
        mcarry = 0.0
        while a_alive or b_alive:
            if a_alive:
                try:
                    next(ga)
                except StopIteration:
                    a_alive = False
            mcarry += ratio if a_alive else 1.0
            while mcarry >= 1.0 and b_alive:
                mcarry -= 1.0
                try:
                    next(gb)
                except StopIteration:
                    b_alive = False
            if not a_alive and not b_alive:
                break

        P.barrier_wait("sync", P.barrier_vals())
        P._wait("sync", "d_y2", P.count.get("d_y2", 0))
        if T >= 4096:
            ybuf = [(KT[:, 2048 * i:2048 * i + 1024], KT[:, 2048 * i + 1024:2048 * i + 2048])
                    for i in range(2)]
            ykeys = [("KT", "KT"), ("KT", "KT")]
        else:
            ybt = [ctx(nc.sbuf_tensor(f"ybt{i}", [128, 1024], BF16)) for i in range(4)]
            ybuf = [(ybt[0][:], ybt[1][:]), (ybt[2][:], ybt[3][:])]
        for tt in range(nT):
            fp = tt % 2
            ya, yb = ybuf[fp]
            P.grp("sync",
                  lambda h, tt=tt, fp=fp: h.dma_start(
                      out=xbuf[fp][:], in_=x_d[tt * 128:(tt + 1) * 128, :]),
                  writes=(f"xb{fp}",), dsem=f"d_xb{fp}")
            P.grp("sync",
                  lambda h, tt=tt, ya=ya: h.dma_start(
                      out=ya, in_=y2_d[tt * 128:(tt + 1) * 128, :]),
                  writes=(f"ya{fp}",), dsem=f"d_ya{fp}")
            P.grp("sync",
                  lambda h, tt=tt, yb=yb: h.dma_start(
                      out=yb, in_=y2_d[T + tt * 128: T + (tt + 1) * 128, :]),
                  writes=(f"yb{fp}",), dsem=f"d_yb{fp}")
            P.grp("vector",
                  lambda h, tt=tt, fp=fp: h.tensor_add(
                      xbuf[fp][:], acc[:, tt * 1024:(tt + 1) * 1024], xbuf[fp][:]),
                  reads=(f"xb{fp}", f"acc{tt}"), writes=(f"xb{fp}",))
            P.grp("vector",
                  lambda h, fp=fp, ya=ya: h.tensor_add(xbuf[fp][:], ya, xbuf[fp][:]),
                  reads=(f"xb{fp}", f"ya{fp}"), writes=(f"xb{fp}",))
            P.grp("vector",
                  lambda h, fp=fp, yb=yb: h.tensor_add(xbuf[fp][:], yb, xbuf[fp][:]),
                  reads=(f"xb{fp}", f"yb{fp}"), writes=(f"xb{fp}",))
            P.grp("sync",
                  lambda h, tt=tt, fp=fp: h.dma_start(
                      out=out_d[tt * 128:(tt + 1) * 128, :], in_=xbuf[fp][:]),
                  reads=(f"xb{fp}",), writes=(), dsem=f"d_out{fp}")
        P._wait("sync", "d_out0", P.count["d_out0"])
        P._wait("sync", "d_out1", P.count["d_out1"])
        psM.close()

        sems = ExitStack()
        P.sems = {k: sems.enter_context(nc.semaphore(f"sem_{k}"))
                  for k in sorted(P.count)}
        with sems:
            with nc.Block() as block:
                @block.sync
                def _(h):
                    for f in P.ops["sync"]:
                        f(h)

                @block.gpsimd
                def _(h):
                    for f in P.ops["gpsimd"]:
                        f(h)

                @block.tensor
                def _(h):
                    for f in P.ops["tensor"]:
                        f(h)

                @block.vector
                def _(h):
                    for f in P.ops["vector"]:
                        f(h)

                @block.scalar
                def _(h):
                    for f in P.ops["scalar"]:
                        f(h)
    return nc


def prepare_host(inputs, T=None):
    f = np.float32
    Wq = np.ascontiguousarray(inputs["L0_attn_Wq"], f)
    Wk = np.ascontiguousarray(inputs["L0_attn_Wk"], f)
    Wv = np.ascontiguousarray(inputs["L0_attn_Wv"], f)
    Wo = np.ascontiguousarray(inputs["L0_attn_Wo"], f)
    bq = np.ascontiguousarray(inputs["L0_attn_bq"], f)
    bk = np.ascontiguousarray(inputs["L0_attn_bk"], f)
    bv = np.ascontiguousarray(inputs["L0_attn_bv"], f)
    bo = np.ascontiguousarray(inputs["L0_attn_bo"], f)
    Wg = np.ascontiguousarray(inputs["L0_ffn_Wg"], f)
    Weg = np.ascontiguousarray(inputs["L0_ffn_We_gate"], f)
    Weu = np.ascontiguousarray(inputs["L0_ffn_We_up"], f)
    Wed = np.ascontiguousarray(inputs["L0_ffn_We_down"], f)
    Wsg_ = np.ascontiguousarray(inputs["L0_ffn_Ws_gate"], f)
    Wsu = np.ascontiguousarray(inputs["L0_ffn_Ws_up"], f)
    Wsd = np.ascontiguousarray(inputs["L0_ffn_Ws_down"], f)
    Wsig = np.ascontiguousarray(inputs["L0_ffn_Wsg"], f)

    qkvwo = np.zeros((NHP, 128, 5376), f)
    for hp in range(NHP):
        cs = slice(hp * 128, hp * 128 + 128)
        for j, W in enumerate((Wq, Wk, Wv)):
            blk = W[cs, :].reshape(128, 8, 128)
            qkvwo[hp][:, j * 1024:(j + 1) * 1024] = (
                blk.transpose(2, 1, 0).reshape(128, 1024))
        qkvwo[hp][:, 3072:4096] = Wo[:, cs].T
        qkvwo[hp][0, 4096:4224] = bv[cs]
        qkvwo[hp][0, 4224:5248] = bo
        qkvwo[hp][0, 5248:5376] = 1.0

    bqkvT = np.empty((128, 16), f)
    for hp in range(NHP):
        bqkvT[:, hp * 2 + 0] = bq[hp * 128:(hp + 1) * 128]
        bqkvT[:, hp * 2 + 1] = bk[hp * 128:(hp + 1) * 128]

    def img_gud(gate, up, down):
        out = np.empty((128, 6144), f)
        out[:, 0:2048] = gate.reshape(256, 8, 128).transpose(2, 1, 0).reshape(128, 2048)
        out[:, 2048:4096] = up.reshape(256, 8, 128).transpose(2, 1, 0).reshape(128, 2048)
        out[:, 4096:6144] = down.reshape(1024, 2, 128).transpose(2, 1, 0).reshape(128, 2048)
        return out

    wunits = np.empty((NU, 128, 6144), f)
    for e in range(NE):
        for half in range(2):
            hs = slice(half * 256, half * 256 + 256)
            wunits[2 * e + half] = img_gud(Weg[e][hs, :], Weu[e][hs, :], Wed[e][:, hs])
    for s in range(NSH):
        for half in range(2):
            o = s * 512 + half * 256
            hs = slice(o, o + 256)
            wunits[2 * NE + 2 * s + half] = img_gud(Wsg_[hs, :], Wsu[hs, :], Wsd[:, hs])

    WgA = np.concatenate([Wg, Wsig], axis=0)
    wgimg = np.ascontiguousarray(
        WgA.reshape(17, 8, 128).transpose(2, 1, 0).reshape(128, 136))

    if T is None:
        T = int(np.asarray(inputs["L0_attn_x"]).shape[1])
    CAP = T // 4
    NB = CAP // 128
    rout = np.zeros((128, 280), f)
    rout[:, 0:128] = np.tril(np.ones((128, 128), f), -1).T
    rout[:, 128] = 1.0
    rout[:, 129] = np.arange(128, dtype=f)
    rout[:, 130:146] = (np.arange(16, dtype=f) * NB)[None, :]
    rout[0, 148:276] = 1.0
    fill = np.zeros((128, 64 * NB), f)
    fill[:, 1::4] = float(2 * T)
    return {
        "qkvwo": qkvwo,
        "wunits": wunits,
        "bqkvT_d": bqkvT,
        "wgimg": wgimg,
        "ident": np.eye(128, dtype=f),
        "routimg": rout,
        "fillimg": fill,
    }


def kernel(**inputs):
    x = np.ascontiguousarray(np.asarray(inputs["L0_attn_x"], np.float32))
    B, T, d = x.shape
    assert d == D
    common = prepare_host(inputs)
    nc = build(T)
    in_maps = []
    for b in range(B):
        m = dict(common)
        m["x"] = np.ascontiguousarray(x[b])
        in_maps.append(m)
    res = run_bass_kernel_spmd(nc, in_maps, core_ids=list(range(B)))
    return np.stack([r["out"] for r in res.results], axis=0)


# version 10
# speedup vs baseline: 1.1409x; 1.0211x over previous
import numpy as np
import ml_dtypes
import concourse.bass as bass
import concourse.mybir as mybir
from concourse.bass_utils import run_bass_kernel_spmd
from contextlib import ExitStack
from concourse.bass_core_rust import IndirectOffsetOnAxis

F32 = mybir.dt.float32
BF16 = mybir.dt.bfloat16
I32 = mybir.dt.int32
AF = mybir.ActivationFunctionType
ALU = mybir.AluOpType
AX = mybir.AxisListType

D = 1024
NHP = 8
NE = 16
NSH = 4
NU = 2 * NE + 2 * NSH
EPS = 1e-5
SIN_SCALE = 30.0 / 8.0
BIG = 1e30


class Prog:
    ENGINES = ("sync", "gpsimd", "tensor", "vector", "scalar")
    ENG_SEM = {"tensor": "pe", "vector": "dve", "scalar": "act",
               "sync": "sq", "gpsimd": "gq"}

    def __init__(self):
        self.ops = {e: [] for e in self.ENGINES}
        self.count = {"pe": 0, "dve": 0, "act": 0}
        self.lastwait = {e: {} for e in self.ENGINES}
        self.regions = {}
        self.sems = None

    def _region(self, key):
        r = self.regions.get(key)
        if r is None:
            r = {"w": None, "r": {}}
            self.regions[key] = r
        return r

    def _wait(self, eng, sem, val):
        if val is None or val <= 0:
            return
        if eng == "tensor" and sem == "pe":
            return
        lw = self.lastwait[eng]
        if lw.get(sem, 0) >= val:
            return
        lw[sem] = val

        def f(h, self=self, sem=sem, val=val):
            h.wait_ge(self.sems[sem], val)
        self.ops[eng].append(f)

    def grp(self, eng, body, reads=(), writes=(), dsem=None):
        if eng in ("sync", "gpsimd"):
            assert dsem is not None
        edges = {}
        for key in reads:
            r = self._region(key)
            if r["w"] is not None:
                s, v = r["w"]
                edges[s] = max(edges.get(s, 0), v)
        for key in writes:
            r = self._region(key)
            if r["w"] is not None:
                s, v = r["w"]
                edges[s] = max(edges.get(s, 0), v)
            for s, v in r["r"].items():
                edges[s] = max(edges.get(s, 0), v)
        for s in sorted(edges):
            self._wait(eng, s, edges[s])
        sem = dsem if dsem is not None else self.ENG_SEM[eng]
        amt = 16 if dsem is not None else 1
        self.count.setdefault(sem, 0)
        self.count[sem] += amt
        val = self.count[sem]

        def f(h, self=self, body=body, sem=sem, amt=amt):
            body(h).then_inc(self.sems[sem], amt)
        self.ops[eng].append(f)
        for key in reads:
            r = self._region(key)
            r["r"][sem] = max(r["r"].get(sem, 0), val)
        for key in writes:
            self.regions[key] = {"w": (sem, val), "r": {}}
        return val

    def barrier_vals(self):
        return dict(self.count)

    def barrier_wait(self, eng, vals, sems=("pe", "dve", "act")):
        for s in sems:
            self._wait(eng, s, vals.get(s, 0))


def _pbanks(nc, handle, shape, dtype):
    ml = nc.lookup_mloc(handle)
    per_part = int(np.prod(shape[1:])) * mybir.dt.size(dtype)
    nb = (per_part + 2047) // 2048
    return tuple(("ps", ml.bank + i) for i in range(max(nb, 1)))


def build(T, debug=False):
    assert T % 512 == 0
    nT = T // 128
    nQC = T // 512
    NCH = T // 512
    nc = bass.Bass("TRN2", target_bir_lowering=False, dynamic_dma_scratch_size=16384)

    x_d = nc.dram_tensor("x", [T, D], F32, kind="ExternalInput").ap()
    out_d = nc.dram_tensor("out", [T, D], F32, kind="ExternalOutput").ap()
    qkvwo_d = nc.dram_tensor("qkvwo", [NHP, 128, 5376], F32, kind="ExternalInput").ap()
    wunits_d = nc.dram_tensor("wunits", [NU, 128, 6144], F32, kind="ExternalInput").ap()
    bqkv_d = nc.dram_tensor("bqkvT_d", [128, 16], F32, kind="ExternalInput").ap()
    CAP = T // 4
    NB = CAP // 128
    SC = min(128, CAP)
    SCB = SC // 128
    rout_d = nc.dram_tensor("routimg", [128, 280], F32, kind="ExternalInput").ap()
    fill_d = nc.dram_tensor("fillimg", [128, 64 * NB], F32, kind="ExternalInput").ap()
    glw_d = nc.dram_tensor("glw", [16 * CAP + 8, 4], F32, kind="Internal").ap()
    y2_d = nc.dram_tensor("y2", [2 * T + 128, 1024], BF16, kind="Internal").ap()
    xn_d = nc.dram_tensor("xnhbm", [T, 1024], BF16, kind="Internal").ap()
    wg_d = nc.dram_tensor("wgimg", [128, 136], F32, kind="ExternalInput").ap()
    id_d = nc.dram_tensor("ident", [128, 128], F32, kind="ExternalInput").ap()

    P = Prog()

    def tap(name, ap, reads, bf=True):
        if not debug or (isinstance(debug, (list, tuple, set)) and name not in debug):
            return
        d = nc.dram_tensor(f"dbg_{name}", list(ap.shape), F32, kind="ExternalOutput").ap()
        eng = "gpsimd" if bf else "sync"
        P.grp(eng, lambda h: h.dma_start(out=d[:], in_=ap), reads=tuple(reads), writes=(),
              dsem="d_tapg" if bf else "d_tap")

    def xn_keys(t0, t1):
        return tuple(f"xnT{i}" for i in range(t0 // 128, (t1 + 127) // 128))

    es = ExitStack()
    ctx = es.enter_context
    with es:
        acc = ctx(nc.sbuf_tensor("acc", [128, nT * 1024], BF16))
        xnT = ctx(nc.sbuf_tensor("xnT", [128, 8 * T], BF16))
        wg_bf = ctx(nc.sbuf_tensor("wg_bf", [128, 136], BF16))
        bqkvT = ctx(nc.sbuf_tensor("bqkvT", [128, 16], F32))
        stats = ctx(nc.sbuf_tensor("stats", [128, 16], F32))
        sig_all = ctx(nc.sbuf_tensor("sig_all", [128, T // 128], F32))
        idxs_i = ctx(nc.sbuf_tensor("idxs_i", [128, 32 * NB], I32))
        wext = ctx(nc.sbuf_tensor("wext", [128, 16 * NB], F32))
        xg_tok = [ctx(nc.sbuf_tensor(f"xg_tok{i}", [128, SCB * 1024], BF16)) for i in range(3)]
        assert SCB == 1
        ysc = ctx(nc.sbuf_tensor("ysc0", [128, 512], BF16))
        qkvb = ctx(nc.sbuf_tensor("qkvb", [128, 5376], BF16))
        if T >= 4096:
            sxb = ExitStack()
            xbuf = [sxb.enter_context(nc.sbuf_tensor(f"xbuf{i}", [128, 1024], F32)) for i in range(2)]
            sxb.close()
            QT = ctx(nc.sbuf_tensor("QT", [128, T], BF16))
            assert nc.lookup_mloc(QT).addr == nc.lookup_mloc(xbuf[0]).addr
        else:
            xbuf = [ctx(nc.sbuf_tensor(f"xbuf{i}", [128, 1024], F32)) for i in range(2)]
            QT = ctx(nc.sbuf_tensor("QT", [128, T], BF16))
        KT = ctx(nc.sbuf_tensor("KT", [128, T], BF16))
        V_sb = ctx(nc.sbuf_tensor("V_sb", [128, T], BF16))
        st_sb = ctx(nc.sbuf_tensor("stsb", [128, 1024], BF16))
        ctx_sb = ctx(nc.sbuf_tensor("ctxsb", [128, 512], BF16))
        id_bf = ctx(nc.sbuf_tensor("id_perm", [128, 128], BF16))
        sg_sb = ctx(nc.sbuf_tensor("sg_sb", [128, 512], BF16))
        srt = ExitStack()
        gat = srt.enter_context(nc.sbuf_tensor("gat", [128, 256], F32))
        rout_sb = srt.enter_context(nc.sbuf_tensor("rout_sb", [128, 280], F32))
        carry = srt.enter_context(nc.sbuf_tensor("carry", [1, 16], F32))
        glw_sb = srt.enter_context(nc.sbuf_tensor("glw_sb", [128, 64 * NB], F32))
        quad = srt.enter_context(nc.sbuf_tensor("quad", [128, 8], F32))
        idx_i = srt.enter_context(nc.sbuf_tensor("idx_i", [128, 2], I32))
        pos_i = srt.enter_context(nc.sbuf_tensor("pos_i", [128, 32], I32))
        hole_start = nc.lookup_mloc(gat).addr
        hole_end = nc.lookup_mloc(pos_i).addr + 128
        srt.close()
        h_sb = ctx(nc.sbuf_tensor("h_sb", [128, 2048], BF16))
        assert nc.lookup_mloc(h_sb).addr == hole_start
        spacer = hole_end - (hole_start + 4096)
        if spacer > 0:
            ctx(nc.sbuf_tensor("h_spacer", [128, spacer], mybir.dt.uint8))
        mbuf = [ctx(nc.sbuf_tensor(f"mbuf{i}", [128, 6144], BF16)) for i in range(2)]
        assert nc.lookup_mloc(mbuf[0]).addr >= hole_end

        if T >= 2048:
            sq_sb = KT[:, 1024:2048]
            xnbf = KT[:, 0:1024]
            sq_key, xnbf_key = "KT", "KT"
        else:
            sq_t = ctx(nc.sbuf_tensor("sq_t", [128, 1024], F32))
            xnbf_t = ctx(nc.sbuf_tensor("xnbf_t", [128, 1024], BF16))
            sq_sb, xnbf = sq_t[:], xnbf_t[:]
            sq_key, xnbf_key = "sqt", "xnbft"
        ps0s = ExitStack()
        psTR = [ps0s.enter_context(nc.psum_tensor(f"psTR{i}", [128, 128], BF16)) for i in range(2)]
        trb = [_pbanks(nc, psTR[i], [128, 128], BF16) for i in range(2)]

        P.grp("sync", lambda h: h.dma_start(out=bqkvT[:], in_=bqkv_d[:]),
              writes=("bqkv",), dsem="d_cst")
        P.grp("gpsimd", lambda h: h.dma_start(out=id_bf[:], in_=id_d[:]),
              writes=("idp",), dsem="d_cstg")
        P.grp("gpsimd", lambda h: h.dma_start(out=wg_bf[:], in_=wg_d[:]),
              writes=("wgbf",), dsem="d_cstg")
        P.grp("sync", lambda h: h.dma_start(out=rout_sb[:], in_=rout_d[:]),
              writes=("rout",), dsem="d_cst")
        P.grp("vector", lambda h: h.memset(carry[:], 0.0), writes=("carry",))
        P.grp("vector", lambda h: h.memset(quad[:], 0.0), writes=("quad",))
        P.grp("sync", lambda h: h.dma_start(out=glw_sb[:], in_=fill_d[:]),
              writes=("glwsb",), dsem="d_cst")
        P.grp("sync",
              lambda h: h.dma_start(
                  out=glw_d[0:16 * CAP, :].rearrange("(p r) c -> p (r c)", p=128),
                  in_=glw_sb[:]),
              reads=("glwsb",), writes=("glwd",), dsem="d_cst")
        for _k in ("bqkv", "rout", "glwsb", "glwd"):
            P.regions[_k]["w"] = ("d_cst", P.count["d_cst"])
        for _k in ("idp", "wgbf"):
            P.regions[_k]["w"] = ("d_cstg", P.count["d_cstg"])
        acc_keys = tuple(f"acc{tt}" for tt in range(nT))
        P.grp("vector", lambda h: h.memset(acc[:], 0.0), writes=acc_keys)

        inv_d = 1.0 / D
        for i in range(nT):
            par = i % 2
            xt = xbuf[par]
            st = par * 8
            P.grp("sync",
                  lambda h, o=xt, i=i: h.dma_start(out=o[:], in_=x_d[i * 128:(i + 1) * 128, :]),
                  writes=(f"xb{par}",), dsem=f"d_xb{par}")
            P.grp("scalar",
                  lambda h, xt=xt, st=st: h.activation(
                      out=sq_sb, in_=xt[:], func=AF.Square,
                      accum_out=stats[:, st + 1:st + 2]),
                  reads=(f"xb{par}",), writes=(sq_key, f"s2_{par}"))
            P.grp("vector",
                  lambda h, xt=xt, st=st: h.reduce_sum(stats[:, st:st + 1], xt[:], axis=AX.X),
                  reads=(f"xb{par}",), writes=(f"s1_{par}",))
            P.grp("vector",
                  lambda h, st=st: h.tensor_scalar(
                      out=stats[:, st + 2:st + 3], in0=stats[:, st:st + 1],
                      scalar1=inv_d, scalar2=None, op0=ALU.mult),
                  reads=(f"s1_{par}",), writes=(f"mean_{par}",))
            P.grp("vector",
                  lambda h, st=st: h.tensor_scalar(
                      out=stats[:, st + 3:st + 4], in0=stats[:, st + 1:st + 2],
                      scalar1=inv_d, scalar2=None, op0=ALU.mult),
                  reads=(f"s2_{par}",), writes=(f"e2_{par}",))
            P.grp("vector",
                  lambda h, st=st: h.tensor_mul(
                      stats[:, st + 4:st + 5], stats[:, st + 2:st + 3], stats[:, st + 2:st + 3]),
                  reads=(f"mean_{par}",), writes=(f"msq_{par}",))
            P.grp("vector",
                  lambda h, st=st: h.tensor_scalar(
                      out=stats[:, st + 5:st + 6], in0=stats[:, st + 3:st + 4],
                      scalar1=stats[:, st + 4:st + 5], scalar2=EPS,
                      op0=ALU.subtract, op1=ALU.add),
                  reads=(f"e2_{par}", f"msq_{par}"), writes=(f"var_{par}",))
            P.grp("scalar",
                  lambda h, st=st: h.activation(
                      out=stats[:, st + 6:st + 7], in_=stats[:, st + 5:st + 6],
                      func=AF.Sqrt),
                  reads=(f"var_{par}",), writes=(f"std_{par}",))
            P.grp("vector",
                  lambda h, st=st: h.reciprocal(stats[:, st + 7:st + 8], stats[:, st + 6:st + 7]),
                  reads=(f"std_{par}",), writes=(f"rstd_{par}",))
            P.grp("vector",
                  lambda h, xt=xt, st=st: h.tensor_scalar(
                      out=xnbf, in0=xt[:],
                      scalar1=stats[:, st + 2:st + 3], scalar2=stats[:, st + 7:st + 8],
                      op0=ALU.subtract, op1=ALU.mult),
                  reads=(f"xb{par}", f"mean_{par}", f"rstd_{par}"),
                  writes=(xnbf_key,))
            P.grp("sync",
                  lambda h, i=i: h.dma_start(
                      out=xn_d[i * 128:(i + 1) * 128, :], in_=xnbf),
                  reads=(xnbf_key,), writes=("xnhbm",), dsem="d_xn")
            for dc in range(8):
                tr = dc % 2
                P.grp("tensor",
                      lambda h, dc=dc, tr=tr: h.transpose(
                          psTR[tr][:], xnbf[:, dc * 128:(dc + 1) * 128], id_bf[:]),
                      reads=(xnbf_key, "idp"), writes=trb[tr])
                P.grp("vector",
                      lambda h, i=i, dc=dc, tr=tr: h.tensor_copy(
                          xnT[:, dc * T + i * 128: dc * T + (i + 1) * 128], psTR[tr][:]),
                      reads=trb[tr], writes=(f"xnT{i}",))
        all_xn = xn_keys(0, T)
        tap("xnT", xnT[:], all_xn)
        vals0 = P.barrier_vals()
        ps0s.close()

        psM = ExitStack()
        psL = psM.enter_context(nc.psum_tensor("psL", [128, 1024], F32))
        psCxA = psM.enter_context(nc.psum_tensor("psCxA", [128, 512], F32))
        psCxB = psM.enter_context(nc.psum_tensor("psCxB", [128, 512], F32))
        pQO = psM.enter_context(nc.psum_tensor("pQO", [128, 512], F32))
        pg = psM.enter_context(nc.psum_tensor("pg", [128, 512], F32))
        pu = psM.enter_context(nc.psum_tensor("pu", [128, 512], F32))
        spx = ExitStack()
        psTX = spx.enter_context(nc.psum_tensor("psTX", [128, 1024], BF16))
        spx.close()
        psY = psM.enter_context(nc.psum_tensor("psY", [128, 512], F32))
        assert nc.lookup_mloc(psTX).bank == nc.lookup_mloc(psY).bank
        psLb = _pbanks(nc, psL, [128, 1024], F32)
        psCxAb = _pbanks(nc, psCxA, [128, 512], F32)
        psCxBb = _pbanks(nc, psCxB, [128, 512], F32)
        pQOb = _pbanks(nc, pQO, [128, 512], F32)
        pgb = _pbanks(nc, pg, [128, 512], F32)
        pub = _pbanks(nc, pu, [128, 512], F32)
        psYb = _pbanks(nc, psY, [128, 512], F32)

        g_eg = gat[:, 0:16]
        g_msk = gat[:, 16:32]
        g_tmp = gat[:, 32:48]
        g_ind = gat[:, 48:64]
        g_wsel = gat[:, 64:80]
        g_nm = gat[:, 80:81]
        g_rmin = gat[:, 81:82]
        g_s2 = gat[:, 82:83]
        g_r2 = gat[:, 83:84]
        g_m1n = gat[:, 84:85]

        def genA():
            for hp in range(NHP):
                P.grp("gpsimd",
                      lambda h, hp=hp: h.dma_start(out=qkvb[:], in_=qkvwo_d[hp, :, :]),
                      writes=("qkvb",), dsem="d_qkvb")
                yield
                for j, dkey in ((0, "QT"), (1, "KT")):
                    dst = QT if j == 0 else KT
                    for tc in range(nQC):
                        def qkbody(h, j=j, tc=tc):
                            last = None
                            for dc in range(8):
                                last = h.matmul(
                                    pQO[:], qkvb[:, j * 1024 + dc * 128: j * 1024 + (dc + 1) * 128],
                                    xnT[:, dc * T + tc * 512: dc * T + tc * 512 + 512],
                                    start=(dc == 0), stop=(dc == 7))
                            return last
                        P.grp("tensor", qkbody,
                              reads=("qkvb",) + xn_keys(tc * 512, tc * 512 + 512), writes=pQOb)
                        P.grp("vector",
                              lambda h, dst=dst, tc=tc, hp=hp, j=j: h.tensor_scalar(
                                  out=dst[:, tc * 512: tc * 512 + 512], in0=pQO[:],
                                  scalar1=bqkvT[:, hp * 2 + j: hp * 2 + j + 1], scalar2=None,
                                  op0=ALU.add),
                              reads=pQOb + ("bqkv",), writes=(dkey,))
                        yield
                for tt in range(nT):
                    def vbody(h, tt=tt, hp=hp):
                        for dc in range(8):
                            h.matmul(pQO[:, 0:128],
                                     xnT[:, dc * T + tt * 128: dc * T + (tt + 1) * 128],
                                     qkvb[:, 2048 + dc * 128: 2048 + (dc + 1) * 128],
                                     start=(dc == 0), stop=False)
                        return h.matmul(pQO[:, 0:128], qkvb[0:1, 5248:5376],
                                        qkvb[0:1, 4096:4224],
                                        start=False, stop=True)
                    P.grp("tensor", vbody,
                          reads=("qkvb",) + xn_keys(tt * 128, tt * 128 + 128),
                          writes=pQOb)
                    P.grp("vector",
                          lambda h, tt=tt: h.tensor_copy(
                              V_sb[:, tt * 128:(tt + 1) * 128], pQO[:, 0:128]),
                          reads=pQOb, writes=("V",))
                    yield
                if hp == 0:
                    tap("QT", QT[:], ("QT",))
                    tap("KT", KT[:], ("KT",))
                    tap("V", V_sb[:], ("V",))
                for qc in range(nQC):
                    def emit_L(kt):
                        def lbody(h, qc=qc, kt=kt):
                            h.matmul(psL[:, 0:512], KT[0:64, kt * 128:(kt + 1) * 128],
                                     QT[0:64, qc * 512: qc * 512 + 512],
                                     start=True, stop=True, tile_position=(0, 0))
                            return h.matmul(psL[:, 512:1024], KT[64:128, kt * 128:(kt + 1) * 128],
                                            QT[64:128, qc * 512: qc * 512 + 512],
                                            start=True, stop=True, tile_position=(64, 0))
                        P.grp("tensor", lbody, reads=("QT", "KT"), writes=psLb)
                        P.grp("scalar",
                              lambda h: h.activation(
                                  out=st_sb[:], in_=psL[:], func=AF.Sin, scale=SIN_SCALE),
                              reads=psLb, writes=("stsb",))

                    def emit_SV(kt):
                        def svbody(h, kt=kt):
                            h.matmul(psCxA[0:64, :], V_sb[:, kt * 128: kt * 128 + 64],
                                     st_sb[:, 0:512],
                                     start=(kt == 0), stop=(kt == nT - 1), tile_position=(0, 0))
                            return h.matmul(psCxB[64:128, :], V_sb[:, kt * 128 + 64: kt * 128 + 128],
                                            st_sb[:, 512:1024],
                                            start=(kt == 0), stop=(kt == nT - 1), tile_position=(0, 64))
                        P.grp("tensor", svbody, reads=("stsb", "V"),
                              writes=psCxAb + psCxBb)

                    emit_L(0)
                    yield
                    for kt in range(1, nT):
                        emit_SV(kt - 1)
                        emit_L(kt)
                        yield
                    emit_SV(nT - 1)

                    def cxbody(h):
                        h.tensor_copy(ctx_sb[0:64, :], psCxA[0:64, :])
                        return h.tensor_copy(ctx_sb[64:128, :], psCxB[64:128, :])
                    P.grp("vector", cxbody, reads=psCxAb + psCxBb, writes=("ctx",))
                    yield
                    for st4 in range(4):
                        for n in range(2):
                            def obody(h, st4=st4, n=n, hp=hp):
                                last = h.matmul(pQO[:], ctx_sb[:, st4 * 128:(st4 + 1) * 128],
                                                qkvb[:, 3072 + n * 512: 3072 + n * 512 + 512],
                                                start=True, stop=(hp != 0))
                                if hp == 0:
                                    last = h.matmul(pQO[:], qkvb[0:1, 5248:5376],
                                                    qkvb[0:1, 4224 + n * 512: 4224 + n * 512 + 512],
                                                    start=False, stop=True)
                                return last
                            P.grp("tensor", obody,
                                  reads=("ctx", "qkvb"), writes=pQOb)
                            tt = qc * 4 + st4
                            P.grp("vector",
                                  lambda h, tt=tt, n=n: h.tensor_tensor(
                                      out=acc[:, tt * 1024 + n * 512: tt * 1024 + n * 512 + 512],
                                      in0=pQO[:],
                                      in1=acc[:, tt * 1024 + n * 512: tt * 1024 + n * 512 + 512],
                                      op=ALU.add),
                                  reads=pQOb + (f"acc{tt}",), writes=(f"acc{tt}",))
                            yield

        r_ind = gat[:, 96:112]
        r_pos = gat[:, 112:128]
        r_q = gat[:, 128:144]
        r_t1 = gat[:, 144:160]
        r_t2 = gat[:, 160:176]
        r_slot = gat[:, 176:192]
        r_w = gat[:, 192:208]
        r_eAv = gat[:, 208:209]
        r_sA = gat[:, 209:210]
        r_sB = gat[:, 210:211]
        r_wA = gat[:, 211:212]
        r_wB = gat[:, 212:213]
        r_tok = gat[:, 213:214]
        r_indA = gat[:, 224:240]
        r_indB = gat[:, 240:256]
        LTm = rout_sb[:, 0:128]
        ONEC = rout_sb[:, 128:129]
        PIDX = rout_sb[:, 129:130]
        EOFF = rout_sb[:, 130:146]
        BIGI = float(1 << 28)

        def genB():
            for tt in range(nT):
                def gbody(h, tt=tt):
                    last = None
                    for dc in range(8):
                        last = h.matmul(psG[:, 0:17],
                                        xnT[:, dc * T + tt * 128: dc * T + (tt + 1) * 128],
                                        wg_bf[:, dc * 17:(dc + 1) * 17],
                                        start=(dc == 0), stop=(dc == 7))
                    return last
                P.grp("tensor", gbody,
                      reads=("wgbf",) + xn_keys(tt * 128, tt * 128 + 128), writes=pgb)
                P.grp("vector",
                      lambda h: h.reduce_max(g_nm, psG[:, 0:16], axis=AX.X, negate=True),
                      reads=pgb, writes=("g_nm",))
                P.grp("scalar",
                      lambda h: h.activation(out=g_eg, in_=psG[:, 0:16], func=AF.Exp, bias=g_nm),
                      reads=pgb + ("g_nm",), writes=("g_eg",))
                P.grp("scalar",
                      lambda h, tt=tt: h.activation(
                          out=sig_all[:, tt:tt + 1], in_=psG[:, 16:17], func=AF.Sigmoid),
                      reads=pgb, writes=(f"sig{tt}",))
                P.grp("vector", lambda h: h.reduce_max(g_m1n, g_eg, axis=AX.X, negate=True),
                      reads=("g_eg",), writes=("g_m1n",))
                P.grp("vector",
                      lambda h: h.tensor_scalar(out=g_msk, in0=g_eg, scalar1=g_m1n,
                                                scalar2=0.0, op0=ALU.add, op1=ALU.is_ge),
                      reads=("g_eg", "g_m1n"), writes=("g_msk",))
                P.grp("vector",
                      lambda h: h.scalar_tensor_tensor(out=g_tmp, in0=g_msk, scalar=BIG,
                                                       in1=g_eg, op0=ALU.mult, op1=ALU.subtract),
                      reads=("g_msk", "g_eg"), writes=("g_tmp",))
                P.grp("vector",
                      lambda h: h.tensor_reduce(g_rmin, g_tmp, axis=AX.X, op=ALU.min),
                      reads=("g_tmp",), writes=("g_rmin",))
                P.grp("vector",
                      lambda h: h.tensor_scalar(out=g_ind, in0=g_eg, scalar1=g_rmin,
                                                scalar2=0.0, op0=ALU.add, op1=ALU.is_ge),
                      reads=("g_eg", "g_rmin"), writes=("g_ind",))
                P.grp("vector", lambda h: h.tensor_mul(g_wsel, g_eg, g_ind),
                      reads=("g_eg", "g_ind"), writes=("g_wsel",))
                P.grp("vector", lambda h: h.reduce_sum(g_s2, g_wsel, axis=AX.X),
                      reads=("g_wsel",), writes=("g_s2",))
                P.grp("vector", lambda h: h.reciprocal(g_r2, g_s2),
                      reads=("g_s2",), writes=("g_r2",))
                P.grp("vector",
                      lambda h: h.tensor_scalar(out=r_w, in0=g_wsel,
                                                scalar1=g_r2, scalar2=None, op0=ALU.mult),
                      reads=("g_wsel", "g_r2"), writes=("r_w",))
                yield
                P.grp("vector",
                      lambda h: h.tensor_scalar(out=r_ind, in0=r_w, scalar1=0.0,
                                                scalar2=None, op0=ALU.is_gt),
                      reads=("r_w",), writes=("r_ind",))

                def cums(h):
                    h.matmul(pu[:, 0:16], LTm, r_ind, start=True, stop=False)
                    h.matmul(pu[:, 0:16], rout_sb[0:1, 148:276], carry[:], start=False, stop=True)
                    return h.matmul(pu[0:1, 16:32], ONEC, r_ind, start=True, stop=True)
                P.grp("tensor", cums, reads=("rout", "carry", "r_ind"), writes=pub)
                P.grp("vector", lambda h: h.tensor_copy(r_pos, pu[:, 0:16]),
                      reads=pub, writes=("r_pos",))
                P.grp("vector", lambda h: h.tensor_add(carry[:], pu[0:1, 16:32], carry[:]),
                      reads=pub + ("carry",), writes=("carry",))
                yield
                P.grp("vector", lambda h: h.tensor_copy(pos_i[:, 0:16], r_pos),
                      reads=("r_pos",), writes=("posi",))
                P.grp("vector",
                      lambda h: h.tensor_scalar(out=pos_i[:, 16:32], in0=pos_i[:, 0:16],
                                                scalar1=127, scalar2=None,
                                                op0=ALU.bitwise_and),
                      reads=("posi",), writes=("posq",))
                P.grp("vector", lambda h: h.tensor_copy(r_q, pos_i[:, 16:32]),
                      reads=("posq",), writes=("r_q",))
                P.grp("vector",
                      lambda h: h.scalar_tensor_tensor(out=r_t1, in0=r_q, scalar=float(16 * NB),
                                                       in1=EOFF, op0=ALU.mult, op1=ALU.add),
                      reads=("r_q", "rout"), writes=("r_t1",))
                P.grp("vector", lambda h: h.tensor_sub(r_t2, r_pos, r_q),
                      reads=("r_pos", "r_q"), writes=("r_t2",))
                P.grp("vector",
                      lambda h: h.scalar_tensor_tensor(out=r_slot, in0=r_t2, scalar=1.0 / 128.0,
                                                       in1=r_t1, op0=ALU.mult, op1=ALU.add),
                      reads=("r_t2", "r_t1"), writes=("r_slot",))
                P.grp("vector",
                      lambda h: h.tensor_scalar(out=r_t1, in0=r_pos, scalar1=float(CAP),
                                                scalar2=BIGI, op0=ALU.is_ge, op1=ALU.mult),
                      reads=("r_pos",), writes=("r_t1",))
                P.grp("vector", lambda h: h.tensor_add(r_slot, r_slot, r_t1),
                      reads=("r_slot", "r_t1"), writes=("r_slot",))
                P.grp("vector",
                      lambda h: h.tensor_scalar(
                          out=r_slot, in0=r_slot, scalar1=float(16 * CAP),
                          scalar2=None, op0=ALU.min),
                      reads=("r_slot",), writes=("r_slot",))
                P.grp("vector",
                      lambda h: h.tensor_scalar(out=r_t2, in0=r_ind, scalar1=-BIGI,
                                                scalar2=BIGI, op0=ALU.mult, op1=ALU.add),
                      reads=("r_ind",), writes=("r_t2",))
                P.grp("vector",
                      lambda h: h.scalar_tensor_tensor(out=r_t1, in0=EOFF, scalar=1.0,
                                                       in1=r_t2, op0=ALU.mult, op1=ALU.add),
                      reads=("rout", "r_t2"), writes=("r_t1",))
                P.grp("vector", lambda h: h.tensor_reduce(r_eAv, r_t1, axis=AX.X, op=ALU.min),
                      reads=("r_t1",), writes=("r_eAv",))
                P.grp("vector",
                      lambda h: h.tensor_scalar(out=r_indA, in0=r_t1, scalar1=r_eAv,
                                                scalar2=None, op0=ALU.is_equal),
                      reads=("r_t1", "r_eAv"), writes=("r_indA",))
                P.grp("vector", lambda h: h.tensor_sub(r_indB, r_ind, r_indA),
                      reads=("r_ind", "r_indA"), writes=("r_indB",))
                for nm, indv, sdst, wdst in (("A", r_indA, r_sA, r_wA), ("B", r_indB, r_sB, r_wB)):
                    P.grp("vector",
                          lambda h, indv=indv: h.tensor_mul(r_t1, r_slot, indv),
                          reads=("r_slot", f"r_ind{nm}"), writes=("r_t1",))
                    P.grp("vector",
                          lambda h, sdst=sdst: h.reduce_sum(sdst, r_t1, axis=AX.X),
                          reads=("r_t1",), writes=(f"r_s{nm}",))
                    P.grp("vector",
                          lambda h, indv=indv: h.tensor_mul(r_t1, r_w, indv),
                          reads=("r_w", f"r_ind{nm}"), writes=("r_t1",))
                    P.grp("vector",
                          lambda h, wdst=wdst: h.reduce_sum(wdst, r_t1, axis=AX.X),
                          reads=("r_t1",), writes=(f"r_w{nm}",))
                yield
                P.grp("vector",
                      lambda h, tt=tt: h.tensor_scalar(out=r_tok, in0=PIDX,
                                                       scalar1=float(tt * 128),
                                                       scalar2=None, op0=ALU.add),
                      reads=("rout",), writes=("r_tok",))
                P.grp("vector", lambda h: h.tensor_copy(quad[:, 0:1], r_tok),
                      reads=("r_tok",), writes=("quad",))
                P.grp("vector", lambda h: h.tensor_copy(quad[:, 1:2], r_tok),
                      reads=("r_tok", "quad"), writes=("quad",))
                P.grp("vector", lambda h: h.tensor_copy(quad[:, 2:3], r_wA),
                      reads=("r_wA", "quad"), writes=("quad",))
                P.grp("vector", lambda h: h.tensor_copy(quad[:, 4:5], r_tok),
                      reads=("r_tok", "quad"), writes=("quad",))
                P.grp("vector",
                      lambda h: h.tensor_scalar(out=quad[:, 5:6], in0=r_tok,
                                                scalar1=float(T), scalar2=None, op0=ALU.add),
                      reads=("r_tok", "quad"), writes=("quad",))
                P.grp("vector", lambda h: h.tensor_copy(quad[:, 6:7], r_wB),
                      reads=("r_wB", "quad"), writes=("quad",))
                P.grp("vector", lambda h: h.tensor_copy(idx_i[:, 0:1], r_sA),
                      reads=("r_sA",), writes=("idxi",))
                P.grp("vector", lambda h: h.tensor_copy(idx_i[:, 1:2], r_sB),
                      reads=("r_sB", "idxi"), writes=("idxi",))
                P.grp("gpsimd",
                      lambda h: h.indirect_dma_start(
                          out=glw_d[:],
                          out_offset=IndirectOffsetOnAxis(ap=idx_i[:, 0:1], axis=0),
                          in_=quad[:, 0:4], in_offset=None),
                      reads=("quad", "idxi", "glwd"), writes=(), dsem="d_sc")
                P.grp("gpsimd",
                      lambda h: h.indirect_dma_start(
                          out=glw_d[:],
                          out_offset=IndirectOffsetOnAxis(ap=idx_i[:, 1:2], axis=0),
                          in_=quad[:, 4:8], in_offset=None),
                      reads=("quad", "idxi", "glwd"), writes=(), dsem="d_sc")
                yield
            P._wait("sync", "d_sc", P.count["d_sc"])
            P.grp("sync",
                  lambda h: h.dma_start(
                      out=glw_sb[:],
                      in_=glw_d[0:16 * CAP, :].rearrange("(p r) c -> p (r c)", p=128)),
                  reads=("glwd",), writes=("glwsb",), dsem="d_glwr")
            P.grp("vector",
                  lambda h: h.tensor_copy(
                      idxs_i[:, 0:16 * NB],
                      glw_sb[:].rearrange("p (r c) -> p r c", c=4)[:, :, 0]),
                  reads=("glwsb",), writes=("idxs",))
            P.grp("vector",
                  lambda h: h.tensor_copy(
                      idxs_i[:, 16 * NB:32 * NB],
                      glw_sb[:].rearrange("p (r c) -> p r c", c=4)[:, :, 1]),
                  reads=("glwsb", "idxs"), writes=("idxs",))
            P.grp("vector",
                  lambda h: h.tensor_copy(
                      wext[:], glw_sb[:].rearrange("p (r c) -> p r c", c=4)[:, :, 2]),
                  reads=("glwsb",), writes=("wext",))
            uctr = 0
            for e in range(NE):
                def emit_gather(scj, e=e):
                    gp = scj % 3
                    P.grp("gpsimd",
                          lambda h, e=e, scj=scj, gp=gp: h.indirect_dma_start(
                              out=xg_tok[gp][:, 0:1024],
                              out_offset=None,
                              in_=xn_d[:],
                              in_offset=IndirectOffsetOnAxis(
                                  ap=idxs_i[:, e * NB + scj: e * NB + scj + 1],
                                  axis=0)),
                          reads=("idxs", "xnhbm"), writes=(f"xgk{gp}",),
                          dsem=f"d_xg{gp}")
                emit_gather(0)
                emit_gather(1)
                bufs = []
                bkeys = []
                for half in range(2):
                    mpar = uctr % 2
                    uctr += 1
                    P.grp("gpsimd",
                          lambda h, u=2 * e + half, mpar=mpar: h.dma_start(
                              out=mbuf[mpar][:], in_=wunits_d[u, :, :]),
                          writes=(f"mbuf{mpar}",), dsem=f"d_mb{mpar}")
                    bufs.append(mbuf[mpar])
                    bkeys.append(f"mbuf{mpar}")
                for scj in range(CAP // SC):
                    gp = scj % 3
                    xgT = xg_tok[gp]
                    if scj + 2 < CAP // SC:
                        emit_gather(scj + 2)

                    def twav(h, xgT=xgT):
                        last = None
                        for k in range(8):
                            last = h.transpose(
                                psTX[:, k * 128:(k + 1) * 128],
                                xgT[:, k * 128: k * 128 + 128],
                                id_bf[:])
                        return last
                    P.grp("tensor", twav, reads=(f"xgk{gp}", "idp"), writes=psYb)

                    def tcop(h, xgT=xgT):
                        last = None
                        for k in range(8):
                            last = h.tensor_copy(
                                xgT[:, k * 128: k * 128 + 128],
                                psTX[:, k * 128:(k + 1) * 128])
                        return last
                    P.grp("vector", tcop, reads=psYb, writes=(f"xgk{gp}",))
                    yield
                    for hc in range(4):
                        hf, hc2 = hc // 2, hc % 2
                        bufh = bufs[hf]
                        bkey = bkeys[hf]

                        def gfill(h, bufh=bufh, hc2=hc2, xgT=xgT):
                            last = None
                            for dc in range(8):
                                last = h.matmul(
                                    pg[:, 0:SC],
                                    bufh[:, dc * 256 + hc2 * 128: dc * 256 + hc2 * 128 + 128],
                                    xgT[:, dc * SC: dc * SC + SC],
                                    start=(dc == 0), stop=(dc == 7))
                            return last
                        P.grp("tensor", gfill, reads=(bkey, f"xgk{gp}"), writes=pgb)
                        P.grp("scalar",
                              lambda h: h.activation(out=sg_sb[:, 0:SC], in_=pg[:, 0:SC], func=AF.Silu),
                              reads=pgb, writes=("sgk",))
                        yield

                        def ufill(h, bufh=bufh, hc2=hc2, xgT=xgT):
                            last = None
                            for dc in range(8):
                                last = h.matmul(
                                    pu[:, 0:SC],
                                    bufh[:, 2048 + dc * 256 + hc2 * 128: 2048 + dc * 256 + hc2 * 128 + 128],
                                    xgT[:, dc * SC: dc * SC + SC],
                                    start=(dc == 0), stop=(dc == 7))
                            return last
                        P.grp("tensor", ufill, reads=(bkey, f"xgk{gp}"), writes=pub)
                        P.grp("vector",
                              lambda h, hc=hc: h.tensor_mul(
                                  h_sb[:, hc * SC:(hc + 1) * SC], sg_sb[:, 0:SC], pu[:, 0:SC]),
                              reads=("sgk",) + pub, writes=(f"hsb{hc}",))
                        yield
                    for blk in range(SCB):
                        bg = scj * SCB + blk
                        stok_ap = idxs_i[:, 16 * NB + e * NB + bg: 16 * NB + e * NB + bg + 1]
                        wcol = wext[:, e * NB + bg: e * NB + bg + 1]
                        for n in range(2):
                            def dbody(h, blk=blk, n=n, bufs=tuple(bufs)):
                                last = None
                                for hc in range(4):
                                    last = h.matmul(
                                        psY[:],
                                        h_sb[:, hc * SC + blk * 128: hc * SC + blk * 128 + 128],
                                        bufs[hc // 2][:, 4096 + (hc % 2) * 1024 + n * 512:
                                                       4096 + (hc % 2) * 1024 + n * 512 + 512],
                                        start=(hc == 0), stop=(hc == 3))
                                return last
                            P.grp("tensor", dbody,
                                  reads=("hsb0", "hsb1", "hsb2", "hsb3") + tuple(bkeys),
                                  writes=psYb)
                            P.grp("vector",
                                  lambda h, wcol=wcol: h.tensor_scalar(
                                      out=ysc[:], in0=psY[:], scalar1=wcol,
                                      scalar2=None, op0=ALU.mult),
                                  reads=psYb + ("wext",), writes=("ysck",))
                            P.grp("gpsimd",
                                  lambda h, n=n, stok_ap=stok_ap: h.indirect_dma_start(
                                      out=y2_d[:],
                                      out_offset=IndirectOffsetOnAxis(ap=stok_ap, axis=0),
                                      in_=ysc[:], in_offset=None,
                                      element_offset=n * 512),
                                  reads=("ysck", "idxs"), writes=(), dsem="d_y2")
                            yield
            for sl in range(NSH):
                bufs = []
                bkeys = []
                for half in range(2):
                    mpar = uctr % 2
                    uctr += 1
                    P.grp("gpsimd",
                          lambda h, u=2 * NE + 2 * sl + half, mpar=mpar: h.dma_start(
                              out=mbuf[mpar][:], in_=wunits_d[u, :, :]),
                          writes=(f"mbuf{mpar}",), dsem=f"d_mb{mpar}")
                    bufs.append(mbuf[mpar])
                    bkeys.append(f"mbuf{mpar}")
                for c in range(NCH):
                    ck = xn_keys(c * 512, c * 512 + 512)
                    for hc in range(4):
                        hf, hc2 = hc // 2, hc % 2
                        bufh = bufs[hf]
                        bkey = bkeys[hf]

                        def gfill(h, bufh=bufh, hc2=hc2, c=c):
                            last = None
                            for dc in range(8):
                                last = h.matmul(
                                    pg[:], bufh[:, dc * 256 + hc2 * 128: dc * 256 + hc2 * 128 + 128],
                                    xnT[:, dc * T + c * 512: dc * T + c * 512 + 512],
                                    start=(dc == 0), stop=(dc == 7))
                            return last
                        P.grp("tensor", gfill, reads=(bkey,) + ck, writes=pgb)
                        P.grp("scalar",
                              lambda h: h.activation(out=sg_sb[:], in_=pg[:], func=AF.Silu),
                              reads=pgb, writes=("sgk",))
                        yield

                        def ufill(h, bufh=bufh, hc2=hc2, c=c):
                            last = None
                            for dc in range(8):
                                last = h.matmul(
                                    pu[:], bufh[:, 2048 + dc * 256 + hc2 * 128: 2048 + dc * 256 + hc2 * 128 + 128],
                                    xnT[:, dc * T + c * 512: dc * T + c * 512 + 512],
                                    start=(dc == 0), stop=(dc == 7))
                            return last
                        P.grp("tensor", ufill, reads=(bkey,) + ck, writes=pub)
                        P.grp("vector",
                              lambda h, hc=hc: h.tensor_mul(
                                  h_sb[:, hc * 512:(hc + 1) * 512], sg_sb[:], pu[:]),
                              reads=("sgk",) + pub, writes=(f"hsb{hc}",))
                        yield
                    for st4 in range(4):
                        tt = c * 4 + st4
                        col = sig_all[:, tt:tt + 1]
                        ckey = f"sig{tt}"
                        for n in range(2):
                            def dbody(h, st4=st4, n=n, bufs=tuple(bufs)):
                                last = None
                                for hc in range(4):
                                    last = h.matmul(
                                        psY[:],
                                        h_sb[:, hc * 512 + st4 * 128: hc * 512 + st4 * 128 + 128],
                                        bufs[hc // 2][:, 4096 + (hc % 2) * 1024 + n * 512:
                                                       4096 + (hc % 2) * 1024 + n * 512 + 512],
                                        start=(hc == 0), stop=(hc == 3))
                                return last
                            P.grp("tensor", dbody,
                                  reads=("hsb0", "hsb1", "hsb2", "hsb3") + tuple(bkeys),
                                  writes=psYb)
                            P.grp("vector",
                                  lambda h, tt=tt, n=n, col=col: h.scalar_tensor_tensor(
                                      out=acc[:, tt * 1024 + n * 512: tt * 1024 + n * 512 + 512],
                                      in0=psY[:], scalar=col,
                                      in1=acc[:, tt * 1024 + n * 512: tt * 1024 + n * 512 + 512],
                                      op0=ALU.mult, op1=ALU.add),
                                  reads=psYb + (ckey, f"acc{tt}"), writes=(f"acc{tt}",))
                            yield

        psG = pg

        nA = NHP * (2 * nQC + nT + nQC * (nT + 1 + 8))
        nB = nT + NE * (CAP // SC) * (2 * SCB + 8 + 2 * SCB * 2) + NSH * NCH * 16
        ratio = nB / nA
        ga, gb = genA(), genB()
        a_alive = b_alive = True
        mcarry = 0.0
        while a_alive or b_alive:
            if a_alive:
                try:
                    next(ga)
                except StopIteration:
                    a_alive = False
            mcarry += ratio if a_alive else 1.0
            while mcarry >= 1.0 and b_alive:
                mcarry -= 1.0
                try:
                    next(gb)
                except StopIteration:
                    b_alive = False
            if not a_alive and not b_alive:
                break

        P.barrier_wait("sync", P.barrier_vals())
        P._wait("sync", "d_y2", P.count.get("d_y2", 0))
        if T >= 4096:
            ybuf = [(KT[:, 2048 * i:2048 * i + 1024], KT[:, 2048 * i + 1024:2048 * i + 2048])
                    for i in range(2)]
            ykeys = [("KT", "KT"), ("KT", "KT")]
        else:
            ybt = [ctx(nc.sbuf_tensor(f"ybt{i}", [128, 1024], BF16)) for i in range(4)]
            ybuf = [(ybt[0][:], ybt[1][:]), (ybt[2][:], ybt[3][:])]
        for tt in range(nT):
            fp = tt % 2
            ya, yb = ybuf[fp]
            P.grp("sync",
                  lambda h, tt=tt, fp=fp: h.dma_start(
                      out=xbuf[fp][:], in_=x_d[tt * 128:(tt + 1) * 128, :]),
                  writes=(f"xb{fp}",), dsem=f"d_xb{fp}")
            P.grp("sync",
                  lambda h, tt=tt, ya=ya: h.dma_start(
                      out=ya, in_=y2_d[tt * 128:(tt + 1) * 128, :]),
                  writes=(f"ya{fp}",), dsem=f"d_ya{fp}")
            P.grp("sync",
                  lambda h, tt=tt, yb=yb: h.dma_start(
                      out=yb, in_=y2_d[T + tt * 128: T + (tt + 1) * 128, :]),
                  writes=(f"yb{fp}",), dsem=f"d_yb{fp}")
            P.grp("vector",
                  lambda h, tt=tt, fp=fp: h.tensor_add(
                      xbuf[fp][:], acc[:, tt * 1024:(tt + 1) * 1024], xbuf[fp][:]),
                  reads=(f"xb{fp}", f"acc{tt}"), writes=(f"xb{fp}",))
            P.grp("vector",
                  lambda h, fp=fp, ya=ya: h.tensor_add(xbuf[fp][:], ya, xbuf[fp][:]),
                  reads=(f"xb{fp}", f"ya{fp}"), writes=(f"xb{fp}",))
            P.grp("vector",
                  lambda h, fp=fp, yb=yb: h.tensor_add(xbuf[fp][:], yb, xbuf[fp][:]),
                  reads=(f"xb{fp}", f"yb{fp}"), writes=(f"xb{fp}",))
            P.grp("sync",
                  lambda h, tt=tt, fp=fp: h.dma_start(
                      out=out_d[tt * 128:(tt + 1) * 128, :], in_=xbuf[fp][:]),
                  reads=(f"xb{fp}",), writes=(), dsem=f"d_out{fp}")
        P._wait("sync", "d_out0", P.count["d_out0"])
        P._wait("sync", "d_out1", P.count["d_out1"])
        psM.close()

        sems = ExitStack()
        P.sems = {k: sems.enter_context(nc.semaphore(f"sem_{k}"))
                  for k in sorted(P.count)}
        with sems:
            with nc.Block() as block:
                @block.sync
                def _(h):
                    for f in P.ops["sync"]:
                        f(h)

                @block.gpsimd
                def _(h):
                    for f in P.ops["gpsimd"]:
                        f(h)

                @block.tensor
                def _(h):
                    for f in P.ops["tensor"]:
                        f(h)

                @block.vector
                def _(h):
                    for f in P.ops["vector"]:
                        f(h)

                @block.scalar
                def _(h):
                    for f in P.ops["scalar"]:
                        f(h)
    return nc


def prepare_host(inputs, T=None):
    f = np.float32
    Wq = np.ascontiguousarray(inputs["L0_attn_Wq"], f)
    Wk = np.ascontiguousarray(inputs["L0_attn_Wk"], f)
    Wv = np.ascontiguousarray(inputs["L0_attn_Wv"], f)
    Wo = np.ascontiguousarray(inputs["L0_attn_Wo"], f)
    bq = np.ascontiguousarray(inputs["L0_attn_bq"], f)
    bk = np.ascontiguousarray(inputs["L0_attn_bk"], f)
    bv = np.ascontiguousarray(inputs["L0_attn_bv"], f)
    bo = np.ascontiguousarray(inputs["L0_attn_bo"], f)
    Wg = np.ascontiguousarray(inputs["L0_ffn_Wg"], f)
    Weg = np.ascontiguousarray(inputs["L0_ffn_We_gate"], f)
    Weu = np.ascontiguousarray(inputs["L0_ffn_We_up"], f)
    Wed = np.ascontiguousarray(inputs["L0_ffn_We_down"], f)
    Wsg_ = np.ascontiguousarray(inputs["L0_ffn_Ws_gate"], f)
    Wsu = np.ascontiguousarray(inputs["L0_ffn_Ws_up"], f)
    Wsd = np.ascontiguousarray(inputs["L0_ffn_Ws_down"], f)
    Wsig = np.ascontiguousarray(inputs["L0_ffn_Wsg"], f)

    qkvwo = np.zeros((NHP, 128, 5376), f)
    for hp in range(NHP):
        cs = slice(hp * 128, hp * 128 + 128)
        for j, W in enumerate((Wq, Wk, Wv)):
            blk = W[cs, :].reshape(128, 8, 128)
            qkvwo[hp][:, j * 1024:(j + 1) * 1024] = (
                blk.transpose(2, 1, 0).reshape(128, 1024))
        qkvwo[hp][:, 3072:4096] = Wo[:, cs].T
        qkvwo[hp][0, 4096:4224] = bv[cs]
        qkvwo[hp][0, 4224:5248] = bo
        qkvwo[hp][0, 5248:5376] = 1.0

    bqkvT = np.empty((128, 16), f)
    for hp in range(NHP):
        bqkvT[:, hp * 2 + 0] = bq[hp * 128:(hp + 1) * 128]
        bqkvT[:, hp * 2 + 1] = bk[hp * 128:(hp + 1) * 128]

    def img_gud(gate, up, down):
        out = np.empty((128, 6144), f)
        out[:, 0:2048] = gate.reshape(256, 8, 128).transpose(2, 1, 0).reshape(128, 2048)
        out[:, 2048:4096] = up.reshape(256, 8, 128).transpose(2, 1, 0).reshape(128, 2048)
        out[:, 4096:6144] = down.reshape(1024, 2, 128).transpose(2, 1, 0).reshape(128, 2048)
        return out

    wunits = np.empty((NU, 128, 6144), f)
    for e in range(NE):
        for half in range(2):
            hs = slice(half * 256, half * 256 + 256)
            wunits[2 * e + half] = img_gud(Weg[e][hs, :], Weu[e][hs, :], Wed[e][:, hs])
    for s in range(NSH):
        for half in range(2):
            o = s * 512 + half * 256
            hs = slice(o, o + 256)
            wunits[2 * NE + 2 * s + half] = img_gud(Wsg_[hs, :], Wsu[hs, :], Wsd[:, hs])

    WgA = np.concatenate([Wg, Wsig], axis=0)
    wgimg = np.ascontiguousarray(
        WgA.reshape(17, 8, 128).transpose(2, 1, 0).reshape(128, 136))

    if T is None:
        T = int(np.asarray(inputs["L0_attn_x"]).shape[1])
    CAP = T // 4
    NB = CAP // 128
    rout = np.zeros((128, 280), f)
    rout[:, 0:128] = np.tril(np.ones((128, 128), f), -1).T
    rout[:, 128] = 1.0
    rout[:, 129] = np.arange(128, dtype=f)
    rout[:, 130:146] = (np.arange(16, dtype=f) * NB)[None, :]
    rout[0, 148:276] = 1.0
    fill = np.zeros((128, 64 * NB), f)
    fill[:, 1::4] = float(2 * T)
    return {
        "qkvwo": qkvwo,
        "wunits": wunits,
        "bqkvT_d": bqkvT,
        "wgimg": wgimg,
        "ident": np.eye(128, dtype=f),
        "routimg": rout,
        "fillimg": fill,
    }


def kernel(**inputs):
    x = np.ascontiguousarray(np.asarray(inputs["L0_attn_x"], np.float32))
    B, T, d = x.shape
    assert d == D
    common = prepare_host(inputs)
    nc = build(T)
    in_maps = []
    for b in range(B):
        m = dict(common)
        m["x"] = np.ascontiguousarray(x[b])
        in_maps.append(m)
    res = run_bass_kernel_spmd(nc, in_maps, core_ids=list(range(B)))
    return np.stack([r["out"] for r in res.results], axis=0)
